# version 28
# baseline (speedup 1.0000x reference)
from contextlib import ExitStack

import numpy as np
import ml_dtypes

import concourse.bass as bass
import concourse.mybir as mybir
from concourse.bass_utils import run_bass_kernel_spmd

F32 = mybir.dt.float32
BF16 = mybir.dt.bfloat16
AF = mybir.ActivationFunctionType
ALU = mybir.AluOpType
AX = mybir.AxisListType
BF = ml_dtypes.bfloat16

D = 1024
H = 16
KVH = 4
HD = 64
FT = 256
HT = 256
EPS_LN = 1e-5
EPS_RMS = 1e-6

ENGINES = ("tensor", "vector", "scalar", "gpsimd", "sync")
SELF_SYNC = True
NDMA = 16
A1_IN_ATTN = False


class Emitter:
    def __init__(self):
        self.ops = []
        self.barriers = []

    def op(self, engine, fn, reads=(), writes=(), dma=False, tag=""):
        self.ops.append(dict(engine=engine, fn=fn, reads=tuple(reads),
                             writes=tuple(writes), dma=dma, tag=tag,
                             idx=len(self.ops)))

    def dma_fence(self, emitfn):
        emitfn()

    def barrier(self):
        if not self.barriers or self.barriers[-1] != len(self.ops):
            self.barriers.append(len(self.ops))

    def finalize(self, nc, sems, dma_sem):
        ops, n = self.ops, len(self.ops)
        last_write, readers = {}, {}
        deps = [set() for _ in range(n)]
        for j, op in enumerate(ops):
            for r in op["reads"]:
                if r in last_write:
                    deps[j].add(last_write[r])
            for w in op["writes"]:
                if w in last_write:
                    deps[j].add(last_write[w])
                for rd in readers.get(w, ()):
                    deps[j].add(rd)
            for r in op["reads"]:
                readers.setdefault(r, []).append(j)
            for w in op["writes"]:
                last_write[w] = j
                readers[w] = []
        bidx = [0] * n
        bi = 0
        for j in range(n):
            while bi < len(self.barriers) and self.barriers[bi] <= j:
                bi += 1
            bidx[j] = bi

        def needs_edge(i, j):
            oi, oj = ops[i], ops[j]
            if oi["dma"]:
                return True
            if bidx[i] < bidx[j]:
                return False
            if oi["engine"] == oj["engine"]:
                if oi["engine"] == "tensor":
                    return False
                if not SELF_SYNC:
                    return False
            return True

        def prodkey(i):
            if ops[i]["dma"]:
                return ("dmaop", i)
            return ops[i]["engine"]

        for j in range(n):
            best = {}
            for i in deps[j]:
                if not needs_edge(i, j):
                    continue
                k = prodkey(i)
                if k[0] == "dmaop" if isinstance(k, tuple) else False:
                    best[k] = i
                else:
                    best[k] = max(best.get(k, -1), i)
            deps[j] = set(best.values())
        published = [False] * n
        for j in range(n):
            for i in deps[j]:
                published[i] = True
        for op in ops:
            if op["dma"]:
                published[op["idx"]] = True
        rank = [0] * n
        lane = [0] * n
        cnt = {e: 0 for e in ENGINES}
        lane_cnt = [0] * (NDMA + 1)
        dma_i = 0
        for j in range(n):
            if ops[j]["dma"]:
                if ops[j]["tag"] == "allred":
                    ln = NDMA
                else:
                    ln = dma_i % NDMA
                    dma_i += 1
                lane[j] = ln
                lane_cnt[ln] += 1
                rank[j] = lane_cnt[ln]
            elif published[j]:
                e = ops[j]["engine"]
                cnt[e] += 1
                rank[j] = cnt[e]
        self.lane_totals = list(lane_cnt)

        dkeys = [("dma", l) for l in range(NDMA + 1)]

        def new_vc():
            d = {e: 0 for e in ENGINES}
            for k in dkeys:
                d[k] = 0
            return d

        vc = {e: new_vc() for e in ENGINES}
        snap = {e: {} for e in ENGINES}
        for k in dkeys:
            snap[k] = {}
        waits = [None] * n
        cur_b = 0
        pubs_at = {e: 0 for e in ENGINES}
        lane_seen = [0] * (NDMA + 1)
        for j in range(n):
            while cur_b < len(self.barriers) and self.barriers[cur_b] <= j:
                for e in ENGINES:
                    for f in ENGINES:
                        if vc[e][f] < pubs_at[f]:
                            vc[e][f] = pubs_at[f]
                cur_b += 1
            op = ops[j]
            e = op["engine"]
            need = {}
            if op["tag"] == "dmafence":
                for l in range(NDMA + 1):
                    v = lane_seen[l] * (1 if l == NDMA else 16)
                    if v:
                        need[("dma", l)] = v
            for i in deps[j]:
                if not needs_edge(i, j):
                    continue
                if ops[i]["dma"]:
                    mult = 1 if lane[i] == NDMA else 16
                    key, val = ("dma", lane[i]), rank[i] * mult
                else:
                    key, val = ops[i]["engine"], rank[i]
                if val > need.get(key, 0):
                    need[key] = val
            if op["dma"] and rank[j] > 1:
                key = ("dma", lane[j])
                v = (rank[j] - 1) * (1 if lane[j] == NDMA else 16)
                if v > need.get(key, 0):
                    need[key] = v
            wl = []
            for key, val in sorted(need.items(), key=str):
                if vc[e][key] < val:
                    wl.append((key, val))
                    sn = snap[key].get(val)
                    if sn is not None:
                        for k2, v2 in sn.items():
                            if vc[e][k2] < v2:
                                vc[e][k2] = v2
                    if vc[e][key] < val:
                        vc[e][key] = val
            waits[j] = wl
            if op["dma"]:
                lane_seen[lane[j]] = rank[j]
            if published[j]:
                if op["dma"]:
                    sn = dict(vc[e])
                    k = ("dma", lane[j])
                    mult = 1 if lane[j] == NDMA else 16
                    sn[k] = rank[j] * mult
                    snap[k][rank[j] * mult] = sn
                else:
                    sn = dict(vc[e])
                    sn[e] = rank[j]
                    snap[e][rank[j]] = sn
                    pubs_at[e] = rank[j]

        semmap = dict(sems)
        for l in range(NDMA + 1):
            semmap[("dma", l)] = dma_sem[l]
        seg_bounds = [0] + list(self.barriers) + [n]
        for s in range(len(seg_bounds) - 1):
            lo, hi = seg_bounds[s], seg_bounds[s + 1]
            if hi <= lo:
                continue
            seg = [ops[j] for j in range(lo, hi)]
            with nc.Block() as blk:
                for e in ENGINES:
                    eops = [o for o in seg if o["engine"] == e]
                    if not eops:
                        continue

                    def body(eng, eops=eops, e=e):
                        for o in eops:
                            j = o["idx"]
                            for key, val in waits[j]:
                                eng.wait_ge(semmap[key], val)
                            ins = o["fn"](eng)
                            if published[j]:
                                if o["dma"]:
                                    ins.then_inc(dma_sem[lane[j]],
                                                 1 if lane[j] == NDMA else 16)
                                else:
                                    ins.then_inc(semmap[e], 1)

                    getattr(blk, e)(body)


def _chw(w):
    kin, nout = w.shape
    return np.ascontiguousarray(
        w.reshape(kin // 128, 128, nout).transpose(1, 0, 2).reshape(128, -1))


def build_consts():
    c = {}
    c["ident"] = np.eye(128, dtype=BF)
    ep = np.zeros((128, 128), np.float32)
    for a in range(2):
        for b in range(2):
            ep[a * 64:(a + 1) * 64, b * 64:(b + 1) * 64] = np.eye(64)
    c["eyepair"] = ep.astype(BF)
    c["onesrB"] = np.ones((1, 512), dtype=BF)
    c["onescB"] = np.ones((128, 1), dtype=BF)
    c["onesrF"] = np.ones((1, 128), dtype=np.float32)
    c["negrF"] = -np.ones((1, 128), dtype=np.float32)
    c["epsc"] = np.tile(np.array([[EPS_LN, EPS_RMS]], np.float32), (128, 1))
    lo = np.zeros((1, 128), np.float32); lo[0, :64] = 1.0
    hi = np.zeros((1, 128), np.float32); hi[0, 64:] = 1.0
    c["lo64"], c["hi64"] = lo, hi
    sel2 = np.zeros((2, 128), np.float32)
    sel2[0, :64] = 1.0
    sel2[1, 64:] = 1.0
    c["sel2"] = sel2
    q = np.arange(128)
    c["cmask"] = (q[None, :] >= q[:, None]).astype(BF)
    c["wmask"] = (q[None, :] < q[:, None]).astype(BF)
    c["zmask"] = np.zeros((128, 128), dtype=BF)
    c["onemask"] = np.ones((128, 128), dtype=BF)
    return c


def build_core_inputs(xjobs, half, TC, consts):
    d = {}
    t0 = half * TC
    for j, xb in enumerate(xjobs):
        d[f"x{j}"] = np.ascontiguousarray(xb[t0:t0 + TC].T).astype(np.float32)
        if half >= 1:
            d[f"xhalo{j}"] = np.ascontiguousarray(
                xb[t0 - HT:t0].T).astype(np.float32)
        else:
            d[f"xhalo{j}"] = np.zeros((D, HT), np.float32)
    nkv = TC // 128 + 2
    inv = (1.0 / 10000.0 ** (np.arange(0, HD, 2) / HD)).astype(np.float32)
    pos = np.arange(t0 - HT, t0 + TC, dtype=np.float32)
    pos = np.maximum(pos, 0.0)
    fr = np.outer(pos, inv)
    d["cosq"] = np.ascontiguousarray(np.cos(fr).astype(BF).reshape(
        nkv, 128, 32).transpose(1, 0, 2).reshape(128, -1))
    d["sinq"] = np.ascontiguousarray(np.sin(fr).astype(BF).reshape(
        nkv, 128, 32).transpose(1, 0, 2).reshape(128, -1))
    if half == 0:
        d["mA"], d["mB"], d["mC"] = (consts["zmask"],) * 3
    else:
        d["mA"], d["mB"], d["mC"] = (consts["wmask"], consts["onemask"],
                                     consts["wmask"])
    return d


def stage_weights(ins):
    g = lambda k: np.ascontiguousarray(np.asarray(ins[k], np.float32))
    d = {}
    d["wcn1"] = _chw(g("L0_ffn_cn_w1"))
    d["wcn2"] = _chw(g("L0_ffn_cn_w2"))
    d["wcn3"] = _chw(g("L0_ffn_cn_w3"))
    d["wcn4"] = _chw(g("L0_ffn_cn_w4"))
    d["wgc1"] = _chw(g("L0_ffn_gc_w1"))
    d["wgc2"] = _chw(g("L0_ffn_gc_w2"))
    d["wgc3"] = _chw(g("L0_ffn_gc_w3"))
    d["wq"] = _chw(np.ascontiguousarray(g("L0_attn_Wq").T))
    wkT = _chw(np.ascontiguousarray(g("L0_attn_Wk").T))
    wvT = _chw(np.ascontiguousarray(g("L0_attn_Wv").T))
    d["wkv"] = np.ascontiguousarray(np.concatenate(
        [wkT.reshape(128, 8, 256), wvT.reshape(128, 8, 256)],
        axis=2).reshape(128, 4096))
    d["wo"] = _chw(np.ascontiguousarray(g("L0_attn_Wo").T))
    d["bcn1"] = np.ascontiguousarray(g("L0_ffn_cn_b1").reshape(8, 128).T)
    d["bcn2"] = np.ascontiguousarray(g("L0_ffn_cn_b2").reshape(4, 128).T)
    d["bcn3"] = np.ascontiguousarray(g("L0_ffn_cn_b3").reshape(2, 128).T)
    d["bgc1"] = np.ascontiguousarray(g("L0_ffn_gc_b1").reshape(4, 128).T)
    d["bgc2"] = np.ascontiguousarray(g("L0_ffn_gc_b2").reshape(2, 128).T)
    d["b4s"] = np.array([[float(g("L0_ffn_cn_b4").reshape(-1)[0]),
                          float(g("L0_ffn_gc_b3").reshape(-1)[0]),
                          float(g("L0_ffn_adaptive_range").reshape(-1)[0]),
                          0.0]], np.float32)
    d["qgain"] = g("L0_attn_q_gain").reshape(1, 16)
    pm = g("L0_attn_pair_mix")
    pm2 = np.zeros((2, 16), np.float32)
    for c in range(8):
        pm2[0, 2 * c] = pm[c, 0, 0]
        pm2[1, 2 * c] = pm[c, 1, 0]
        pm2[0, 2 * c + 1] = pm[c, 0, 1]
        pm2[1, 2 * c + 1] = pm[c, 1, 1]
    d["pm2"] = pm2
    return d


class Kern:
    def __init__(self, TC, n_pairs, njobs=2):
        self.TC = TC
        self.n_pairs = n_pairs
        self.njobs = njobs
        self.NFT = TC // FT
        self.NQT = TC // 128
        self.nc = bass.Bass("TRN2", target_bir_lowering=False)
        self.E = Emitter()
        self.es = ExitStack()
        self.dram_in = {}

    def din(self, name, shape, dt=F32):
        t = self.nc.dram_tensor(name, list(shape), dt, kind="ExternalInput")
        self.dram_in[name] = t
        return t

    def sb(self, name, shape, dt, scope=None):
        return (scope or self.es).enter_context(
            self.nc.sbuf_tensor(name, list(shape), dt))

    def dma(self, dst, src, reads, writes, tag=""):
        self.E.op("sync", lambda e, d=dst, s=src: e.dma_start(d, s),
                  reads=reads, writes=writes, dma=True, tag=tag)

    def mm(self, out, lhsT, rhs, start, stop, reads, writes, tag="", tp=None):
        self.E.op("tensor",
                  lambda e, o=out, l=lhsT, r=rhs, st=start, sp=stop, t=tp:
                  e.matmul(o, l, r, start=st, stop=sp, tile_position=t),
                  reads=reads, writes=writes, tag=tag)

    def act(self, out, in_, func, reads, writes, bias=0.0, scale=1.0, tag=""):
        self.E.op("scalar",
                  lambda e, o=out, i=in_, f=func, b=bias, s=scale:
                  e.activation(o, i, f, bias=b, scale=s),
                  reads=reads, writes=writes, tag=tag)

    def ew(self, engine, meth, *args, reads=(), writes=(), tag="", **kw):
        def fn(e, m=meth, a=args, k=kw):
            return getattr(e, m)(*a, **k)
        self.E.op(engine, fn, reads=reads, writes=writes, tag=tag)

    def load_weight(self, name, width, scope, tag="w"):
        dst = self.sb(name + "_b" + tag, [128, width], BF16, scope)
        if name in self.wcached:
            off = self.wcached[name]
            self.dma(dst[:, :], self.wdram[:, off:off + width],
                     reads=[("wd", name), ("wspill",)],
                     writes=[("fw", name)], tag="wfast")
            return dst
        src = self.dram_in[name]
        for i in range(0, width, 1024):
            hi = min(width, i + 1024)
            stg = self.wstage
            kst = ("wstg",)
            self.dma(stg[:, 0:hi - i], src[:, i:hi],
                     reads=[("wspill",)], writes=[kst], tag="wload")
            self.ew("vector", "tensor_copy", dst[:, i:hi],
                    stg[:, 0:hi - i],
                    reads=[kst], writes=[("fw", name)],
                    tag="wcast")
        return dst

    def emit_allreduce(self, msum, ccin, ccout):
        self.dma(ccin[:, :], msum[:, :], reads=[("msum",)], writes=[("ccin",)])
        groups = [[2 * i, 2 * i + 1] for i in range(self.n_pairs)]
        self.E.op("gpsimd",
                  lambda e: e.collective_compute("AllReduce", ALU.add, groups,
                                                 [ccin.ap().opt()],
                                                 [ccout.ap().opt()]),
                  reads=[("ccin",)], writes=[("ccout",)], dma=True,
                  tag="allred")
        self.dma(msum[:, :], ccout[:, :], reads=[("ccout",)],
                 writes=[("msum",)])

    def spill_weight(self, name, width, tensor):
        off = self.wd_off
        self.wd_off += width
        self.wcached[name] = off
        self.dma(self.wdram[:, off:off + width], tensor[:, :],
                 reads=[("fw", name)], writes=[("wd", name), ("wspill",)],
                 tag="wspill")

    def build(self):
        nc, E, TC = self.nc, self.E, self.TC
        NFT, NQT = self.NFT, self.NQT
        nkv = NQT + 2

        x_in = [self.din(f"x{j}", [D, TC]) for j in range(self.njobs)]
        xh_in = [self.din(f"xhalo{j}", [D, HT]) for j in range(self.njobs)]
        for k, shp in dict(wcn1=(128, 8192), wcn2=(128, 4096), wcn3=(128, 1024),
                           wcn4=(128, 2), wgc1=(128, 4096), wgc2=(128, 1024),
                           wgc3=(128, 2), wq=(128, 8192), wkv=(128, 4096),
                           wo=(128, 8192)).items():
            self.din(k, shp)
        cshapes = dict(ident=((128, 128), BF16), eyepair=((128, 128), BF16),
                       onesrB=((1, 512), BF16),
                       onescB=((128, 1), BF16), onesrF=((1, 128), F32),
                       lo64=((1, 128), F32), hi64=((1, 128), F32),
                       negrF=((1, 128), F32),
                       sel2=((2, 128), F32), epsc=((128, 2), F32),
                       cmask=((128, 128), BF16),
                       wmask=((128, 128), BF16), mA=((128, 128), BF16),
                       mB=((128, 128), BF16), mC=((128, 128), BF16),
                       cosq=((128, nkv * 32), BF16), sinq=((128, nkv * 32), BF16),
                       bcn1=((128, 8), F32), bcn2=((128, 4), F32),
                       bcn3=((128, 2), F32), bgc1=((128, 4), F32),
                       bgc2=((128, 2), F32), b4s=((1, 4), F32),
                       qgain=((1, 16), F32), pm2=((2, 16), F32))
        for k, (shp, dt) in cshapes.items():
            self.din(k, shp, dt)
        y_out = [nc.dram_tensor(f"y{j}", [D, TC], F32, kind="ExternalOutput")
                 for j in range(self.njobs)]
        statsd = nc.dram_tensor("statsd", [4 * (NFT + 1), 2 * FT], F32,
                                kind="Internal")
        ccin = nc.dram_tensor("ccin", [128, 8], F32, kind="Internal")
        ccout = nc.dram_tensor("ccout", [128, 8], F32, kind="Internal")

        x_sb = self.sb("x_sb", [128, 8 * TC], F32)
        xh_sb = self.sb("xh_sb", [128, 8 * HT], F32)
        cons = {}
        for k, (shp, dt) in cshapes.items():
            cons[k] = self.sb("c_" + k, shp, dt)
        rowc = self.sb("rowc", [1, 32], F32)
        msum = self.sb("msum", [128, 8], F32)
        gmb = self.sb("gmb", [128, 8], BF16)
        gfh1 = self.sb("gfh1", [128, 4], BF16)
        gfh2 = self.sb("gfh2", [128, 2], BF16)
        grow = self.sb("grow", [128, 16], F32)
        mneg = self.sb("mneg", [1, 16], F32)
        self.wstage = self.sb("wstage", [128, 1024], F32)
        self.wstage2 = self.wstage
        self.wdram = nc.dram_tensor("wdram", [128, 40960], BF16,
                                    kind="Internal")
        self.wcached = {}
        self.wd_off = 0

        pb = [self.es.enter_context(
            nc.psum_tensor(f"pb{i}", [128, 512], F32)) for i in range(8)]
        self.pb = pb

        sems = {e: self.es.enter_context(nc.semaphore("sem_" + e))
                for e in ENGINES}
        dma_sem = [self.es.enter_context(nc.semaphore(f"sem_dma{l}"))
                   for l in range(NDMA + 1)]

        for k in cons:
            self.dma(cons[k][:, :], self.dram_in[k][:, :], reads=(),
                     writes=[("c", k)], tag="cload")
        self.ew("vector", "tensor_scalar", rowc[:, 8:9], cons["b4s"][:, 0:1],
                0.5, None, ALU.mult, reads=[("c", "b4s")], writes=[("c", "rowc")])
        self.ew("vector", "tensor_scalar", rowc[:, 9:10], cons["b4s"][:, 1:2],
                0.5, None, ALU.mult, reads=[("c", "b4s")], writes=[("c", "rowc")])
        self.ew("vector", "tensor_scalar", mneg[:, :], cons["qgain"][:, :],
                -1.0, None, ALU.mult, reads=[("c", "qgain")],
                writes=[("c", "mneg")])
        self.ew("vector", "tensor_tensor", mneg[:, :], mneg[:, :],
                cons["qgain"][:, :], ALU.max,
                reads=[("c", "mneg"), ("c", "qgain")], writes=[("c", "mneg")])
        self.ew("vector", "tensor_scalar", mneg[:, :], mneg[:, :], -8.08,
                -1e-3, ALU.mult, ALU.add, reads=[("c", "mneg")],
                writes=[("c", "mneg")])
        self.ew("vector", "tensor_scalar", rowc[:, 16:32], cons["qgain"][:, :],
                0.125, None, ALU.mult, reads=[("c", "qgain")],
                writes=[("c", "rowc")])
        self.mm(pb[0][:, 0:16], cons["onesrF"][:, :], rowc[:, 16:32], True,
                True, reads=[("c", "onesrF"), ("c", "rowc")],
                writes=[("pb", 0)], tag="growb")
        self.ew("vector", "tensor_copy", grow[:, :], pb[0][:, 0:16],
                reads=[("pb", 0)], writes=[("c", "grow")])
        E.barrier()

        for job in range(self.njobs):
            xv_all = x_in[job][:, :].rearrange("(c p) t -> p c t", p=128)
            for ft in range(NFT):
                dst = x_sb[:, :].rearrange("p (c t) -> p c t", t=TC)[
                    :, :, ft * FT:(ft + 1) * FT]
                self.dma(dst, xv_all[:, :, ft * FT:(ft + 1) * FT], reads=(),
                         writes=[("x", ft * (FT // 128) + q)
                                 for q in range(FT // 128)], tag="xload")
            self.dma(xh_sb[:, :].rearrange("p (c t) -> p c t", t=HT),
                     xh_in[job][:, :].rearrange("(c p) t -> p c t", p=128),
                     reads=(), writes=[("xh", 0), ("xh", 1)])

            with ExitStack() as sc:
                wspec = dict(wcn1=8192, wcn2=4096, wcn3=1024, wcn4=2,
                             wgc1=4096, wgc2=1024, wgc3=2)
                fw = {k: self.load_weight(k, w, sc, tag=f"1{job}")
                      for k, w in wspec.items()}
                if job == 0:
                    for k, w in wspec.items():
                        self.spill_weight(k, w, fw[k])
                    self.E.op("sync", lambda e: e.nop(),
                              reads=[("wspill",)], writes=(), tag="wjoin")
                self.ffn_phase(fw, cons, rowc, msum, gmb, gfh1, gfh2, grow,
                               x_sb, xh_sb, statsd, ccin, ccout, sc,
                               call=2 * job + 0, halo=True, store=None,
                               y_out=None)
            E.op("sync", lambda e: e.nop(), reads=(), writes=(),
                 tag="dmafence")
            E.barrier()

            with ExitStack() as sc:
                self.attn_phase(cons, grow, mneg, x_sb, xh_sb, sc, job,
                                ffn2=(2 * job + 1, statsd, ccin, ccout, msum)
                                if A1_IN_ATTN else None)
            E.op("sync", lambda e: e.nop(), reads=(), writes=(),
                 tag="dmafence")
            E.barrier()
            if job == 0:
                self.E.op("sync", lambda e: e.nop(), reads=[("wspill",)],
                          writes=(), tag="wjoin")
                E.barrier()

            with ExitStack() as sc:
                fw = {k: self.load_weight(k, w, sc, tag=f"2{job}") for k, w in
                      dict(wcn1=8192, wcn2=4096, wcn3=1024, wcn4=2, wgc1=4096,
                           wgc2=1024, wgc3=2).items()}
                self.ffn_phase(fw, cons, rowc, msum, gmb, gfh1, gfh2, grow,
                               x_sb, xh_sb, statsd, ccin, ccout, sc,
                               call=2 * job + 1, halo=False, store=True,
                               y_out=y_out[job], skip_a1=A1_IN_ATTN)
            E.op("sync", lambda e: e.nop(), reads=(), writes=(),
                 tag="dmafence")
            E.barrier()

        def finwait(e):
            ins = None
            for l in range(NDMA + 1):
                if E.lane_totals[l]:
                    m = 1 if l == NDMA else 16
                    ins = e.wait_ge(dma_sem[l], m * E.lane_totals[l])
            return ins
        E.op("sync", finwait, reads=(), writes=(), tag="final")

        E.finalize(nc, sems, dma_sem)
        nc.finalize()
        return self

    def ffn_phase(self, fw, cons, rowc, msum, gmb, gfh1, gfh2, grow,
                  x_sb, xh_sb, statsd, ccin, ccout, sc,
                  call, halo, store, y_out, skip_a1=False):
        nc, E, TC, NFT = self.nc, self.E, self.TC, self.NFT
        pb = self.pb
        W = FT
        sbs = lambda nm, shape, dt: self.sb(f"{nm}_{call}", shape, dt, sc)
        NP = 3
        out0 = [sbs(f"out0{p}", [128, 8 * W], F32) for p in range(NP)]
        ob = [sbs(f"ob{p}", [128, 8 * W], BF16) for p in range(NP)]
        h1b = [sbs(f"h1b{p}", [128, 8 * W], BF16) for p in range(NP)]
        h2b = [sbs(f"h2b{p}", [128, 4 * W], BF16) for p in range(2)]
        h3b = [sbs(f"h3b{p}", [128, 2 * W], BF16) for p in range(2)]
        utmp = [sbs(f"utmp{p}", [128, 4 * W], F32) for p in range(2)]
        xb2 = [sbs(f"xb{p}", [128, 8 * W], BF16) for p in range(2)]
        strow2 = [sbs(f"strow{p}", [1, 2 * W], F32) for p in range(2)]
        trow2 = [sbs(f"trow{p}", [1, W], F32) for p in range(2)]
        crow2 = [sbs(f"crow{p}", [1, 2 * W], F32) for p in range(2)]
        red2 = [sbs(f"red{p}", [128, 8], F32) for p in range(2)]
        xsqb = [sbs("xsqb", [128, 8 * W], BF16)] * 2

        ntile = NFT + (1 if halo else 0)

        def tile_info(ft):
            is_halo = ft == NFT
            src_ = xh_sb if is_halo else x_sb
            TT = HT if is_halo else TC
            toff = 0 if is_halo else ft * W
            pfx = "xh" if is_halo else "x"
            xkeys = [(pfx, (toff // 128) + q) for q in range(W // 128)]
            xv = src_[:, :].rearrange("p (c t) -> p c t", t=TT)[
                :, :, toff:toff + W]
            return is_halo, xv, xkeys

        if not skip_a1:
            self.ew("vector", "memset", msum[:, :], 0.0, reads=(),
                    writes=[("msum",)])
        for ft in range(0 if skip_a1 else ntile):
            is_halo, xv, xkeys = tile_info(ft)
            p = ft % 2
            xb, strow, red = xb2[p], strow2[p], red2[p]
            kxb, kst = ("xb", call, p), ("strow", call, p)
            srow_bank = 7 - 2 * p
            bc_bank = 6 - 2 * p
            xbv = xb[:, :].rearrange("p (c t) -> p c t", t=W)
            self.ew("vector", "tensor_copy", xbv, xv, reads=xkeys,
                    writes=[kxb], tag="a1cast")
            for ic in range(8):
                self.mm(pb[srow_bank][0:1, 0:W], cons["onescB"][:, :],
                        xb[:, ic * W:(ic + 1) * W], ic == 0, ic == 7,
                        reads=[("c", "onescB"), kxb],
                        writes=[("pb", srow_bank)], tag="a1s1")
            xsq = xsqb[p]
            kxs = ("xsqb", call)
            self.act(xsq[:, :], xb[:, :], AF.Square, reads=[kxb],
                     writes=[kxs], tag="a1sq")
            for ic in range(8):
                self.mm(pb[srow_bank][0:1, W:2 * W], cons["onescB"][:, :],
                        xsq[:, ic * W:(ic + 1) * W], ic == 0, ic == 7,
                        reads=[("c", "onescB"), kxs],
                        writes=[("pb", srow_bank)], tag="a1s2")
            self.ew("scalar", "mul", strow[:, 0:W],
                    pb[srow_bank][0:1, 0:W], 1.0 / D,
                    reads=[("pb", srow_bank)], writes=[kst], tag="a1m")
            self.act(strow[:, W:2 * W], strow[:, 0:W], AF.Square,
                     reads=[kst], writes=[kst], tag="a1msq")
            self.ew("vector", "scalar_tensor_tensor", strow[:, W:2 * W],
                    pb[srow_bank][0:1, W:2 * W], 1.0 / D, strow[:, W:2 * W],
                    ALU.mult, ALU.subtract,
                    reads=[("pb", srow_bank), kst], writes=[kst], tag="a1var")
            self.act(strow[:, W:2 * W], strow[:, W:2 * W], AF.Ln,
                     reads=[kst, ("c", "epsc")], writes=[kst],
                     bias=cons["epsc"][0:1, 0:1])
            self.act(strow[:, W:2 * W], strow[:, W:2 * W], AF.Exp,
                     reads=[kst], writes=[kst], scale=-0.5)
            self.dma(statsd[call * (NFT + 1) + ft:call * (NFT + 1) + ft + 1,
                            0:2 * W], strow[0:1, 0:2 * W],
                     reads=[kst], writes=[("std", call, ft)])
            if is_halo:
                continue
            self.mm(pb[bc_bank][:, 0:W], cons["onesrF"][:, :], strow[:, 0:W],
                    True, True, reads=[("c", "onesrF"), kst],
                    writes=[("pb", bc_bank)], tag="a1bm")
            self.mm(pb[bc_bank][:, W:2 * W], cons["onesrF"][:, :],
                    strow[:, W:2 * W], True, True,
                    reads=[("c", "onesrF"), kst],
                    writes=[("pb", bc_bank)], tag="a1br")
            self.ew("vector", "tensor_tensor", xbv, xv,
                    pb[bc_bank][:, None, 0:W].broadcast_to([128, 8, W]),
                    ALU.subtract, reads=xkeys + [("pb", bc_bank)],
                    writes=[kxb], tag="a1sub")
            self.ew("vector", "tensor_tensor", xbv, xbv,
                    pb[bc_bank][:, None, W:2 * W].broadcast_to([128, 8, W]),
                    ALU.mult, reads=[kxb, ("pb", bc_bank)], writes=[kxb],
                    tag="a1mul")
            self.ew("vector", "reduce_sum", red[:, 0:8], xbv, axis=AX.X,
                    reads=[kxb], writes=[("red", call, p)], tag="a1red")
            self.ew("vector", "tensor_tensor", msum[:, :], msum[:, :],
                    red[:, 0:8], ALU.add,
                    reads=[("msum",), ("red", call, p)], writes=[("msum",)])

        if not skip_a1:
            self.emit_allreduce(msum, ccin, ccout)

        def emit_a2():
         self.ew("vector", "tensor_scalar", gmb[:, :], msum[:, :],
                 1.0 / (2 * TC), None, ALU.mult, reads=[("msum",)],
                 writes=[("gmb",)], tag="gmb")
         for oc in range(4):
             for ic in range(8):
                 self.mm(pb[6][:, oc:oc + 1],
                         fw["wgc1"][:, ic * 512 + oc * 128:
                                    ic * 512 + oc * 128 + 128],
                         gmb[:, ic:ic + 1], ic == 0, ic == 7,
                         reads=[("fw", "wgc1"), ("gmb",)], writes=[("pb", 6)],
                         tag="gf1")
         for oc in range(4):
             self.act(gfh1[:, oc:oc + 1], pb[6][:, oc:oc + 1], AF.Gelu,
                      reads=[("pb", 6), ("c", "bgc1")], writes=[("gfh1",)],
                      bias=cons["bgc1"][:, oc:oc + 1])
         for oc in range(2):
             for ic in range(4):
                 self.mm(pb[6][:, 8 + oc:9 + oc],
                         fw["wgc2"][:, ic * 256 + oc * 128:
                                    ic * 256 + oc * 128 + 128],
                         gfh1[:, ic:ic + 1], ic == 0, ic == 3,
                         reads=[("fw", "wgc2"), ("gfh1",)], writes=[("pb", 6)],
                         tag="gf2")
         for oc in range(2):
             self.act(gfh2[:, oc:oc + 1], pb[6][:, 8 + oc:9 + oc], AF.Gelu,
                      reads=[("pb", 6), ("c", "bgc2")], writes=[("gfh2",)],
                      bias=cons["bgc2"][:, oc:oc + 1])
         for ic in range(2):
             self.mm(pb[6][0:1, 12:13], fw["wgc3"][:, ic:ic + 1],
                     gfh2[:, ic:ic + 1], ic == 0, ic == 1,
                     reads=[("fw", "wgc3"), ("gfh2",)], writes=[("pb", 6)],
                     tag="gf3")
         self.act(rowc[:, 10:11], pb[6][0:1, 12:13], AF.Tanh,
                  reads=[("pb", 6), ("c", "rowc")], writes=[("c", "rowc")],
                  bias=rowc[:, 9:10], scale=0.5, tag="gftanh")
         self.ew("vector", "tensor_scalar", rowc[:, 0:1], cons["b4s"][:, 2:3],
                 0.3, None, ALU.mult, reads=[("c", "b4s")],
                 writes=[("c", "rowc")])
         self.ew("vector", "tensor_tensor", rowc[:, 11:12], rowc[:, 10:11],
                 cons["b4s"][:, 2:3], ALU.mult,
                 reads=[("c", "rowc"), ("c", "b4s")], writes=[("c", "rowc")])
         self.ew("vector", "tensor_scalar", rowc[:, 1:2], rowc[:, 11:12], 0.7,
                 0.1, ALU.mult, ALU.add, reads=[("c", "rowc")],
                 writes=[("c", "rowc")])
         self.ew("vector", "tensor_scalar", rowc[:, 2:4], rowc[:, 0:2], 0.5,
                 None, ALU.mult, reads=[("c", "rowc")], writes=[("c", "rowc")])
         self.ew("vector", "tensor_scalar", rowc[:, 4:8], rowc[:, 0:4], -1.0,
                 None, ALU.mult, reads=[("c", "rowc")], writes=[("c", "rowc")])

        def emit_xhat(ft):
            is_halo, xv, xkeys = tile_info(ft)
            p = ft % 3
            strow = strow2[ft % 2]
            kst = ("strow", call, ft % 2)
            o0 = out0[p]
            o0v = o0[:, :].rearrange("p (c t) -> p c t", t=W)
            o0keys = [("o0", call, p, q) for q in range(4)]
            bcb = 6
            self.dma(strow[0:1, 0:2 * W],
                     statsd[call * (NFT + 1) + ft:call * (NFT + 1) + ft + 1,
                            0:2 * W],
                     reads=[("std", call, ft)], writes=[kst], tag="streload")
            self.mm(pb[bcb][:, 0:W], cons["onesrF"][:, :], strow[:, 0:W],
                    True, True, reads=[("c", "onesrF"), kst],
                    writes=[("pb", bcb)], tag="a3bm")
            self.mm(pb[bcb][:, W:2 * W], cons["onesrF"][:, :],
                    strow[:, W:2 * W], True, True,
                    reads=[("c", "onesrF"), kst],
                    writes=[("pb", bcb)], tag="a3br")
            self.ew("vector", "tensor_tensor", o0v, xv,
                    pb[bcb][:, None, 0:W].broadcast_to([128, 8, W]),
                    ALU.subtract, reads=xkeys + [("pb", bcb)],
                    writes=o0keys, tag="a3sub")
            self.ew("vector", "tensor_tensor", o0v, o0v,
                    pb[bcb][:, None, W:2 * W].broadcast_to([128, 8, W]),
                    ALU.mult, reads=o0keys + [("pb", bcb)],
                    writes=o0keys, tag="a3mul")
            self.ew("vector", "tensor_copy", ob[p][:, :], o0[:, :],
                    reads=o0keys, writes=[("ob", call, p)], tag="obcast")

        def emit_cycle(ft, cyc):
            p = ft % 3
            p2 = ft % 2
            ba, bb_ = 2 * p, 2 * p + 1
            o0 = out0[p]
            obp, h1p, h2p, h3p = ob[p], h1b[p], h2b[p2], h3b[p2]
            utp, trow, crow = utmp[p2], trow2[p2], crow2[p2]
            ko = ("ob", call, p)
            kh1, kh2, kh3 = ("h1b", call, p), ("h2b", call, p2), ("h3b", call, p2)
            ku, ktr, kcr = ("ut", call, p2), ("trow", call, p2), ("crow", call, p2)
            o0keys = [("o0", call, p, q) for q in range(4)]
            for oc in range(8):
                bank = ba if oc % 2 == 0 else bb_
                for ic in range(8):
                    self.mm(pb[bank][:, 0:W],
                            fw["wcn1"][:, ic * 1024 + oc * 128:
                                       ic * 1024 + oc * 128 + 128],
                            obp[:, ic * W:(ic + 1) * W], ic == 0, ic == 7,
                            reads=[("fw", "wcn1"), ko],
                            writes=[("pb", bank)], tag="h1mm")
                self.act(h1p[:, oc * W:(oc + 1) * W], pb[bank][:, 0:W],
                         AF.Gelu, reads=[("pb", bank), ("c", "bcn1")],
                         writes=[kh1], bias=cons["bcn1"][:, oc:oc + 1],
                         tag="h1act")
            for oc in range(4):
                bank = ba if oc % 2 == 0 else bb_
                for ic in range(8):
                    self.mm(pb[bank][:, 0:W],
                            fw["wcn2"][:, ic * 512 + oc * 128:
                                       ic * 512 + oc * 128 + 128],
                            h1p[:, ic * W:(ic + 1) * W], ic == 0, ic == 7,
                            reads=[("fw", "wcn2"), kh1],
                            writes=[("pb", bank)], tag="h2mm")
                self.act(h2p[:, oc * W:(oc + 1) * W], pb[bank][:, 0:W],
                         AF.Gelu, reads=[("pb", bank), ("c", "bcn2")],
                         writes=[kh2], bias=cons["bcn2"][:, oc:oc + 1],
                         tag="h2act")
            for oc in range(2):
                bank = ba if oc % 2 == 0 else bb_
                for ic in range(4):
                    self.mm(pb[bank][:, 0:W],
                            fw["wcn3"][:, ic * 256 + oc * 128:
                                       ic * 256 + oc * 128 + 128],
                            h2p[:, ic * W:(ic + 1) * W], ic == 0, ic == 3,
                            reads=[("fw", "wcn3"), kh2],
                            writes=[("pb", bank)], tag="h3mm")
                self.act(h3p[:, oc * W:(oc + 1) * W], pb[bank][:, 0:W],
                         AF.Gelu, reads=[("pb", bank), ("c", "bcn3")],
                         writes=[kh3], bias=cons["bcn3"][:, oc:oc + 1],
                         tag="h3act")
            for ic in range(2):
                self.mm(pb[ba][0:1, 0:W], fw["wcn4"][:, ic:ic + 1],
                        h3p[:, ic * W:(ic + 1) * W], ic == 0, ic == 1,
                        reads=[("fw", "wcn4"), kh3],
                        writes=[("pb", ba)], tag="zmm")
            self.act(trow[:, 0:W], pb[ba][0:1, 0:W], AF.Tanh,
                     reads=[("pb", ba), ("c", "rowc")], writes=[ktr],
                     bias=rowc[:, 8:9], scale=0.5, tag="tanh")

        def emit_tail(ft, cyc):
            p = ft % 3
            p2 = ft % 2
            ba, bb_ = 2 * p, 2 * p + 1
            o0 = out0[p]
            obp = ob[p]
            utp, trow, crow = utmp[p2], trow2[p2], crow2[p2]
            ko = ("ob", call, p)
            ku, ktr, kcr = ("ut", call, p2), ("trow", call, p2), ("crow", call, p2)
            o0keys = [("o0", call, p, q) for q in range(4)]
            aoff = 2 if cyc == 2 else 0
            self.ew("gpsimd", "tensor_scalar", crow[:, 0:W], trow[:, 0:W],
                    rowc[:, aoff:aoff + 1], rowc[:, 1 + aoff:2 + aoff],
                    ALU.mult, ALU.add, reads=[ktr, ("c", "rowc")],
                    writes=[kcr], tag="crowp")
            self.ew("gpsimd", "tensor_scalar", crow[:, W:2 * W], trow[:, 0:W],
                    rowc[:, 4 + aoff:5 + aoff], rowc[:, 5 + aoff:6 + aoff],
                    ALU.mult, ALU.add, reads=[ktr, ("c", "rowc")],
                    writes=[kcr], tag="crown")
            self.mm(pb[bb_][:, 0:2 * W], cons["onesrF"][:, :],
                    crow[:, 0:2 * W], True, True,
                    reads=[("c", "onesrF"), kcr], writes=[("pb", bb_)],
                    tag="cbmm")
            for (qa, qb_, sgn_a, sgn_b) in ((0, 2, 0, W), (1, 3, W, 0)):
                offa, offb = 2 * qa * W, 2 * qb_ * W
                va = o0[:, offa:offa + 2 * W].rearrange(
                    "p (c t) -> p c t", t=W)
                vb = o0[:, offb:offb + 2 * W].rearrange(
                    "p (c t) -> p c t", t=W)
                ta = utp[:, 0:2 * W].rearrange("p (c t) -> p c t", t=W)
                tb = utp[:, 2 * W:4 * W].rearrange("p (c t) -> p c t", t=W)
                ka, kb_ = ("o0", call, p, qa), ("o0", call, p, qb_)
                self.ew("vector", "tensor_tensor", ta, vb,
                        pb[bb_][:, None, sgn_a:sgn_a + W].broadcast_to(
                            [128, 2, W]), ALU.mult,
                        reads=[kb_, ("pb", bb_)], writes=[ku], tag="umulA")
                self.ew("vector", "tensor_tensor", tb, va,
                        pb[bb_][:, None, sgn_b:sgn_b + W].broadcast_to(
                            [128, 2, W]), ALU.mult,
                        reads=[ka, ("pb", bb_)], writes=[ku], tag="umulB")
                if cyc < 2:
                    self.ew("vector", "tensor_tensor", va, va, ta, ALU.add,
                            reads=[ka, ku], writes=[ka], tag="uaddA")
                    self.ew("vector", "tensor_tensor", vb, vb, tb, ALU.add,
                            reads=[kb_, ku], writes=[kb_], tag="uaddB")
                else:
                    self.ew("vector", "scalar_tensor_tensor", va, va, 0.5,
                            ta, ALU.mult, ALU.add, reads=[ka, ku],
                            writes=[ka], tag="uhalfA")
                    self.ew("vector", "scalar_tensor_tensor", vb, vb, 0.5,
                            tb, ALU.mult, ALU.add, reads=[kb_, ku],
                            writes=[kb_], tag="uhalfB")
            if cyc < 2:
                self.ew("vector", "tensor_copy", obp[:, :], o0[:, :],
                        reads=o0keys, writes=[ko], tag="obcast")

        def emit_resid(ft):
            is_halo, xv, xkeys = tile_info(ft)
            p = ft % 3
            o0v = out0[p][:, :].rearrange("p (c t) -> p c t", t=W)
            o0keys = [("o0", call, p, q) for q in range(4)]
            self.ew("vector", "tensor_tensor", xv, xv, o0v, ALU.add,
                    reads=xkeys + o0keys, writes=xkeys, tag="resid")
            if store is not None and not is_halo:
                dst = y_out[:, :].rearrange("(c p) t -> p c t", p=128)[
                    :, :, ft * W:(ft + 1) * W]
                self.dma(dst, xv, reads=xkeys, writes=[("yout", call, ft)],
                         tag="ystore")

        for t in range(min(3, ntile)):
            emit_xhat(t)
        first = [True]

        def cyc_full(t, cyc):
            emit_cycle(t, cyc)
            if first[0]:
                emit_a2()
                first[0] = False
            emit_tail(t, cyc)

        for k in range(0, ntile, 3):
            tiles = [t for t in range(k, min(k + 3, ntile))]
            for cyc in range(2):
                for t in tiles:
                    cyc_full(t, cyc)
            for t in tiles:
                cyc_full(t, 2)
                emit_resid(t)
                if t + 3 < ntile:
                    emit_xhat(t + 3)

    def attn_phase(self, cons, grow, mneg, x_sb, xh_sb, sc, job=0,
                   ffn2=None):
        nc, E, TC, NQT = self.nc, self.E, self.TC, self.NQT
        pb = self.pb
        jj = str(job)
        wq = self.load_weight("wq", 8192, sc, tag=f"a{job}")
        wkv = self.load_weight("wkv", 4096, sc, tag=f"a{job}")
        womb = self.sb("womb" + jj, [128, 8192], BF16, sc)
        wochunk = self.sb("wochunk" + jj, [128, 1024], BF16, sc)
        mcb = self.sb("mcb" + jj, [128, 128], BF16, sc)
        scol = self.sb("scol" + jj, [128, 2], F32, sc)
        eyep = cons["eyepair"]
        xbq2 = [self.sb(f"xbq{p}" + jj, [128, 1024], BF16, sc)
                for p in range(2)]
        xsq2 = [self.sb(f"xsq{p}" + jj, [128, 1024], BF16, sc)
                for p in range(2)]
        colb2 = [self.sb(f"colb{p}" + jj, [128, 4], F32, sc)
                 for p in range(2)]
        wsq = self.sb("wsq" + jj, [128, 1024], F32, sc)
        wskv = self.sb("wskv" + jj, [128, 512], F32, sc)
        qb = self.sb("qb" + jj, [128, 1024], BF16, sc)
        kb = self.sb("kb" + jj, [128, 256], BF16, sc)
        q2b = self.sb("q2b" + jj, [128, 1024], BF16, sc)
        rsq = self.sb("rsq" + jj, [128, 24], F32, sc)
        ropq = self.sb("ropq" + jj, [128, 1024], BF16, sc)
        ropk = self.sb("ropk" + jj, [128, 256], BF16, sc)
        rt1 = self.sb("rt1" + jj, [128, 512], BF16, sc)
        rt2 = self.sb("rt2" + jj, [128, 512], BF16, sc)
        sct = self.sb("sct" + jj, [128, 512], BF16, sc)
        sst = self.sb("sst" + jj, [128, 512], BF16, sc)
        sck = self.sb("sck" + jj, [128, 128], BF16, sc)
        ssk = self.sb("ssk" + jj, [128, 128], BF16, sc)
        qaug = [self.sb(f"qaug{g}" + jj, [65, 1024], BF16, sc)
                for g in range(4)]
        kring = [self.sb(f"kring{r}" + jj, [65, 512], BF16, sc)
                 for r in range(4)]
        vring = [self.sb(f"vring{r}" + jj, [128, 256], BF16, sc)
                 for r in range(4)]
        attb = [self.sb(f"attb{g}" + jj, [128, 3072], BF16, sc)
                for g in range(2)]
        ybuf = self.sb("ybuf" + jj, [128, 2048], BF16, sc)
        srow = self.sb("srow" + jj, [1, 1024], F32, sc)
        rbsb = self.sb("rbsb" + jj, [128, 512], F32, sc)
        strw = self.sb("strw" + jj, [1, 256], F32, sc)
        if ffn2 is not None:
            f2call, f2statsd, f2ccin, f2ccout, f2msum = ffn2
            a1xb = self.sb("a1xb" + jj, [128, 8 * FT], BF16, sc)
            a1sq = self.sb("a1sq" + jj, [128, 8 * FT], BF16, sc)
            a1st = self.sb("a1st" + jj, [1, 2 * FT], F32, sc)
            a1rd = self.sb("a1rd" + jj, [128, 8], F32, sc)
            a1bc = self.sb("a1bc" + jj, [128, 512], F32, sc)
            self.ew("vector", "memset", f2msum[:, :], 0.0, reads=(),
                    writes=[("msum",)])

        def a1_tile(ft):
            W = FT
            NFT = self.NFT
            xkeys = [("x", 2 * ft), ("x", 2 * ft + 1)]
            xv = x_sb[:, :].rearrange("p (c t) -> p c t", t=TC)[
                :, :, ft * W:(ft + 1) * W]
            xbv = a1xb[:, :].rearrange("p (c t) -> p c t", t=W)
            kxb, kxs, kst = ("a1xb",), ("a1sq",), ("a1st",)
            self.ew("vector", "tensor_copy", xbv, xv, reads=xkeys,
                    writes=[kxb], tag="fa1cast")
            for ic in range(8):
                self.mm(pb[4][0:1, 0:W], cons["onescB"][:, :],
                        a1xb[:, ic * W:(ic + 1) * W], ic == 0, ic == 7,
                        reads=[("c", "onescB"), kxb], writes=[("pb", 4)],
                        tag="fa1s1")
            self.act(a1sq[:, :], a1xb[:, :], AF.Square, reads=[kxb],
                     writes=[kxs], tag="fa1sq")
            for ic in range(8):
                self.mm(pb[4][0:1, W:2 * W], cons["onescB"][:, :],
                        a1sq[:, ic * W:(ic + 1) * W], ic == 0, ic == 7,
                        reads=[("c", "onescB"), kxs], writes=[("pb", 4)],
                        tag="fa1s2")
            self.ew("scalar", "mul", a1st[:, 0:W], pb[4][0:1, 0:W], 1.0 / D,
                    reads=[("pb", 4)], writes=[kst], tag="fa1m")
            self.act(a1st[:, W:2 * W], a1st[:, 0:W], AF.Square,
                     reads=[kst], writes=[kst], tag="fa1msq")
            self.ew("vector", "scalar_tensor_tensor", a1st[:, W:2 * W],
                    pb[4][0:1, W:2 * W], 1.0 / D, a1st[:, W:2 * W],
                    ALU.mult, ALU.subtract, reads=[("pb", 4), kst],
                    writes=[kst], tag="fa1var")
            self.act(a1st[:, W:2 * W], a1st[:, W:2 * W], AF.Ln,
                     reads=[kst, ("c", "epsc")], writes=[kst],
                     bias=cons["epsc"][0:1, 0:1])
            self.act(a1st[:, W:2 * W], a1st[:, W:2 * W], AF.Exp,
                     reads=[kst], writes=[kst], scale=-0.5)
            self.dma(f2statsd[f2call * (NFT + 1) + ft:
                              f2call * (NFT + 1) + ft + 1, 0:2 * W],
                     a1st[0:1, 0:2 * W], reads=[kst],
                     writes=[("std", f2call, ft)], tag="fa1save")
            self.dma(a1bc[:, 0:2 * W],
                     f2statsd[f2call * (NFT + 1) + ft:
                              f2call * (NFT + 1) + ft + 1,
                              0:2 * W].partition_broadcast(128),
                     reads=[("std", f2call, ft)], writes=[("a1bc",)],
                     tag="fa1bc")
            self.ew("vector", "tensor_tensor", xbv, xv,
                    a1bc[:, None, 0:W].broadcast_to([128, 8, W]),
                    ALU.subtract, reads=xkeys + [("a1bc",)], writes=[kxb],
                    tag="fa1sub")
            self.ew("vector", "tensor_tensor", xbv, xbv,
                    a1bc[:, None, W:2 * W].broadcast_to([128, 8, W]),
                    ALU.mult, reads=[kxb, ("a1bc",)], writes=[kxb],
                    tag="fa1mul")
            self.ew("vector", "reduce_sum", a1rd[:, 0:8], xbv, axis=AX.X,
                    reads=[kxb], writes=[("a1rd",)], tag="fa1red")
            self.ew("vector", "tensor_tensor", f2msum[:, :], f2msum[:, :],
                    a1rd[:, 0:8], ALU.add, reads=[("msum",), ("a1rd",)],
                    writes=[("msum",)], tag="fa1acc")

        if "wq" not in self.wcached:
            self.spill_weight("wq", 8192, wq)
            self.spill_weight("wkv", 4096, wkv)
        for ic in range(8):
            self.mm(pb[4][0:1, 0:512], cons["onescB"][:, :],
                    wq[:, ic * 1024:ic * 1024 + 512], ic == 0, ic == 7,
                    reads=[("c", "onescB"), ("fw", "wq")], writes=[("pb", 4)],
                    tag="wsum")
            self.mm(pb[4][0:1, 512:1024] if False else pb[5][0:1, 0:512],
                    cons["onescB"][:, :],
                    wq[:, ic * 1024 + 512:(ic + 1) * 1024], ic == 0, ic == 7,
                    reads=[("c", "onescB"), ("fw", "wq")], writes=[("pb", 5)],
                    tag="wsum")
            self.mm(pb[6][0:1, 0:512], cons["onescB"][:, :],
                    wkv[:, ic * 512:(ic + 1) * 512], ic == 0, ic == 7,
                    reads=[("c", "onescB"), ("fw", "wkv")], writes=[("pb", 6)],
                    tag="wsum")
        self.ew("vector", "tensor_copy", srow[:, 0:512], pb[4][0:1, 0:512],
                reads=[("pb", 4)], writes=[("srow",)], tag="wsev")
        self.ew("vector", "tensor_copy", srow[:, 512:1024], pb[5][0:1, 0:512],
                reads=[("pb", 5)], writes=[("srow",)], tag="wsev")
        for h in range(2):
            self.mm(pb[4][:, 0:512], cons["negrF"][:, :],
                    srow[:, h * 512:(h + 1) * 512], True, True,
                    reads=[("c", "negrF"), ("srow",)], writes=[("pb", 4)],
                    tag="wsbc")
            self.ew("vector", "tensor_copy", wsq[:, h * 512:(h + 1) * 512],
                    pb[4][:, 0:512], reads=[("pb", 4)], writes=[("wsq",)],
                    tag="wsev2")
        self.ew("vector", "tensor_copy", srow[:, 0:512], pb[6][0:1, 0:512],
                reads=[("pb", 6)], writes=[("srow",)], tag="wsev")
        self.mm(pb[4][:, 0:512], cons["negrF"][:, :], srow[:, 0:512],
                True, True, reads=[("c", "negrF"), ("srow",)],
                writes=[("pb", 4)], tag="wsbc")
        self.ew("vector", "tensor_copy", wskv[:, :], pb[4][:, 0:512],
                reads=[("pb", 4)], writes=[("wskv",)], tag="wsev2")

        for gg in range(4):
            g = gg % 2
            for h8 in range(8):
                h = 8 * g + h8
                self.ew("vector", "tensor_scalar",
                        qaug[gg][64:65, h8 * 128:(h8 + 1) * 128],
                        cons["onesrF"][:, 0:128], mneg[0:1, h:h + 1], None,
                        ALU.mult, reads=[("c", "onesrF"), ("c", "mneg")],
                        writes=[("qaug", gg)], tag="qaug64")
        for r in range(4):
            self.ew("vector", "memset", kring[r][64:65, :], 1.0, reads=(),
                    writes=[("kring", r)], tag="kr64")

        def precast(i, is_halo):
            if is_halo:
                src, TT, toff = xh_sb, HT, i * 128
                xkeys = [("xh", i)]
            else:
                src, TT, toff = x_sb, TC, i * 128
                xkeys = [("x", i)]
            xv = src[:, :].rearrange("p (c t) -> p c t", t=TT)[
                :, :, toff:toff + 128]
            p = i % 2
            xbv = xbq2[p][:, :].rearrange("p (c t) -> p c t", t=128)
            self.ew("vector", "tensor_copy", xbv, xv, reads=xkeys,
                    writes=[("xbq", p)], tag="l2cast")
            self.act(xsq2[p][:, :], xbq2[p][:, :], AF.Square,
                     reads=[("xbq", p)], writes=[("xsq", p)], tag="l2sq")

        def x1(i, is_halo, kv_only):
            p = i % 2
            xbq, xsq, colb = xbq2[p], xsq2[p], colb2[p]
            kxb, kxs, kcb = ("xbq", p), ("xsq", p), ("colb", p)
            if not kv_only:
                for ic in range(8):
                    lhs = xbq[:, ic * 128:(ic + 1) * 128]
                    self.mm(pb[0][:, 0:512], lhs,
                            wq[:, ic * 1024:ic * 1024 + 512], ic == 0,
                            ic == 7, reads=[kxb, ("fw", "wq")],
                            writes=[("pb", 0)], tag="qmm")
                    self.mm(pb[1][:, 0:512], lhs,
                            wq[:, ic * 1024 + 512:(ic + 1) * 1024], ic == 0,
                            ic == 7, reads=[kxb, ("fw", "wq")],
                            writes=[("pb", 1)], tag="qmm")
            for ic in range(8):
                self.mm(pb[2][:, 0:512], xbq[:, ic * 128:(ic + 1) * 128],
                        wkv[:, ic * 512:(ic + 1) * 512],
                        ic == 0, ic == 7, reads=[kxb, ("fw", "wkv")],
                        writes=[("pb", 2)], tag="kvmm")
            for ic in range(8):
                self.mm(pb[3][:, 0:1], xbq[:, ic * 128:(ic + 1) * 128],
                        cons["onescB"][:, :], ic == 0, ic == 7,
                        reads=[kxb, ("c", "onescB")], writes=[("pb", 3)],
                        tag="l2s1")
            for ic in range(8):
                self.mm(pb[3][:, 1:2], xsq[:, ic * 128:(ic + 1) * 128],
                        cons["onescB"][:, :], ic == 0, ic == 7,
                        reads=[kxs, ("c", "onescB")], writes=[("pb", 3)],
                        tag="l2s2")
            self.ew("scalar", "mul", colb[:, 0:1], pb[3][:, 0:1], 1.0 / D,
                    reads=[("pb", 3)], writes=[kcb], tag="l2m")
            self.act(colb[:, 2:3], colb[:, 0:1], AF.Square,
                     reads=[kcb], writes=[kcb], tag="l2msq")
            self.ew("vector", "scalar_tensor_tensor", colb[:, 2:3],
                    pb[3][:, 1:2], 1.0 / D, colb[:, 2:3], ALU.mult,
                    ALU.subtract, reads=[("pb", 3), kcb],
                    writes=[kcb], tag="l2var")
            self.act(colb[:, 2:3], colb[:, 2:3], AF.Ln,
                     reads=[kcb, ("c", "epsc")], writes=[kcb],
                     bias=cons["epsc"][:, 0:1])
            self.act(colb[:, 1:2], colb[:, 2:3], AF.Exp,
                     reads=[kcb], writes=[kcb], scale=-0.5,
                     tag="l2rstd")
            self.ew("vector", "tensor_tensor", colb[:, 3:4], colb[:, 0:1],
                    colb[:, 1:2], ALU.mult, reads=[kcb],
                    writes=[kcb], tag="l2rm")
            if not kv_only:
                for h in range(2):
                    self.act(qb[:, h * 512:(h + 1) * 512], pb[h][:, 0:512],
                             AF.Copy, reads=[("pb", h), kcb],
                             writes=[("qb",)], scale=colb[:, 1:2],
                             tag="qfix1")
                    self.ew("vector", "scalar_tensor_tensor",
                            qb[:, h * 512:(h + 1) * 512],
                            wsq[:, h * 512:(h + 1) * 512], colb[:, 3:4],
                            qb[:, h * 512:(h + 1) * 512], ALU.mult, ALU.add,
                            reads=[("wsq",), kcb, ("qb",)],
                            writes=[("qb",)], tag="qfix2")
            r = (i + (0 if is_halo else 2)) % 4
            self.act(kb[:, :], pb[2][:, 0:256], AF.Copy,
                     reads=[("pb", 2), kcb], writes=[("kb",)],
                     scale=colb[:, 1:2], tag="kfix1")
            self.ew("vector", "scalar_tensor_tensor", kb[:, :],
                    wskv[:, 0:256], colb[:, 3:4], kb[:, :], ALU.mult, ALU.add,
                    reads=[("wskv",), kcb, ("kb",)], writes=[("kb",)],
                    tag="kfix2")
            self.act(vring[r][:, :], pb[2][:, 256:512], AF.Copy,
                     reads=[("pb", 2), kcb], writes=[("vring", r)],
                     scale=colb[:, 1:2], tag="vfix1")
            self.ew("vector", "scalar_tensor_tensor", vring[r][:, :],
                    wskv[:, 256:512], colb[:, 3:4], vring[r][:, :], ALU.mult,
                    ALU.add, reads=[("wskv",), kcb, ("vring", r)],
                    writes=[("vring", r)], tag="vfix2")

        def x2a(i, is_halo):
            kvi = i + (0 if is_halo else 2)
            cos = cons["cosq"][:, kvi * 32:(kvi + 1) * 32]
            sin = cons["sinq"][:, kvi * 32:(kvi + 1) * 32]
            self.ew("vector", "tensor_tensor", q2b[:, 0:256], kb[:, :],
                    kb[:, :], ALU.mult, reads=[("kb",)], writes=[("q2b",)],
                    tag="krms")
            self.ew("vector", "reduce_sum", rsq[:, 16:20],
                    q2b[:, 0:256].rearrange("p (h d) -> p h d", d=64),
                    axis=AX.X, reads=[("q2b",)], writes=[("rsq",)])
            self.act(rsq[:, 16:20], rsq[:, 16:20], AF.Ln,
                     reads=[("rsq",), ("c", "epsc")], writes=[("rsq",)],
                     bias=cons["epsc"][:, 1:2], scale=1.0 / HD)
            self.act(rsq[:, 16:20], rsq[:, 16:20], AF.Exp, reads=[("rsq",)],
                     writes=[("rsq",)], scale=-0.5)
            self.ew("vector", "tensor_tensor",
                    sck[:, :].rearrange("p (h j) -> p h j", j=32),
                    cos[:, None, :].broadcast_to([128, 4, 32]),
                    rsq[:, 16:20, None].broadcast_to([128, 4, 32]), ALU.mult,
                    reads=[("c", "cosq"), ("rsq",)], writes=[("sck",)],
                    tag="sck")
            self.ew("vector", "tensor_tensor",
                    ssk[:, :].rearrange("p (h j) -> p h j", j=32),
                    sin[:, None, :].broadcast_to([128, 4, 32]),
                    rsq[:, 16:20, None].broadcast_to([128, 4, 32]), ALU.mult,
                    reads=[("c", "sinq"), ("rsq",)], writes=[("ssk",)],
                    tag="ssk")
            kv1 = kb[:, :].rearrange("p (h d) -> p h d", d=64)[:, :, 0:32]
            kv2 = kb[:, :].rearrange("p (h d) -> p h d", d=64)[:, :, 32:64]
            scv = sck[:, :].rearrange("p (h j) -> p h j", j=32)
            ssv = ssk[:, :].rearrange("p (h j) -> p h j", j=32)
            r1 = rt1[:, 0:128].rearrange("p (h d) -> p h d", d=32)
            r2 = rt2[:, 0:128].rearrange("p (h d) -> p h d", d=32)
            ro1 = ropk[:, :].rearrange("p (h d) -> p h d", d=64)[:, :, 0:32]
            ro2 = ropk[:, :].rearrange("p (h d) -> p h d", d=64)[:, :, 32:64]
            self.ew("vector", "tensor_tensor", r1, kv1, scv, ALU.mult,
                    reads=[("kb",), ("sck",)], writes=[("rt1",)])
            self.ew("vector", "tensor_tensor", r2, kv2, ssv, ALU.mult,
                    reads=[("kb",), ("ssk",)], writes=[("rt2",)])
            self.ew("vector", "tensor_tensor", ro1, r1, r2, ALU.add,
                    reads=[("rt1",), ("rt2",)], writes=[("ropk",)])
            self.ew("vector", "tensor_tensor", r1, kv2, scv, ALU.mult,
                    reads=[("kb",), ("sck",)], writes=[("rt1",)])
            self.ew("vector", "tensor_tensor", r2, kv1, ssv, ALU.mult,
                    reads=[("kb",), ("ssk",)], writes=[("rt2",)])
            self.ew("vector", "tensor_tensor", ro2, r1, r2, ALU.subtract,
                    reads=[("rt1",), ("rt2",)], writes=[("ropk",)])
            if is_halo:
                return
            self.ew("vector", "tensor_tensor", q2b[:, :], qb[:, :], qb[:, :],
                    ALU.mult, reads=[("qb",)], writes=[("q2b",)], tag="qrms")
            self.ew("vector", "reduce_sum", rsq[:, 0:16],
                    q2b[:, :].rearrange("p (h d) -> p h d", d=64), axis=AX.X,
                    reads=[("q2b",)], writes=[("rsq",)])
            self.act(rsq[:, 0:16], rsq[:, 0:16], AF.Ln,
                     reads=[("rsq",), ("c", "epsc")], writes=[("rsq",)],
                     bias=cons["epsc"][:, 1:2], scale=1.0 / HD)
            self.act(rsq[:, 0:16], rsq[:, 0:16], AF.Exp, reads=[("rsq",)],
                     writes=[("rsq",)], scale=-0.5)
            self.ew("vector", "tensor_tensor", rsq[:, 0:16], rsq[:, 0:16],
                    grow[:, :], ALU.mult, reads=[("rsq",), ("c", "grow")],
                    writes=[("rsq",)], tag="qgsc")
            self.ew("vector", "tensor_tensor",
                    sct[:, :].rearrange("p (h j) -> p h j", j=32),
                    cos[:, None, :].broadcast_to([128, 16, 32]),
                    rsq[:, 0:16, None].broadcast_to([128, 16, 32]), ALU.mult,
                    reads=[("c", "cosq"), ("rsq",)], writes=[("sct",)],
                    tag="sct")
            self.ew("vector", "tensor_tensor",
                    sst[:, :].rearrange("p (h j) -> p h j", j=32),
                    sin[:, None, :].broadcast_to([128, 16, 32]),
                    rsq[:, 0:16, None].broadcast_to([128, 16, 32]), ALU.mult,
                    reads=[("c", "sinq"), ("rsq",)], writes=[("sst",)],
                    tag="sst")
            qv1 = qb[:, :].rearrange("p (h d) -> p h d", d=64)[:, :, 0:32]
            qv2 = qb[:, :].rearrange("p (h d) -> p h d", d=64)[:, :, 32:64]
            scv = sct[:, :].rearrange("p (h j) -> p h j", j=32)
            ssv = sst[:, :].rearrange("p (h j) -> p h j", j=32)
            r1 = rt1[:, :].rearrange("p (h d) -> p h d", d=32)
            r2 = rt2[:, :].rearrange("p (h d) -> p h d", d=32)
            ro1 = ropq[:, :].rearrange("p (h d) -> p h d", d=64)[:, :, 0:32]
            ro2 = ropq[:, :].rearrange("p (h d) -> p h d", d=64)[:, :, 32:64]
            self.ew("vector", "tensor_tensor", r1, qv1, scv, ALU.mult,
                    reads=[("qb",), ("sct",)], writes=[("rt1",)])
            self.ew("vector", "tensor_tensor", r2, qv2, ssv, ALU.mult,
                    reads=[("qb",), ("sst",)], writes=[("rt2",)])
            self.ew("vector", "tensor_tensor", ro1, r1, r2, ALU.add,
                    reads=[("rt1",), ("rt2",)], writes=[("ropq",)])
            self.ew("vector", "tensor_tensor", r1, qv2, scv, ALU.mult,
                    reads=[("qb",), ("sct",)], writes=[("rt1",)])
            self.ew("vector", "tensor_tensor", r2, qv1, ssv, ALU.mult,
                    reads=[("qb",), ("sst",)], writes=[("rt2",)])
            self.ew("vector", "tensor_tensor", ro2, r1, r2, ALU.subtract,
                    reads=[("rt1",), ("rt2",)], writes=[("ropq",)])

        def x2b(i, is_halo):
            kvi = i + (0 if is_halo else 2)
            r = kvi % 4
            for kv in range(4):
                self.mm(pb[2][0:64, kv * 128:(kv + 1) * 128],
                        ropk[:, kv * 64:(kv + 1) * 64], cons["ident"][:, :],
                        True, True, reads=[("ropk",), ("c", "ident")],
                        writes=[("pb", 2)], tag="ktr")
            self.act(kring[r][0:64, :], pb[2][0:64, 0:512], AF.Copy,
                     reads=[("pb", 2)],
                     writes=[("kring", r)], tag="krev")
            if is_halo:
                return
            par = i % 2
            for w in range(4):
                bank = w % 2
                for j in range(4):
                    h = 4 * w + j
                    self.mm(pb[bank][0:64, j * 128:(j + 1) * 128],
                            ropq[:, h * 64:(h + 1) * 64], cons["ident"][:, :],
                            True, True, reads=[("ropq",), ("c", "ident")],
                            writes=[("pb", bank)], tag="qtr")
                gg = 2 * (w // 2) + par
                off = (w % 2) * 512
                self.act(qaug[gg][0:64, off:off + 512], pb[bank][0:64, 0:512],
                         AF.Copy, reads=[("pb", bank)], writes=[("qaug", gg)],
                         tag="qtrev")

        def y_round(i, g):
            par = i % 2
            gg = 2 * g + par
            ktiles = [i, i + 1, i + 2]
            for n_kt, kvi in enumerate(ktiles):
                r = kvi % 4
                sb0 = 4 if n_kt != 1 else 6
                for kv2 in range(2):
                    kv = 2 * g + kv2
                    self.mm(pb[sb0 + kv2][:, 0:512],
                            kring[r][0:65, kv * 128:(kv + 1) * 128],
                            qaug[gg][0:65, kv2 * 512:(kv2 + 1) * 512],
                            True, True, reads=[("kring", r), ("qaug", gg)],
                            writes=[("pb", sb0 + kv2)], tag="smm")
                ab = attb[g][:, n_kt * 1024:(n_kt + 1) * 1024]
                for b in range(2):
                    self.act(ab[:, b * 512:(b + 1) * 512],
                             pb[sb0 + b][:, 0:512], AF.Exp,
                             reads=[("pb", sb0 + b)], writes=[("attb", g)],
                             tag="exp")
                mk = None
                if n_kt == 2:
                    mk, mkk = cons["cmask"], ("c", "cmask")
                elif n_kt == 0:
                    if i == 0:
                        mk, mkk = cons["mA"], ("c", "mA")
                    elif i == 1:
                        mk, mkk = cons["mC"], ("c", "mC")
                    else:
                        mk, mkk = cons["wmask"], ("c", "wmask")
                elif n_kt == 1 and i == 0:
                    mk, mkk = cons["mB"], ("c", "mB")
                if mk is not None:
                    self.ew("vector", "tensor_tensor",
                            ab.rearrange("p (h t) -> p h t", t=128),
                            ab.rearrange("p (h t) -> p h t", t=128),
                            mk[:, None, :].broadcast_to([128, 8, 128]),
                            ALU.mult, reads=[("attb", g), mkk],
                            writes=[("attb", g)], tag="mask")
            for half in range(2):
                for n_kt in range(3):
                    self.mm(pb[7][32 * half:32 * half + 1, 0:512],
                            cons["onescB"][:, :],
                            attb[g][:, n_kt * 1024 + half * 512:
                                    n_kt * 1024 + (half + 1) * 512],
                            n_kt == 0, n_kt == 2,
                            reads=[("attb", g), ("c", "onescB")],
                            writes=[("pb", 7)], tag="sums",
                            tp=(0, 32) if half else None)
            for kv2 in range(2):
                kv = 2 * g + kv2
                for j in range(4):
                    h8 = kv2 * 4 + j
                    pr = h8 // 2
                    odd = h8 % 2
                    for n_kt in range(3):
                        r = (i + n_kt) % 4
                        vl = vring[r][:, kv * 64:(kv + 1) * 64]
                        self.mm(pb[6][odd * 64:odd * 64 + 64,
                                      pr * 128:(pr + 1) * 128],
                                vl,
                                attb[g][:, n_kt * 1024 + h8 * 128:
                                        n_kt * 1024 + (h8 + 1) * 128],
                                n_kt == 0, n_kt == 2,
                                reads=[("vring", r), ("attb", g)],
                                writes=[("pb", 6)], tag="avmm",
                                tp=(0, 64) if odd else None)
            self.act(srow[:, 0:512], pb[7][0:1, 0:512], AF.Ln,
                     reads=[("pb", 7)], writes=[("srow",)], tag="sln")
            self.act(srow[:, 512:1024], pb[7][32:33, 0:512], AF.Ln,
                     reads=[("pb", 7)], writes=[("srow",)], tag="sln")
            self.act(srow[:, 0:1024], srow[:, 0:1024], AF.Exp,
                     reads=[("srow",)], writes=[("srow",)], scale=-1.0,
                     tag="sexp")
            for pr in range(4):
                he = srow[:, pr * 256:pr * 256 + 128]
                ho = srow[:, pr * 256 + 128:pr * 256 + 256]
                self.mm(pb[7][:, pr * 128:(pr + 1) * 128],
                        cons["lo64"][:, :], he, True, False,
                        reads=[("c", "lo64"), ("srow",)],
                        writes=[("pb", 7)], tag="rbmm")
                self.mm(pb[7][:, pr * 128:(pr + 1) * 128],
                        cons["hi64"][:, :], ho, False, True,
                        reads=[("c", "hi64"), ("srow",)],
                        writes=[("pb", 7)], tag="rbmm")
            self.ew("scalar", "copy", rbsb[:, :], pb[7][:, 0:512],
                    reads=[("pb", 7)], writes=[("rbsb",)], tag="rbev")
            yv = ybuf[:, :].rearrange("p (c t) -> p c t", t=256)[
                :, 4 * g:4 * g + 4, par * 128:par * 128 + 128]
            self.ew("vector", "tensor_tensor", yv,
                    pb[6][:, 0:512].rearrange("p (c t) -> p c t", t=128),
                    rbsb[:, :].rearrange("p (c t) -> p c t", t=128),
                    ALU.mult, reads=[("pb", 6), ("rbsb",)],
                    writes=[("ybuf", par, g)], tag="ydiv")

        def wo_resid(i):
            ykeys = [("ybuf", 0, 0), ("ybuf", 0, 1), ("ybuf", 1, 0),
                     ("ybuf", 1, 1)]
            xpv = x_sb[:, :].rearrange("p (c t) -> p c t", t=TC)
            xk = [("x", i - 1), ("x", i)]
            for ocg in range(2):
                for oc2 in range(4):
                    oc = 4 * ocg + oc2
                    bank = pb[4 + oc2 // 2]
                    slot = (oc2 % 2) * 256
                    for ic in range(8):
                        self.mm(bank[:, slot:slot + 256],
                                womb[:, ic * 1024 + oc * 128:
                                     ic * 1024 + oc * 128 + 128],
                                ybuf[:, ic * 256:(ic + 1) * 256],
                                ic == 0, ic == 7,
                                reads=[("fw", "womb")] + ykeys,
                                writes=[("pb", 4 + oc2 // 2)], tag="womm")
                for b2 in range(2):
                    ch = 4 * ocg + 2 * b2
                    xslice = xpv[:, ch:ch + 2, (i - 1) * 128:(i + 1) * 128]
                    self.ew("vector", "tensor_tensor", xslice, xslice,
                            pb[4 + b2][:, 0:512].rearrange(
                                "p (c t) -> p c t", t=256),
                            ALU.add, reads=xk + [("pb", 4 + b2)], writes=xk,
                            tag="aresid")

        for j in range(2):
            precast(j, True)
            x1(j, True, True)
            x2a(j, True)
            x2b(j, True)
        if "womb" in self.wcached:
            off = self.wcached["womb"]
            self.dma(womb[:, :], self.wdram[:, off:off + 8192],
                     reads=[("wd", "womb"), ("wspill",)],
                     writes=[("fw", "womb")], tag="wfast")
        for c in range(8) if "womb" not in self.wcached else []:
            for j in range(2):
                self.mm(pb[3][:, j:j + 1], cons["sel2"][:, :],
                        cons["pm2"][:, 2 * c + j:2 * c + j + 1], True, True,
                        reads=[("c", "sel2"), ("c", "pm2")],
                        writes=[("pb", 3)], tag="pmcol")
            self.ew("vector", "tensor_copy", scol[:, :], pb[3][:, 0:2],
                    reads=[("pb", 3)], writes=[("scol",)])
            self.ew("vector", "tensor_scalar", mcb[:, 0:64], eyep[:, 0:64],
                    scol[:, 0:1], None, ALU.mult,
                    reads=[("c", "eyepair"), ("scol",)], writes=[("mcb",)])
            self.ew("vector", "tensor_scalar", mcb[:, 64:128], eyep[:, 64:128],
                    scol[:, 1:2], None, ALU.mult,
                    reads=[("c", "eyepair"), ("scol",)], writes=[("mcb",)])
            self.dma(self.wstage[:, :], self.dram_in["wo"][:, c * 1024:
                                                           (c + 1) * 1024],
                     reads=(), writes=[("wstg",)], tag="wload")
            self.ew("vector", "tensor_copy", wochunk[:, :], self.wstage[:, :],
                    reads=[("wstg",)], writes=[("woch",)])
            for j in range(2):
                self.mm(pb[4][:, 0:512], mcb[:, :],
                        wochunk[:, j * 512:(j + 1) * 512], True, True,
                        reads=[("mcb",), ("woch",)], writes=[("pb", 4)],
                        tag="wommm")
                self.ew("scalar", "copy", womb[:, c * 1024 + j * 512:
                                               c * 1024 + (j + 1) * 512],
                        pb[4][:, 0:512], reads=[("pb", 4)],
                        writes=[("fw", "womb")], tag="womev")
        if "womb" not in self.wcached:
            self.spill_weight("womb", 8192, womb)

        precast(0, False)
        x1(0, False, False)
        x2a(0, False)
        precast(1, False)
        for i in range(NQT):
            if i + 2 < NQT:
                precast(i + 2, False)
            if i + 1 < NQT:
                x1(i + 1, False, False)
            x2b(i, False)
            y_round(i, 0)
            y_round(i, 1)
            if i + 1 < NQT:
                x2a(i + 1, False)
            if i % 2 == 1:
                wo_resid(i)
                if ffn2 is not None:
                    a1_tile((i - 1) // 2)
        if ffn2 is not None:
            self.emit_allreduce(f2msum, f2ccin, f2ccout)


def make_in_maps(inputs, k, TC, n_pairs):
    consts = build_consts()
    wst = stage_weights(inputs)
    x = np.asarray(inputs["L0_attn_x"], np.float32)
    in_maps = []
    for core in range(2 * n_pairs):
        p, half = core // 2, core % 2
        m = dict(wst)
        m.update(consts)
        xjobs = [x[j * n_pairs + p] for j in range(x.shape[0] // n_pairs)]
        m.update(build_core_inputs(xjobs, half, TC, consts))
        m = {n: v for n, v in m.items() if n in k.dram_in}
        missing = set(k.dram_in) - set(m)
        assert not missing, missing
        in_maps.append(m)
    return in_maps


def run(inputs, TC=2048, n_pairs=4):
    k = Kern(TC, n_pairs).build()
    in_maps = make_in_maps(inputs, k, TC, n_pairs)
    res = run_bass_kernel_spmd(k.nc, in_maps, core_ids=list(range(2 * n_pairs)))
    B = np.asarray(inputs["L0_attn_x"]).shape[0]
    outs = []
    for b in range(B):
        j, p = b // n_pairs, b % n_pairs
        y0 = res.results[2 * p][f"y{j}"]
        y1 = res.results[2 * p + 1][f"y{j}"]
        outs.append(np.concatenate([y0, y1], axis=1).T)
    return np.stack(outs)


def kernel(**inputs):
    return run(inputs, TC=2048, n_pairs=4).astype(np.float32)


# version 30
# speedup vs baseline: 1.0215x; 1.0215x over previous
from contextlib import ExitStack

import numpy as np
import ml_dtypes

import concourse.bass as bass
import concourse.mybir as mybir
from concourse.bass_utils import run_bass_kernel_spmd

F32 = mybir.dt.float32
BF16 = mybir.dt.bfloat16
AF = mybir.ActivationFunctionType
ALU = mybir.AluOpType
AX = mybir.AxisListType
BF = ml_dtypes.bfloat16

D = 1024
H = 16
KVH = 4
HD = 64
FT = 256
HT = 256
EPS_LN = 1e-5
EPS_RMS = 1e-6

ENGINES = ("tensor", "vector", "scalar", "gpsimd", "sync")
SELF_SYNC = True
NDMA = 16
A1_IN_ATTN = False


class Emitter:
    def __init__(self):
        self.ops = []
        self.barriers = []

    def op(self, engine, fn, reads=(), writes=(), dma=False, tag=""):
        self.ops.append(dict(engine=engine, fn=fn, reads=tuple(reads),
                             writes=tuple(writes), dma=dma, tag=tag,
                             idx=len(self.ops)))

    def dma_fence(self, emitfn):
        emitfn()

    def barrier(self):
        if not self.barriers or self.barriers[-1] != len(self.ops):
            self.barriers.append(len(self.ops))

    def finalize(self, nc, sems, dma_sem):
        ops, n = self.ops, len(self.ops)
        last_write, readers = {}, {}
        deps = [set() for _ in range(n)]
        for j, op in enumerate(ops):
            for r in op["reads"]:
                if r in last_write:
                    deps[j].add(last_write[r])
            for w in op["writes"]:
                if w in last_write:
                    deps[j].add(last_write[w])
                for rd in readers.get(w, ()):
                    deps[j].add(rd)
            for r in op["reads"]:
                readers.setdefault(r, []).append(j)
            for w in op["writes"]:
                last_write[w] = j
                readers[w] = []
        bidx = [0] * n
        bi = 0
        for j in range(n):
            while bi < len(self.barriers) and self.barriers[bi] <= j:
                bi += 1
            bidx[j] = bi

        def needs_edge(i, j):
            oi, oj = ops[i], ops[j]
            if oi["dma"]:
                return True
            if bidx[i] < bidx[j]:
                return False
            if oi["engine"] == oj["engine"]:
                if oi["engine"] == "tensor":
                    return False
                if not SELF_SYNC:
                    return False
            return True

        def prodkey(i):
            if ops[i]["dma"]:
                return ("dmaop", i)
            return ops[i]["engine"]

        for j in range(n):
            best = {}
            for i in deps[j]:
                if not needs_edge(i, j):
                    continue
                k = prodkey(i)
                if k[0] == "dmaop" if isinstance(k, tuple) else False:
                    best[k] = i
                else:
                    best[k] = max(best.get(k, -1), i)
            deps[j] = set(best.values())
        published = [False] * n
        for j in range(n):
            for i in deps[j]:
                published[i] = True
        for op in ops:
            if op["dma"]:
                published[op["idx"]] = True
        rank = [0] * n
        lane = [0] * n
        cnt = {e: 0 for e in ENGINES}
        lane_cnt = [0] * (NDMA + 1)
        dma_i = 0
        for j in range(n):
            if ops[j]["dma"]:
                if ops[j]["tag"] == "allred":
                    ln = NDMA
                else:
                    ln = dma_i % NDMA
                    dma_i += 1
                lane[j] = ln
                lane_cnt[ln] += 1
                rank[j] = lane_cnt[ln]
            elif published[j]:
                e = ops[j]["engine"]
                cnt[e] += 1
                rank[j] = cnt[e]
        self.lane_totals = list(lane_cnt)

        dkeys = [("dma", l) for l in range(NDMA + 1)]

        def new_vc():
            d = {e: 0 for e in ENGINES}
            for k in dkeys:
                d[k] = 0
            return d

        vc = {e: new_vc() for e in ENGINES}
        snap = {e: {} for e in ENGINES}
        for k in dkeys:
            snap[k] = {}
        waits = [None] * n
        cur_b = 0
        pubs_at = {e: 0 for e in ENGINES}
        lane_seen = [0] * (NDMA + 1)
        for j in range(n):
            while cur_b < len(self.barriers) and self.barriers[cur_b] <= j:
                for e in ENGINES:
                    for f in ENGINES:
                        if vc[e][f] < pubs_at[f]:
                            vc[e][f] = pubs_at[f]
                cur_b += 1
            op = ops[j]
            e = op["engine"]
            need = {}
            if op["tag"] == "dmafence":
                for l in range(NDMA + 1):
                    v = lane_seen[l] * (1 if l == NDMA else 16)
                    if v:
                        need[("dma", l)] = v
            for i in deps[j]:
                if not needs_edge(i, j):
                    continue
                if ops[i]["dma"]:
                    mult = 1 if lane[i] == NDMA else 16
                    key, val = ("dma", lane[i]), rank[i] * mult
                else:
                    key, val = ops[i]["engine"], rank[i]
                if val > need.get(key, 0):
                    need[key] = val
            if op["dma"] and rank[j] > 1:
                key = ("dma", lane[j])
                v = (rank[j] - 1) * (1 if lane[j] == NDMA else 16)
                if v > need.get(key, 0):
                    need[key] = v
            wl = []
            for key, val in sorted(need.items(), key=str):
                if vc[e][key] < val:
                    wl.append((key, val))
                    sn = snap[key].get(val)
                    if sn is not None:
                        for k2, v2 in sn.items():
                            if vc[e][k2] < v2:
                                vc[e][k2] = v2
                    if vc[e][key] < val:
                        vc[e][key] = val
            waits[j] = wl
            if op["dma"]:
                lane_seen[lane[j]] = rank[j]
            if published[j]:
                if op["dma"]:
                    sn = dict(vc[e])
                    k = ("dma", lane[j])
                    mult = 1 if lane[j] == NDMA else 16
                    sn[k] = rank[j] * mult
                    snap[k][rank[j] * mult] = sn
                else:
                    sn = dict(vc[e])
                    sn[e] = rank[j]
                    snap[e][rank[j]] = sn
                    pubs_at[e] = rank[j]

        semmap = dict(sems)
        for l in range(NDMA + 1):
            semmap[("dma", l)] = dma_sem[l]
        seg_bounds = [0] + list(self.barriers) + [n]
        for s in range(len(seg_bounds) - 1):
            lo, hi = seg_bounds[s], seg_bounds[s + 1]
            if hi <= lo:
                continue
            seg = [ops[j] for j in range(lo, hi)]
            with nc.Block() as blk:
                for e in ENGINES:
                    eops = [o for o in seg if o["engine"] == e]
                    if not eops:
                        continue

                    def body(eng, eops=eops, e=e):
                        for o in eops:
                            j = o["idx"]
                            for key, val in waits[j]:
                                eng.wait_ge(semmap[key], val)
                            ins = o["fn"](eng)
                            if published[j]:
                                if o["dma"]:
                                    ins.then_inc(dma_sem[lane[j]],
                                                 1 if lane[j] == NDMA else 16)
                                else:
                                    ins.then_inc(semmap[e], 1)

                    getattr(blk, e)(body)


def _chw(w):
    kin, nout = w.shape
    return np.ascontiguousarray(
        w.reshape(kin // 128, 128, nout).transpose(1, 0, 2).reshape(128, -1))


def build_consts():
    c = {}
    c["ident"] = np.eye(128, dtype=BF)
    ep = np.zeros((128, 128), np.float32)
    for a in range(2):
        for b in range(2):
            ep[a * 64:(a + 1) * 64, b * 64:(b + 1) * 64] = np.eye(64)
    c["eyepair"] = ep.astype(BF)
    c["onesrB"] = np.ones((1, 512), dtype=BF)
    c["onescB"] = np.ones((128, 1), dtype=BF)
    c["onesrF"] = np.ones((1, 128), dtype=np.float32)
    c["negrF"] = -np.ones((1, 128), dtype=np.float32)
    c["epsc"] = np.tile(np.array([[EPS_LN, EPS_RMS]], np.float32), (128, 1))
    lo = np.zeros((1, 128), np.float32); lo[0, :64] = 1.0
    hi = np.zeros((1, 128), np.float32); hi[0, 64:] = 1.0
    c["lo64"], c["hi64"] = lo, hi
    sel2 = np.zeros((2, 128), np.float32)
    sel2[0, :64] = 1.0
    sel2[1, 64:] = 1.0
    c["sel2"] = sel2
    q = np.arange(128)
    c["cmask"] = (q[None, :] >= q[:, None]).astype(BF)
    c["wmask"] = (q[None, :] < q[:, None]).astype(BF)
    c["zmask"] = np.zeros((128, 128), dtype=BF)
    c["onemask"] = np.ones((128, 128), dtype=BF)
    return c


def build_core_inputs(xjobs, half, TC, consts):
    d = {}
    t0 = half * TC
    for j, xb in enumerate(xjobs):
        d[f"x{j}"] = np.ascontiguousarray(xb[t0:t0 + TC].T).astype(np.float32)
        if half >= 1:
            d[f"xhalo{j}"] = np.ascontiguousarray(
                xb[t0 - HT:t0].T).astype(np.float32)
        else:
            d[f"xhalo{j}"] = np.zeros((D, HT), np.float32)
    nkv = TC // 128 + 2
    inv = (1.0 / 10000.0 ** (np.arange(0, HD, 2) / HD)).astype(np.float32)
    pos = np.arange(t0 - HT, t0 + TC, dtype=np.float32)
    pos = np.maximum(pos, 0.0)
    fr = np.outer(pos, inv)
    d["cosq"] = np.ascontiguousarray(np.cos(fr).astype(BF).reshape(
        nkv, 128, 32).transpose(1, 0, 2).reshape(128, -1))
    d["sinq"] = np.ascontiguousarray(np.sin(fr).astype(BF).reshape(
        nkv, 128, 32).transpose(1, 0, 2).reshape(128, -1))
    if half == 0:
        d["mA"], d["mB"], d["mC"] = (consts["zmask"],) * 3
    else:
        d["mA"], d["mB"], d["mC"] = (consts["wmask"], consts["onemask"],
                                     consts["wmask"])
    return d


def stage_weights(ins):
    g = lambda k: np.ascontiguousarray(np.asarray(ins[k], np.float32))
    d = {}
    d["wcn1"] = _chw(g("L0_ffn_cn_w1"))
    d["wcn2"] = _chw(g("L0_ffn_cn_w2"))
    d["wcn3"] = _chw(g("L0_ffn_cn_w3"))
    d["wcn4"] = _chw(g("L0_ffn_cn_w4"))
    d["wgc1"] = _chw(g("L0_ffn_gc_w1"))
    d["wgc2"] = _chw(g("L0_ffn_gc_w2"))
    d["wgc3"] = _chw(g("L0_ffn_gc_w3"))
    d["wq"] = _chw(np.ascontiguousarray(g("L0_attn_Wq").T))
    wkT = _chw(np.ascontiguousarray(g("L0_attn_Wk").T))
    wvT = _chw(np.ascontiguousarray(g("L0_attn_Wv").T))
    d["wkv"] = np.ascontiguousarray(np.concatenate(
        [wkT.reshape(128, 8, 256), wvT.reshape(128, 8, 256)],
        axis=2).reshape(128, 4096))
    d["wo"] = _chw(np.ascontiguousarray(g("L0_attn_Wo").T))
    d["bcn1"] = np.ascontiguousarray(g("L0_ffn_cn_b1").reshape(8, 128).T)
    d["bcn2"] = np.ascontiguousarray(g("L0_ffn_cn_b2").reshape(4, 128).T)
    d["bcn3"] = np.ascontiguousarray(g("L0_ffn_cn_b3").reshape(2, 128).T)
    d["bgc1"] = np.ascontiguousarray(g("L0_ffn_gc_b1").reshape(4, 128).T)
    d["bgc2"] = np.ascontiguousarray(g("L0_ffn_gc_b2").reshape(2, 128).T)
    d["b4s"] = np.array([[float(g("L0_ffn_cn_b4").reshape(-1)[0]),
                          float(g("L0_ffn_gc_b3").reshape(-1)[0]),
                          float(g("L0_ffn_adaptive_range").reshape(-1)[0]),
                          0.0]], np.float32)
    d["qgain"] = g("L0_attn_q_gain").reshape(1, 16)
    pm = g("L0_attn_pair_mix")
    pm2 = np.zeros((2, 16), np.float32)
    for c in range(8):
        pm2[0, 2 * c] = pm[c, 0, 0]
        pm2[1, 2 * c] = pm[c, 1, 0]
        pm2[0, 2 * c + 1] = pm[c, 0, 1]
        pm2[1, 2 * c + 1] = pm[c, 1, 1]
    d["pm2"] = pm2
    return d


class Kern:
    def __init__(self, TC, n_pairs, njobs=2):
        self.TC = TC
        self.n_pairs = n_pairs
        self.njobs = njobs
        self.NFT = TC // FT
        self.NQT = TC // 128
        self.nc = bass.Bass("TRN2", target_bir_lowering=False)
        self.E = Emitter()
        self.es = ExitStack()
        self.dram_in = {}

    def din(self, name, shape, dt=F32):
        t = self.nc.dram_tensor(name, list(shape), dt, kind="ExternalInput")
        self.dram_in[name] = t
        return t

    def sb(self, name, shape, dt, scope=None):
        return (scope or self.es).enter_context(
            self.nc.sbuf_tensor(name, list(shape), dt))

    def dma(self, dst, src, reads, writes, tag=""):
        self.E.op("sync", lambda e, d=dst, s=src: e.dma_start(d, s),
                  reads=reads, writes=writes, dma=True, tag=tag)

    def mm(self, out, lhsT, rhs, start, stop, reads, writes, tag="", tp=None):
        self.E.op("tensor",
                  lambda e, o=out, l=lhsT, r=rhs, st=start, sp=stop, t=tp:
                  e.matmul(o, l, r, start=st, stop=sp, tile_position=t),
                  reads=reads, writes=writes, tag=tag)

    def act(self, out, in_, func, reads, writes, bias=0.0, scale=1.0, tag=""):
        self.E.op("scalar",
                  lambda e, o=out, i=in_, f=func, b=bias, s=scale:
                  e.activation(o, i, f, bias=b, scale=s),
                  reads=reads, writes=writes, tag=tag)

    def ew(self, engine, meth, *args, reads=(), writes=(), tag="", **kw):
        def fn(e, m=meth, a=args, k=kw):
            return getattr(e, m)(*a, **k)
        self.E.op(engine, fn, reads=reads, writes=writes, tag=tag)

    def load_weight(self, name, width, scope, tag="w"):
        dst = self.sb(name + "_b" + tag, [128, width], BF16, scope)
        if name in self.wcached:
            off = self.wcached[name]
            self.dma(dst[:, :], self.wdram[:, off:off + width],
                     reads=[("wd", name), ("wspill",)],
                     writes=[("fw", name)], tag="wfast")
            return dst
        src = self.dram_in[name]
        for i in range(0, width, 1024):
            hi = min(width, i + 1024)
            stg = self.wstage
            kst = ("wstg",)
            self.dma(stg[:, 0:hi - i], src[:, i:hi],
                     reads=[("wspill",)], writes=[kst], tag="wload")
            self.ew("vector", "tensor_copy", dst[:, i:hi],
                    stg[:, 0:hi - i],
                    reads=[kst], writes=[("fw", name)],
                    tag="wcast")
        return dst

    def emit_allreduce(self, msum, ccin, ccout):
        self.dma(ccin[:, :], msum[:, :], reads=[("msum",)], writes=[("ccin",)])
        groups = [[2 * i, 2 * i + 1] for i in range(self.n_pairs)]
        self.E.op("gpsimd",
                  lambda e: e.collective_compute("AllReduce", ALU.add, groups,
                                                 [ccin.ap().opt()],
                                                 [ccout.ap().opt()]),
                  reads=[("ccin",)], writes=[("ccout",)], dma=True,
                  tag="allred")
        self.dma(msum[:, :], ccout[:, :], reads=[("ccout",)],
                 writes=[("msum",)])

    def spill_weight(self, name, width, tensor):
        off = self.wd_off
        self.wd_off += width
        self.wcached[name] = off
        self.dma(self.wdram[:, off:off + width], tensor[:, :],
                 reads=[("fw", name)], writes=[("wd", name), ("wspill",)],
                 tag="wspill")

    def build(self):
        nc, E, TC = self.nc, self.E, self.TC
        NFT, NQT = self.NFT, self.NQT
        nkv = NQT + 2

        x_in = [self.din(f"x{j}", [D, TC]) for j in range(self.njobs)]
        xh_in = [self.din(f"xhalo{j}", [D, HT]) for j in range(self.njobs)]
        for k, shp in dict(wcn1=(128, 8192), wcn2=(128, 4096), wcn3=(128, 1024),
                           wcn4=(128, 2), wgc1=(128, 4096), wgc2=(128, 1024),
                           wgc3=(128, 2), wq=(128, 8192), wkv=(128, 4096),
                           wo=(128, 8192)).items():
            self.din(k, shp)
        cshapes = dict(ident=((128, 128), BF16), eyepair=((128, 128), BF16),
                       onesrB=((1, 512), BF16),
                       onescB=((128, 1), BF16), onesrF=((1, 128), F32),
                       lo64=((1, 128), F32), hi64=((1, 128), F32),
                       negrF=((1, 128), F32),
                       sel2=((2, 128), F32), epsc=((128, 2), F32),
                       cmask=((128, 128), BF16),
                       wmask=((128, 128), BF16), mA=((128, 128), BF16),
                       mB=((128, 128), BF16), mC=((128, 128), BF16),
                       cosq=((128, nkv * 32), BF16), sinq=((128, nkv * 32), BF16),
                       bcn1=((128, 8), F32), bcn2=((128, 4), F32),
                       bcn3=((128, 2), F32), bgc1=((128, 4), F32),
                       bgc2=((128, 2), F32), b4s=((1, 4), F32),
                       qgain=((1, 16), F32), pm2=((2, 16), F32))
        for k, (shp, dt) in cshapes.items():
            self.din(k, shp, dt)
        y_out = [nc.dram_tensor(f"y{j}", [D, TC], F32, kind="ExternalOutput")
                 for j in range(self.njobs)]
        statsd = nc.dram_tensor("statsd", [4 * (NFT + 1), 2 * FT], F32,
                                kind="Internal")
        ccin = nc.dram_tensor("ccin", [128, 8], F32, kind="Internal")
        ccout = nc.dram_tensor("ccout", [128, 8], F32, kind="Internal")

        x_sb = self.sb("x_sb", [128, 8 * TC], F32)
        xh_sb = self.sb("xh_sb", [128, 8 * HT], F32)
        cons = {}
        for k, (shp, dt) in cshapes.items():
            cons[k] = self.sb("c_" + k, shp, dt)
        rowc = self.sb("rowc", [1, 32], F32)
        msum = self.sb("msum", [128, 8], F32)
        gmb = self.sb("gmb", [128, 8], BF16)
        gfh1 = self.sb("gfh1", [128, 4], BF16)
        gfh2 = self.sb("gfh2", [128, 2], BF16)
        grow = self.sb("grow", [128, 16], F32)
        mneg = self.sb("mneg", [1, 16], F32)
        self.wstage = self.sb("wstage", [128, 1024], F32)
        self.wstage2 = self.wstage
        self.wdram = nc.dram_tensor("wdram", [128, 40960], BF16,
                                    kind="Internal")
        self.wcached = {}
        self.wd_off = 0

        pb = [self.es.enter_context(
            nc.psum_tensor(f"pb{i}", [128, 512], F32)) for i in range(8)]
        self.pb = pb

        sems = {e: self.es.enter_context(nc.semaphore("sem_" + e))
                for e in ENGINES}
        dma_sem = [self.es.enter_context(nc.semaphore(f"sem_dma{l}"))
                   for l in range(NDMA + 1)]

        for k in cons:
            self.dma(cons[k][:, :], self.dram_in[k][:, :], reads=(),
                     writes=[("c", k)], tag="cload")
        self.ew("vector", "tensor_scalar", rowc[:, 8:9], cons["b4s"][:, 0:1],
                0.5, None, ALU.mult, reads=[("c", "b4s")], writes=[("c", "rowc")])
        self.ew("vector", "tensor_scalar", rowc[:, 9:10], cons["b4s"][:, 1:2],
                0.5, None, ALU.mult, reads=[("c", "b4s")], writes=[("c", "rowc")])
        self.ew("vector", "tensor_scalar", mneg[:, :], cons["qgain"][:, :],
                -1.0, None, ALU.mult, reads=[("c", "qgain")],
                writes=[("c", "mneg")])
        self.ew("vector", "tensor_tensor", mneg[:, :], mneg[:, :],
                cons["qgain"][:, :], ALU.max,
                reads=[("c", "mneg"), ("c", "qgain")], writes=[("c", "mneg")])
        self.ew("vector", "tensor_scalar", mneg[:, :], mneg[:, :], -8.08,
                -1e-3, ALU.mult, ALU.add, reads=[("c", "mneg")],
                writes=[("c", "mneg")])
        self.ew("vector", "tensor_scalar", rowc[:, 16:32], cons["qgain"][:, :],
                0.125, None, ALU.mult, reads=[("c", "qgain")],
                writes=[("c", "rowc")])
        self.mm(pb[0][:, 0:16], cons["onesrF"][:, :], rowc[:, 16:32], True,
                True, reads=[("c", "onesrF"), ("c", "rowc")],
                writes=[("pb", 0)], tag="growb")
        self.ew("vector", "tensor_copy", grow[:, :], pb[0][:, 0:16],
                reads=[("pb", 0)], writes=[("c", "grow")])
        E.barrier()

        for job in range(self.njobs):
            xv_all = x_in[job][:, :].rearrange("(c p) t -> p c t", p=128)
            for ft in range(NFT):
                dst = x_sb[:, :].rearrange("p (c t) -> p c t", t=TC)[
                    :, :, ft * FT:(ft + 1) * FT]
                self.dma(dst, xv_all[:, :, ft * FT:(ft + 1) * FT], reads=(),
                         writes=[("x", ft * (FT // 128) + q)
                                 for q in range(FT // 128)], tag="xload")
            self.dma(xh_sb[:, :].rearrange("p (c t) -> p c t", t=HT),
                     xh_in[job][:, :].rearrange("(c p) t -> p c t", p=128),
                     reads=(), writes=[("xh", 0), ("xh", 1)])

            with ExitStack() as sc:
                wspec = dict(wcn1=8192, wcn2=4096, wcn3=1024, wcn4=2,
                             wgc1=4096, wgc2=1024, wgc3=2)
                fw = {k: self.load_weight(k, w, sc, tag=f"1{job}")
                      for k, w in wspec.items()}
                if job == 0:
                    for k, w in wspec.items():
                        self.spill_weight(k, w, fw[k])
                    self.E.op("sync", lambda e: e.nop(),
                              reads=[("wspill",)], writes=(), tag="wjoin")
                self.ffn_phase(fw, cons, rowc, msum, gmb, gfh1, gfh2, grow,
                               x_sb, xh_sb, statsd, ccin, ccout, sc,
                               call=2 * job + 0, halo=True, store=None,
                               y_out=None)
            E.op("sync", lambda e: e.nop(), reads=(), writes=(),
                 tag="dmafence")
            E.barrier()

            with ExitStack() as sc:
                self.attn_phase(cons, grow, mneg, x_sb, xh_sb, sc, job,
                                ffn2=(2 * job + 1, statsd, ccin, ccout, msum)
                                if A1_IN_ATTN else None)
            E.op("sync", lambda e: e.nop(), reads=(), writes=(),
                 tag="dmafence")
            E.barrier()
            if job == 0:
                self.E.op("sync", lambda e: e.nop(), reads=[("wspill",)],
                          writes=(), tag="wjoin")
                E.barrier()

            with ExitStack() as sc:
                fw = {k: self.load_weight(k, w, sc, tag=f"2{job}") for k, w in
                      dict(wcn1=8192, wcn2=4096, wcn3=1024, wcn4=2, wgc1=4096,
                           wgc2=1024, wgc3=2).items()}
                self.ffn_phase(fw, cons, rowc, msum, gmb, gfh1, gfh2, grow,
                               x_sb, xh_sb, statsd, ccin, ccout, sc,
                               call=2 * job + 1, halo=False, store=True,
                               y_out=y_out[job], skip_a1=A1_IN_ATTN)
            E.op("sync", lambda e: e.nop(), reads=(), writes=(),
                 tag="dmafence")
            E.barrier()

        def finwait(e):
            ins = None
            for l in range(NDMA + 1):
                if E.lane_totals[l]:
                    m = 1 if l == NDMA else 16
                    ins = e.wait_ge(dma_sem[l], m * E.lane_totals[l])
            return ins
        E.op("sync", finwait, reads=(), writes=(), tag="final")

        E.finalize(nc, sems, dma_sem)
        nc.finalize()
        return self

    def ffn_phase(self, fw, cons, rowc, msum, gmb, gfh1, gfh2, grow,
                  x_sb, xh_sb, statsd, ccin, ccout, sc,
                  call, halo, store, y_out, skip_a1=False):
        nc, E, TC, NFT = self.nc, self.E, self.TC, self.NFT
        pb = self.pb
        W = FT
        sbs = lambda nm, shape, dt: self.sb(f"{nm}_{call}", shape, dt, sc)
        NP = 3
        out0 = [sbs(f"out0{p}", [128, 8 * W], F32) for p in range(NP)]
        ob = [sbs(f"ob{p}", [128, 8 * W], BF16) for p in range(NP)]
        h1b = [sbs(f"h1b{p}", [128, 8 * W], BF16) for p in range(NP)]
        h2b = [sbs(f"h2b{p}", [128, 4 * W], BF16) for p in range(2)]
        h3b = [sbs(f"h3b{p}", [128, 2 * W], BF16) for p in range(2)]
        utmp = [sbs(f"utmp{p}", [128, 4 * W], F32) for p in range(2)]
        xb2 = [sbs(f"xb{p}", [128, 8 * W], BF16) for p in range(2)]
        strow2 = [sbs(f"strow{p}", [1, 2 * W], F32) for p in range(2)]
        trow2 = [sbs(f"trow{p}", [1, W], F32) for p in range(2)]
        crow2 = [sbs(f"crow{p}", [1, 2 * W], F32) for p in range(2)]
        red2 = [sbs(f"red{p}", [128, 8], F32) for p in range(2)]
        xsqb = [sbs("xsqb", [128, 8 * W], BF16)] * 2

        ntile = NFT + (1 if halo else 0)

        def tile_info(ft):
            is_halo = ft == NFT
            src_ = xh_sb if is_halo else x_sb
            TT = HT if is_halo else TC
            toff = 0 if is_halo else ft * W
            pfx = "xh" if is_halo else "x"
            xkeys = [(pfx, (toff // 128) + q) for q in range(W // 128)]
            xv = src_[:, :].rearrange("p (c t) -> p c t", t=TT)[
                :, :, toff:toff + W]
            return is_halo, xv, xkeys

        if not skip_a1:
            self.ew("vector", "memset", msum[:, :], 0.0, reads=(),
                    writes=[("msum",)])
        for ft in range(0 if skip_a1 else ntile):
            is_halo, xv, xkeys = tile_info(ft)
            p = ft % 2
            xb, strow, red = xb2[p], strow2[p], red2[p]
            kxb, kst = ("xb", call, p), ("strow", call, p)
            srow_bank = 7 - 2 * p
            bc_bank = 6 - 2 * p
            xbv = xb[:, :].rearrange("p (c t) -> p c t", t=W)
            self.ew("vector", "tensor_copy", xbv, xv, reads=xkeys,
                    writes=[kxb], tag="a1cast")
            for ic in range(8):
                self.mm(pb[srow_bank][0:1, 0:W], cons["onescB"][:, :],
                        xb[:, ic * W:(ic + 1) * W], ic == 0, ic == 7,
                        reads=[("c", "onescB"), kxb],
                        writes=[("pb", srow_bank)], tag="a1s1")
            xsq = xsqb[p]
            kxs = ("xsqb", call)
            self.act(xsq[:, :], xb[:, :], AF.Square, reads=[kxb],
                     writes=[kxs], tag="a1sq")
            for ic in range(8):
                self.mm(pb[srow_bank][0:1, W:2 * W], cons["onescB"][:, :],
                        xsq[:, ic * W:(ic + 1) * W], ic == 0, ic == 7,
                        reads=[("c", "onescB"), kxs],
                        writes=[("pb", srow_bank)], tag="a1s2")
            self.ew("scalar", "mul", strow[:, 0:W],
                    pb[srow_bank][0:1, 0:W], 1.0 / D,
                    reads=[("pb", srow_bank)], writes=[kst], tag="a1m")
            self.act(strow[:, W:2 * W], strow[:, 0:W], AF.Square,
                     reads=[kst], writes=[kst], tag="a1msq")
            self.ew("vector", "scalar_tensor_tensor", strow[:, W:2 * W],
                    pb[srow_bank][0:1, W:2 * W], 1.0 / D, strow[:, W:2 * W],
                    ALU.mult, ALU.subtract,
                    reads=[("pb", srow_bank), kst], writes=[kst], tag="a1var")
            self.act(strow[:, W:2 * W], strow[:, W:2 * W], AF.Ln,
                     reads=[kst, ("c", "epsc")], writes=[kst],
                     bias=cons["epsc"][0:1, 0:1])
            self.act(strow[:, W:2 * W], strow[:, W:2 * W], AF.Exp,
                     reads=[kst], writes=[kst], scale=-0.5)
            self.dma(statsd[call * (NFT + 1) + ft:call * (NFT + 1) + ft + 1,
                            0:2 * W], strow[0:1, 0:2 * W],
                     reads=[kst], writes=[("std", call, ft)])
            if is_halo:
                continue
            self.mm(pb[bc_bank][:, 0:W], cons["onesrF"][:, :], strow[:, 0:W],
                    True, True, reads=[("c", "onesrF"), kst],
                    writes=[("pb", bc_bank)], tag="a1bm")
            self.mm(pb[bc_bank][:, W:2 * W], cons["onesrF"][:, :],
                    strow[:, W:2 * W], True, True,
                    reads=[("c", "onesrF"), kst],
                    writes=[("pb", bc_bank)], tag="a1br")
            self.ew("vector", "tensor_tensor", xbv, xv,
                    pb[bc_bank][:, None, 0:W].broadcast_to([128, 8, W]),
                    ALU.subtract, reads=xkeys + [("pb", bc_bank)],
                    writes=[kxb], tag="a1sub")
            self.ew("vector", "tensor_tensor", xbv, xbv,
                    pb[bc_bank][:, None, W:2 * W].broadcast_to([128, 8, W]),
                    ALU.mult, reads=[kxb, ("pb", bc_bank)], writes=[kxb],
                    tag="a1mul")
            self.ew("vector", "reduce_sum", red[:, 0:8], xbv, axis=AX.X,
                    reads=[kxb], writes=[("red", call, p)], tag="a1red")
            self.ew("vector", "tensor_tensor", msum[:, :], msum[:, :],
                    red[:, 0:8], ALU.add,
                    reads=[("msum",), ("red", call, p)], writes=[("msum",)])

        if not skip_a1:
            self.emit_allreduce(msum, ccin, ccout)

        def emit_a2():
         self.ew("vector", "tensor_scalar", gmb[:, :], msum[:, :],
                 1.0 / (2 * TC), None, ALU.mult, reads=[("msum",)],
                 writes=[("gmb",)], tag="gmb")
         for oc in range(4):
             for ic in range(8):
                 self.mm(pb[6][:, oc:oc + 1],
                         fw["wgc1"][:, ic * 512 + oc * 128:
                                    ic * 512 + oc * 128 + 128],
                         gmb[:, ic:ic + 1], ic == 0, ic == 7,
                         reads=[("fw", "wgc1"), ("gmb",)], writes=[("pb", 6)],
                         tag="gf1")
         for oc in range(4):
             self.act(gfh1[:, oc:oc + 1], pb[6][:, oc:oc + 1], AF.Gelu,
                      reads=[("pb", 6), ("c", "bgc1")], writes=[("gfh1",)],
                      bias=cons["bgc1"][:, oc:oc + 1])
         for oc in range(2):
             for ic in range(4):
                 self.mm(pb[6][:, 8 + oc:9 + oc],
                         fw["wgc2"][:, ic * 256 + oc * 128:
                                    ic * 256 + oc * 128 + 128],
                         gfh1[:, ic:ic + 1], ic == 0, ic == 3,
                         reads=[("fw", "wgc2"), ("gfh1",)], writes=[("pb", 6)],
                         tag="gf2")
         for oc in range(2):
             self.act(gfh2[:, oc:oc + 1], pb[6][:, 8 + oc:9 + oc], AF.Gelu,
                      reads=[("pb", 6), ("c", "bgc2")], writes=[("gfh2",)],
                      bias=cons["bgc2"][:, oc:oc + 1])
         for ic in range(2):
             self.mm(pb[6][0:1, 12:13], fw["wgc3"][:, ic:ic + 1],
                     gfh2[:, ic:ic + 1], ic == 0, ic == 1,
                     reads=[("fw", "wgc3"), ("gfh2",)], writes=[("pb", 6)],
                     tag="gf3")
         self.act(rowc[:, 10:11], pb[6][0:1, 12:13], AF.Tanh,
                  reads=[("pb", 6), ("c", "rowc")], writes=[("c", "rowc")],
                  bias=rowc[:, 9:10], scale=0.5, tag="gftanh")
         self.ew("vector", "tensor_scalar", rowc[:, 0:1], cons["b4s"][:, 2:3],
                 0.3, None, ALU.mult, reads=[("c", "b4s")],
                 writes=[("c", "rowc")])
         self.ew("vector", "tensor_tensor", rowc[:, 11:12], rowc[:, 10:11],
                 cons["b4s"][:, 2:3], ALU.mult,
                 reads=[("c", "rowc"), ("c", "b4s")], writes=[("c", "rowc")])
         self.ew("vector", "tensor_scalar", rowc[:, 1:2], rowc[:, 11:12], 0.7,
                 0.1, ALU.mult, ALU.add, reads=[("c", "rowc")],
                 writes=[("c", "rowc")])
         self.ew("vector", "tensor_scalar", rowc[:, 2:4], rowc[:, 0:2], 0.5,
                 None, ALU.mult, reads=[("c", "rowc")], writes=[("c", "rowc")])
         self.ew("vector", "tensor_scalar", rowc[:, 4:8], rowc[:, 0:4], -1.0,
                 None, ALU.mult, reads=[("c", "rowc")], writes=[("c", "rowc")])

        def emit_xhat(ft):
            is_halo, xv, xkeys = tile_info(ft)
            p = ft % 3
            strow = strow2[ft % 2]
            kst = ("strow", call, ft % 2)
            o0 = out0[p]
            o0v = o0[:, :].rearrange("p (c t) -> p c t", t=W)
            o0keys = [("o0", call, p, q) for q in range(4)]
            bcb = 6
            self.dma(strow[0:1, 0:2 * W],
                     statsd[call * (NFT + 1) + ft:call * (NFT + 1) + ft + 1,
                            0:2 * W],
                     reads=[("std", call, ft)], writes=[kst], tag="streload")
            self.mm(pb[bcb][:, 0:W], cons["onesrF"][:, :], strow[:, 0:W],
                    True, True, reads=[("c", "onesrF"), kst],
                    writes=[("pb", bcb)], tag="a3bm")
            self.mm(pb[bcb][:, W:2 * W], cons["onesrF"][:, :],
                    strow[:, W:2 * W], True, True,
                    reads=[("c", "onesrF"), kst],
                    writes=[("pb", bcb)], tag="a3br")
            self.ew("vector", "tensor_tensor", o0v, xv,
                    pb[bcb][:, None, 0:W].broadcast_to([128, 8, W]),
                    ALU.subtract, reads=xkeys + [("pb", bcb)],
                    writes=o0keys, tag="a3sub")
            self.ew("vector", "tensor_tensor", o0v, o0v,
                    pb[bcb][:, None, W:2 * W].broadcast_to([128, 8, W]),
                    ALU.mult, reads=o0keys + [("pb", bcb)],
                    writes=o0keys, tag="a3mul")
            self.ew("vector", "tensor_copy", ob[p][:, :], o0[:, :],
                    reads=o0keys, writes=[("ob", call, p)], tag="obcast")

        def emit_cycle(ft, cyc):
            p = ft % 3
            p2 = ft % 2
            ba, bb_ = 2 * p, 2 * p + 1
            o0 = out0[p]
            obp, h1p, h2p, h3p = ob[p], h1b[p], h2b[p2], h3b[p2]
            utp, trow, crow = utmp[p2], trow2[p2], crow2[p2]
            ko = ("ob", call, p)
            kh1, kh2, kh3 = ("h1b", call, p), ("h2b", call, p2), ("h3b", call, p2)
            ku, ktr, kcr = ("ut", call, p2), ("trow", call, p2), ("crow", call, p2)
            o0keys = [("o0", call, p, q) for q in range(4)]
            for oc in range(8):
                bank = ba if oc % 2 == 0 else bb_
                for ic in range(8):
                    self.mm(pb[bank][:, 0:W],
                            fw["wcn1"][:, ic * 1024 + oc * 128:
                                       ic * 1024 + oc * 128 + 128],
                            obp[:, ic * W:(ic + 1) * W], ic == 0, ic == 7,
                            reads=[("fw", "wcn1"), ko],
                            writes=[("pb", bank)], tag="h1mm")
                self.act(h1p[:, oc * W:(oc + 1) * W], pb[bank][:, 0:W],
                         AF.Gelu, reads=[("pb", bank), ("c", "bcn1")],
                         writes=[kh1], bias=cons["bcn1"][:, oc:oc + 1],
                         tag="h1act")
            for oc in range(4):
                bank = ba if oc % 2 == 0 else bb_
                for ic in range(8):
                    self.mm(pb[bank][:, 0:W],
                            fw["wcn2"][:, ic * 512 + oc * 128:
                                       ic * 512 + oc * 128 + 128],
                            h1p[:, ic * W:(ic + 1) * W], ic == 0, ic == 7,
                            reads=[("fw", "wcn2"), kh1],
                            writes=[("pb", bank)], tag="h2mm")
                self.act(h2p[:, oc * W:(oc + 1) * W], pb[bank][:, 0:W],
                         AF.Gelu, reads=[("pb", bank), ("c", "bcn2")],
                         writes=[kh2], bias=cons["bcn2"][:, oc:oc + 1],
                         tag="h2act")
            for oc in range(2):
                bank = ba if oc % 2 == 0 else bb_
                for ic in range(4):
                    self.mm(pb[bank][:, 0:W],
                            fw["wcn3"][:, ic * 256 + oc * 128:
                                       ic * 256 + oc * 128 + 128],
                            h2p[:, ic * W:(ic + 1) * W], ic == 0, ic == 3,
                            reads=[("fw", "wcn3"), kh2],
                            writes=[("pb", bank)], tag="h3mm")
                self.act(h3p[:, oc * W:(oc + 1) * W], pb[bank][:, 0:W],
                         AF.Gelu, reads=[("pb", bank), ("c", "bcn3")],
                         writes=[kh3], bias=cons["bcn3"][:, oc:oc + 1],
                         tag="h3act")
            for ic in range(2):
                self.mm(pb[ba][0:1, 0:W], fw["wcn4"][:, ic:ic + 1],
                        h3p[:, ic * W:(ic + 1) * W], ic == 0, ic == 1,
                        reads=[("fw", "wcn4"), kh3],
                        writes=[("pb", ba)], tag="zmm")
            self.act(trow[:, 0:W], pb[ba][0:1, 0:W], AF.Tanh,
                     reads=[("pb", ba), ("c", "rowc")], writes=[ktr],
                     bias=rowc[:, 8:9], scale=0.5, tag="tanh")

        def emit_tail(ft, cyc):
            p = ft % 3
            p2 = ft % 2
            ba, bb_ = 2 * p, 2 * p + 1
            o0 = out0[p]
            obp = ob[p]
            utp, trow, crow = utmp[p2], trow2[p2], crow2[p2]
            ko = ("ob", call, p)
            ku, ktr, kcr = ("ut", call, p2), ("trow", call, p2), ("crow", call, p2)
            o0keys = [("o0", call, p, q) for q in range(4)]
            aoff = 2 if cyc == 2 else 0
            self.ew("gpsimd", "tensor_scalar", crow[:, 0:W], trow[:, 0:W],
                    rowc[:, aoff:aoff + 1], rowc[:, 1 + aoff:2 + aoff],
                    ALU.mult, ALU.add, reads=[ktr, ("c", "rowc")],
                    writes=[kcr], tag="crowp")
            self.ew("gpsimd", "tensor_scalar", crow[:, W:2 * W], trow[:, 0:W],
                    rowc[:, 4 + aoff:5 + aoff], rowc[:, 5 + aoff:6 + aoff],
                    ALU.mult, ALU.add, reads=[ktr, ("c", "rowc")],
                    writes=[kcr], tag="crown")
            self.mm(pb[bb_][:, 0:2 * W], cons["onesrF"][:, :],
                    crow[:, 0:2 * W], True, True,
                    reads=[("c", "onesrF"), kcr], writes=[("pb", bb_)],
                    tag="cbmm")
            for (qa, qb_, sgn_a, sgn_b) in ((0, 2, 0, W), (1, 3, W, 0)):
                offa, offb = 2 * qa * W, 2 * qb_ * W
                va = o0[:, offa:offa + 2 * W].rearrange(
                    "p (c t) -> p c t", t=W)
                vb = o0[:, offb:offb + 2 * W].rearrange(
                    "p (c t) -> p c t", t=W)
                ta = utp[:, 0:2 * W].rearrange("p (c t) -> p c t", t=W)
                tb = utp[:, 2 * W:4 * W].rearrange("p (c t) -> p c t", t=W)
                ka, kb_ = ("o0", call, p, qa), ("o0", call, p, qb_)
                self.ew("vector", "tensor_tensor", ta, vb,
                        pb[bb_][:, None, sgn_a:sgn_a + W].broadcast_to(
                            [128, 2, W]), ALU.mult,
                        reads=[kb_, ("pb", bb_)], writes=[ku], tag="umulA")
                self.ew("vector", "tensor_tensor", tb, va,
                        pb[bb_][:, None, sgn_b:sgn_b + W].broadcast_to(
                            [128, 2, W]), ALU.mult,
                        reads=[ka, ("pb", bb_)], writes=[ku], tag="umulB")
                if cyc < 2:
                    self.ew("vector", "tensor_tensor", va, va, ta, ALU.add,
                            reads=[ka, ku], writes=[ka], tag="uaddA")
                    self.ew("vector", "tensor_tensor", vb, vb, tb, ALU.add,
                            reads=[kb_, ku], writes=[kb_], tag="uaddB")
                else:
                    self.ew("vector", "scalar_tensor_tensor", va, va, 0.5,
                            ta, ALU.mult, ALU.add, reads=[ka, ku],
                            writes=[ka], tag="uhalfA")
                    self.ew("vector", "scalar_tensor_tensor", vb, vb, 0.5,
                            tb, ALU.mult, ALU.add, reads=[kb_, ku],
                            writes=[kb_], tag="uhalfB")
            if cyc < 2:
                self.ew("vector", "tensor_copy", obp[:, :], o0[:, :],
                        reads=o0keys, writes=[ko], tag="obcast")

        def emit_resid(ft):
            is_halo, xv, xkeys = tile_info(ft)
            p = ft % 3
            o0v = out0[p][:, :].rearrange("p (c t) -> p c t", t=W)
            o0keys = [("o0", call, p, q) for q in range(4)]
            self.ew("vector", "tensor_tensor", xv, xv, o0v, ALU.add,
                    reads=xkeys + o0keys, writes=xkeys, tag="resid")
            if store is not None and not is_halo:
                dst = y_out[:, :].rearrange("(c p) t -> p c t", p=128)[
                    :, :, ft * W:(ft + 1) * W]
                self.dma(dst, xv, reads=xkeys, writes=[("yout", call, ft)],
                         tag="ystore")

        for t in range(min(3, ntile)):
            emit_xhat(t)
        first = [True]

        def cyc_full(t, cyc):
            emit_cycle(t, cyc)
            if first[0]:
                emit_a2()
                first[0] = False
            emit_tail(t, cyc)

        for k in range(0, ntile, 3):
            tiles = [t for t in range(k, min(k + 3, ntile))]
            for cyc in range(2):
                for t in tiles:
                    cyc_full(t, cyc)
            for t in tiles:
                cyc_full(t, 2)
                emit_resid(t)
                if t + 3 < ntile:
                    emit_xhat(t + 3)

    def attn_phase(self, cons, grow, mneg, x_sb, xh_sb, sc, job=0,
                   ffn2=None):
        nc, E, TC, NQT = self.nc, self.E, self.TC, self.NQT
        pb = self.pb
        jj = str(job)
        wq = self.load_weight("wq", 8192, sc, tag=f"a{job}")
        wkv = self.load_weight("wkv", 4096, sc, tag=f"a{job}")
        womb = self.sb("womb" + jj, [128, 8192], BF16, sc)
        wochunk = self.sb("wochunk" + jj, [128, 1024], BF16, sc)
        mcb = self.sb("mcb" + jj, [128, 128], BF16, sc)
        scol = self.sb("scol" + jj, [128, 2], F32, sc)
        eyep = cons["eyepair"]
        xbq2 = [self.sb(f"xbq{p}" + jj, [128, 1024], BF16, sc)
                for p in range(2)]
        xsq2 = [self.sb(f"xsq{p}" + jj, [128, 1024], BF16, sc)
                for p in range(2)]
        colb2 = [self.sb(f"colb{p}" + jj, [128, 4], F32, sc)
                 for p in range(2)]
        wsq = self.sb("wsq" + jj, [128, 1024], F32, sc)
        wskv = self.sb("wskv" + jj, [128, 512], F32, sc)
        qb = self.sb("qb" + jj, [128, 1024], BF16, sc)
        kb = self.sb("kb" + jj, [128, 256], BF16, sc)
        q2b = self.sb("q2b" + jj, [128, 1024], BF16, sc)
        rsq = self.sb("rsq" + jj, [128, 24], F32, sc)
        ropq = self.sb("ropq" + jj, [128, 1024], BF16, sc)
        ropk = self.sb("ropk" + jj, [128, 256], BF16, sc)
        rt1 = self.sb("rt1" + jj, [128, 512], BF16, sc)
        rt2 = self.sb("rt2" + jj, [128, 512], BF16, sc)
        sct = self.sb("sct" + jj, [128, 512], BF16, sc)
        sst = self.sb("sst" + jj, [128, 512], BF16, sc)
        sck = self.sb("sck" + jj, [128, 128], BF16, sc)
        ssk = self.sb("ssk" + jj, [128, 128], BF16, sc)
        qaug = [self.sb(f"qaug{g}" + jj, [65, 1024], BF16, sc)
                for g in range(4)]
        kring = [self.sb(f"kring{r}" + jj, [65, 512], BF16, sc)
                 for r in range(4)]
        vring = [self.sb(f"vring{r}" + jj, [128, 256], BF16, sc)
                 for r in range(4)]
        attb = [self.sb(f"attb{g}" + jj, [128, 3072], BF16, sc)
                for g in range(2)]
        ybuf = self.sb("ybuf" + jj, [128, 2048], BF16, sc)
        srow = self.sb("srow" + jj, [1, 1024], F32, sc)
        rbsb = self.sb("rbsb" + jj, [128, 512], F32, sc)
        strw = self.sb("strw" + jj, [1, 256], F32, sc)
        if ffn2 is not None:
            f2call, f2statsd, f2ccin, f2ccout, f2msum = ffn2
            a1xb = self.sb("a1xb" + jj, [128, 8 * FT], BF16, sc)
            a1sq = self.sb("a1sq" + jj, [128, 8 * FT], BF16, sc)
            a1st = self.sb("a1st" + jj, [1, 2 * FT], F32, sc)
            a1rd = self.sb("a1rd" + jj, [128, 8], F32, sc)
            a1bc = self.sb("a1bc" + jj, [128, 512], F32, sc)
            self.ew("vector", "memset", f2msum[:, :], 0.0, reads=(),
                    writes=[("msum",)])

        def a1_tile(ft):
            W = FT
            NFT = self.NFT
            xkeys = [("x", 2 * ft), ("x", 2 * ft + 1)]
            xv = x_sb[:, :].rearrange("p (c t) -> p c t", t=TC)[
                :, :, ft * W:(ft + 1) * W]
            xbv = a1xb[:, :].rearrange("p (c t) -> p c t", t=W)
            kxb, kxs, kst = ("a1xb",), ("a1sq",), ("a1st",)
            self.ew("vector", "tensor_copy", xbv, xv, reads=xkeys,
                    writes=[kxb], tag="fa1cast")
            for ic in range(8):
                self.mm(pb[4][0:1, 0:W], cons["onescB"][:, :],
                        a1xb[:, ic * W:(ic + 1) * W], ic == 0, ic == 7,
                        reads=[("c", "onescB"), kxb], writes=[("pb", 4)],
                        tag="fa1s1")
            self.act(a1sq[:, :], a1xb[:, :], AF.Square, reads=[kxb],
                     writes=[kxs], tag="fa1sq")
            for ic in range(8):
                self.mm(pb[4][0:1, W:2 * W], cons["onescB"][:, :],
                        a1sq[:, ic * W:(ic + 1) * W], ic == 0, ic == 7,
                        reads=[("c", "onescB"), kxs], writes=[("pb", 4)],
                        tag="fa1s2")
            self.ew("scalar", "mul", a1st[:, 0:W], pb[4][0:1, 0:W], 1.0 / D,
                    reads=[("pb", 4)], writes=[kst], tag="fa1m")
            self.act(a1st[:, W:2 * W], a1st[:, 0:W], AF.Square,
                     reads=[kst], writes=[kst], tag="fa1msq")
            self.ew("vector", "scalar_tensor_tensor", a1st[:, W:2 * W],
                    pb[4][0:1, W:2 * W], 1.0 / D, a1st[:, W:2 * W],
                    ALU.mult, ALU.subtract, reads=[("pb", 4), kst],
                    writes=[kst], tag="fa1var")
            self.act(a1st[:, W:2 * W], a1st[:, W:2 * W], AF.Ln,
                     reads=[kst, ("c", "epsc")], writes=[kst],
                     bias=cons["epsc"][0:1, 0:1])
            self.act(a1st[:, W:2 * W], a1st[:, W:2 * W], AF.Exp,
                     reads=[kst], writes=[kst], scale=-0.5)
            self.dma(f2statsd[f2call * (NFT + 1) + ft:
                              f2call * (NFT + 1) + ft + 1, 0:2 * W],
                     a1st[0:1, 0:2 * W], reads=[kst],
                     writes=[("std", f2call, ft)], tag="fa1save")
            self.dma(a1bc[:, 0:2 * W],
                     f2statsd[f2call * (NFT + 1) + ft:
                              f2call * (NFT + 1) + ft + 1,
                              0:2 * W].partition_broadcast(128),
                     reads=[("std", f2call, ft)], writes=[("a1bc",)],
                     tag="fa1bc")
            self.ew("vector", "tensor_tensor", xbv, xv,
                    a1bc[:, None, 0:W].broadcast_to([128, 8, W]),
                    ALU.subtract, reads=xkeys + [("a1bc",)], writes=[kxb],
                    tag="fa1sub")
            self.ew("vector", "tensor_tensor", xbv, xbv,
                    a1bc[:, None, W:2 * W].broadcast_to([128, 8, W]),
                    ALU.mult, reads=[kxb, ("a1bc",)], writes=[kxb],
                    tag="fa1mul")
            self.ew("vector", "reduce_sum", a1rd[:, 0:8], xbv, axis=AX.X,
                    reads=[kxb], writes=[("a1rd",)], tag="fa1red")
            self.ew("vector", "tensor_tensor", f2msum[:, :], f2msum[:, :],
                    a1rd[:, 0:8], ALU.add, reads=[("msum",), ("a1rd",)],
                    writes=[("msum",)], tag="fa1acc")

        if "wq" not in self.wcached:
            self.spill_weight("wq", 8192, wq)
            self.spill_weight("wkv", 4096, wkv)
        for ic in range(8):
            self.mm(pb[4][0:1, 0:512], cons["onescB"][:, :],
                    wq[:, ic * 1024:ic * 1024 + 512], ic == 0, ic == 7,
                    reads=[("c", "onescB"), ("fw", "wq")], writes=[("pb", 4)],
                    tag="wsum")
            self.mm(pb[4][0:1, 512:1024] if False else pb[5][0:1, 0:512],
                    cons["onescB"][:, :],
                    wq[:, ic * 1024 + 512:(ic + 1) * 1024], ic == 0, ic == 7,
                    reads=[("c", "onescB"), ("fw", "wq")], writes=[("pb", 5)],
                    tag="wsum")
            self.mm(pb[6][0:1, 0:512], cons["onescB"][:, :],
                    wkv[:, ic * 512:(ic + 1) * 512], ic == 0, ic == 7,
                    reads=[("c", "onescB"), ("fw", "wkv")], writes=[("pb", 6)],
                    tag="wsum")
        self.ew("vector", "tensor_copy", srow[:, 0:512], pb[4][0:1, 0:512],
                reads=[("pb", 4)], writes=[("srow",)], tag="wsev")
        self.ew("vector", "tensor_copy", srow[:, 512:1024], pb[5][0:1, 0:512],
                reads=[("pb", 5)], writes=[("srow",)], tag="wsev")
        for h in range(2):
            self.mm(pb[4][:, 0:512], cons["negrF"][:, :],
                    srow[:, h * 512:(h + 1) * 512], True, True,
                    reads=[("c", "negrF"), ("srow",)], writes=[("pb", 4)],
                    tag="wsbc")
            self.ew("vector", "tensor_copy", wsq[:, h * 512:(h + 1) * 512],
                    pb[4][:, 0:512], reads=[("pb", 4)], writes=[("wsq",)],
                    tag="wsev2")
        self.ew("vector", "tensor_copy", srow[:, 0:512], pb[6][0:1, 0:512],
                reads=[("pb", 6)], writes=[("srow",)], tag="wsev")
        self.mm(pb[4][:, 0:512], cons["negrF"][:, :], srow[:, 0:512],
                True, True, reads=[("c", "negrF"), ("srow",)],
                writes=[("pb", 4)], tag="wsbc")
        self.ew("vector", "tensor_copy", wskv[:, :], pb[4][:, 0:512],
                reads=[("pb", 4)], writes=[("wskv",)], tag="wsev2")

        for gg in range(4):
            g = gg % 2
            for h8 in range(8):
                h = 8 * g + h8
                self.ew("vector", "tensor_scalar",
                        qaug[gg][64:65, h8 * 128:(h8 + 1) * 128],
                        cons["onesrF"][:, 0:128], mneg[0:1, h:h + 1], None,
                        ALU.mult, reads=[("c", "onesrF"), ("c", "mneg")],
                        writes=[("qaug", gg)], tag="qaug64")
        for r in range(4):
            self.ew("vector", "memset", kring[r][64:65, :], 1.0, reads=(),
                    writes=[("kring", r)], tag="kr64")

        def precast(i, is_halo):
            if is_halo:
                src, TT, toff = xh_sb, HT, i * 128
                xkeys = [("xh", i)]
            else:
                src, TT, toff = x_sb, TC, i * 128
                xkeys = [("x", i)]
            xv = src[:, :].rearrange("p (c t) -> p c t", t=TT)[
                :, :, toff:toff + 128]
            p = i % 2
            xbv = xbq2[p][:, :].rearrange("p (c t) -> p c t", t=128)
            self.ew("vector", "tensor_copy", xbv, xv, reads=xkeys,
                    writes=[("xbq", p)], tag="l2cast")
            self.act(xsq2[p][:, :], xbq2[p][:, :], AF.Square,
                     reads=[("xbq", p)], writes=[("xsq", p)], tag="l2sq")

        def x1(i, is_halo, kv_only):
            p = i % 2
            xbq, xsq, colb = xbq2[p], xsq2[p], colb2[p]
            kxb, kxs, kcb = ("xbq", p), ("xsq", p), ("colb", p)
            if not kv_only:
                for ic in range(8):
                    lhs = xbq[:, ic * 128:(ic + 1) * 128]
                    self.mm(pb[0][:, 0:512], lhs,
                            wq[:, ic * 1024:ic * 1024 + 512], ic == 0,
                            ic == 7, reads=[kxb, ("fw", "wq")],
                            writes=[("pb", 0)], tag="qmm")
                    self.mm(pb[1][:, 0:512], lhs,
                            wq[:, ic * 1024 + 512:(ic + 1) * 1024], ic == 0,
                            ic == 7, reads=[kxb, ("fw", "wq")],
                            writes=[("pb", 1)], tag="qmm")
            for ic in range(8):
                self.mm(pb[2][:, 0:512], xbq[:, ic * 128:(ic + 1) * 128],
                        wkv[:, ic * 512:(ic + 1) * 512],
                        ic == 0, ic == 7, reads=[kxb, ("fw", "wkv")],
                        writes=[("pb", 2)], tag="kvmm")
            for ic in range(8):
                self.mm(pb[3][:, 0:1], xbq[:, ic * 128:(ic + 1) * 128],
                        cons["onescB"][:, :], ic == 0, ic == 7,
                        reads=[kxb, ("c", "onescB")], writes=[("pb", 3)],
                        tag="l2s1")
            for ic in range(8):
                self.mm(pb[3][:, 1:2], xsq[:, ic * 128:(ic + 1) * 128],
                        cons["onescB"][:, :], ic == 0, ic == 7,
                        reads=[kxs, ("c", "onescB")], writes=[("pb", 3)],
                        tag="l2s2")
            self.ew("scalar", "mul", colb[:, 0:1], pb[3][:, 0:1], 1.0 / D,
                    reads=[("pb", 3)], writes=[kcb], tag="l2m")
            self.act(colb[:, 2:3], colb[:, 0:1], AF.Square,
                     reads=[kcb], writes=[kcb], tag="l2msq")
            self.ew("vector", "scalar_tensor_tensor", colb[:, 2:3],
                    pb[3][:, 1:2], 1.0 / D, colb[:, 2:3], ALU.mult,
                    ALU.subtract, reads=[("pb", 3), kcb],
                    writes=[kcb], tag="l2var")
            self.act(colb[:, 2:3], colb[:, 2:3], AF.Ln,
                     reads=[kcb, ("c", "epsc")], writes=[kcb],
                     bias=cons["epsc"][:, 0:1])
            self.act(colb[:, 1:2], colb[:, 2:3], AF.Exp,
                     reads=[kcb], writes=[kcb], scale=-0.5,
                     tag="l2rstd")
            self.ew("vector", "tensor_tensor", colb[:, 3:4], colb[:, 0:1],
                    colb[:, 1:2], ALU.mult, reads=[kcb],
                    writes=[kcb], tag="l2rm")
            if not kv_only:
                for h in range(2):
                    self.act(qb[:, h * 512:(h + 1) * 512], pb[h][:, 0:512],
                             AF.Copy, reads=[("pb", h), kcb],
                             writes=[("qb",)], scale=colb[:, 1:2],
                             tag="qfix1")
                    self.ew("vector", "scalar_tensor_tensor",
                            qb[:, h * 512:(h + 1) * 512],
                            wsq[:, h * 512:(h + 1) * 512], colb[:, 3:4],
                            qb[:, h * 512:(h + 1) * 512], ALU.mult, ALU.add,
                            reads=[("wsq",), kcb, ("qb",)],
                            writes=[("qb",)], tag="qfix2")
            r = (i + (0 if is_halo else 2)) % 4
            self.act(kb[:, :], pb[2][:, 0:256], AF.Copy,
                     reads=[("pb", 2), kcb], writes=[("kb",)],
                     scale=colb[:, 1:2], tag="kfix1")
            self.ew("vector", "scalar_tensor_tensor", kb[:, :],
                    wskv[:, 0:256], colb[:, 3:4], kb[:, :], ALU.mult, ALU.add,
                    reads=[("wskv",), kcb, ("kb",)], writes=[("kb",)],
                    tag="kfix2")
            self.act(vring[r][:, :], pb[2][:, 256:512], AF.Copy,
                     reads=[("pb", 2), kcb], writes=[("vring", r)],
                     scale=colb[:, 1:2], tag="vfix1")
            self.ew("vector", "scalar_tensor_tensor", vring[r][:, :],
                    wskv[:, 256:512], colb[:, 3:4], vring[r][:, :], ALU.mult,
                    ALU.add, reads=[("wskv",), kcb, ("vring", r)],
                    writes=[("vring", r)], tag="vfix2")

        def x2a(i, is_halo):
            kvi = i + (0 if is_halo else 2)
            cos = cons["cosq"][:, kvi * 32:(kvi + 1) * 32]
            sin = cons["sinq"][:, kvi * 32:(kvi + 1) * 32]
            self.ew("vector", "tensor_tensor", q2b[:, 0:256], kb[:, :],
                    kb[:, :], ALU.mult, reads=[("kb",)], writes=[("q2b",)],
                    tag="krms")
            self.ew("vector", "reduce_sum", rsq[:, 16:20],
                    q2b[:, 0:256].rearrange("p (h d) -> p h d", d=64),
                    axis=AX.X, reads=[("q2b",)], writes=[("rsq",)])
            self.act(rsq[:, 16:20], rsq[:, 16:20], AF.Ln,
                     reads=[("rsq",), ("c", "epsc")], writes=[("rsq",)],
                     bias=cons["epsc"][:, 1:2], scale=1.0 / HD)
            self.act(rsq[:, 16:20], rsq[:, 16:20], AF.Exp, reads=[("rsq",)],
                     writes=[("rsq",)], scale=-0.5)
            self.ew("vector", "tensor_tensor",
                    sck[:, :].rearrange("p (h j) -> p h j", j=32),
                    cos[:, None, :].broadcast_to([128, 4, 32]),
                    rsq[:, 16:20, None].broadcast_to([128, 4, 32]), ALU.mult,
                    reads=[("c", "cosq"), ("rsq",)], writes=[("sck",)],
                    tag="sck")
            self.ew("vector", "tensor_tensor",
                    ssk[:, :].rearrange("p (h j) -> p h j", j=32),
                    sin[:, None, :].broadcast_to([128, 4, 32]),
                    rsq[:, 16:20, None].broadcast_to([128, 4, 32]), ALU.mult,
                    reads=[("c", "sinq"), ("rsq",)], writes=[("ssk",)],
                    tag="ssk")
            kv1 = kb[:, :].rearrange("p (h d) -> p h d", d=64)[:, :, 0:32]
            kv2 = kb[:, :].rearrange("p (h d) -> p h d", d=64)[:, :, 32:64]
            scv = sck[:, :].rearrange("p (h j) -> p h j", j=32)
            ssv = ssk[:, :].rearrange("p (h j) -> p h j", j=32)
            r1 = rt1[:, 0:128].rearrange("p (h d) -> p h d", d=32)
            r2 = rt2[:, 0:128].rearrange("p (h d) -> p h d", d=32)
            ro1 = ropk[:, :].rearrange("p (h d) -> p h d", d=64)[:, :, 0:32]
            ro2 = ropk[:, :].rearrange("p (h d) -> p h d", d=64)[:, :, 32:64]
            self.ew("vector", "tensor_tensor", r1, kv1, scv, ALU.mult,
                    reads=[("kb",), ("sck",)], writes=[("rt1",)])
            self.ew("vector", "tensor_tensor", r2, kv2, ssv, ALU.mult,
                    reads=[("kb",), ("ssk",)], writes=[("rt2",)])
            self.ew("vector", "tensor_tensor", ro1, r1, r2, ALU.add,
                    reads=[("rt1",), ("rt2",)], writes=[("ropk",)])
            self.ew("vector", "tensor_tensor", r1, kv2, scv, ALU.mult,
                    reads=[("kb",), ("sck",)], writes=[("rt1",)])
            self.ew("vector", "tensor_tensor", r2, kv1, ssv, ALU.mult,
                    reads=[("kb",), ("ssk",)], writes=[("rt2",)])
            self.ew("vector", "tensor_tensor", ro2, r1, r2, ALU.subtract,
                    reads=[("rt1",), ("rt2",)], writes=[("ropk",)])
            if is_halo:
                return
            self.ew("vector", "tensor_tensor", q2b[:, :], qb[:, :], qb[:, :],
                    ALU.mult, reads=[("qb",)], writes=[("q2b",)], tag="qrms")
            self.ew("vector", "reduce_sum", rsq[:, 0:16],
                    q2b[:, :].rearrange("p (h d) -> p h d", d=64), axis=AX.X,
                    reads=[("q2b",)], writes=[("rsq",)])
            self.act(rsq[:, 0:16], rsq[:, 0:16], AF.Ln,
                     reads=[("rsq",), ("c", "epsc")], writes=[("rsq",)],
                     bias=cons["epsc"][:, 1:2], scale=1.0 / HD)
            self.act(rsq[:, 0:16], rsq[:, 0:16], AF.Exp, reads=[("rsq",)],
                     writes=[("rsq",)], scale=-0.5)
            self.ew("vector", "tensor_tensor", rsq[:, 0:16], rsq[:, 0:16],
                    grow[:, :], ALU.mult, reads=[("rsq",), ("c", "grow")],
                    writes=[("rsq",)], tag="qgsc")
            self.ew("vector", "tensor_tensor",
                    sct[:, :].rearrange("p (h j) -> p h j", j=32),
                    cos[:, None, :].broadcast_to([128, 16, 32]),
                    rsq[:, 0:16, None].broadcast_to([128, 16, 32]), ALU.mult,
                    reads=[("c", "cosq"), ("rsq",)], writes=[("sct",)],
                    tag="sct")
            self.ew("vector", "tensor_tensor",
                    sst[:, :].rearrange("p (h j) -> p h j", j=32),
                    sin[:, None, :].broadcast_to([128, 16, 32]),
                    rsq[:, 0:16, None].broadcast_to([128, 16, 32]), ALU.mult,
                    reads=[("c", "sinq"), ("rsq",)], writes=[("sst",)],
                    tag="sst")
            qv1 = qb[:, :].rearrange("p (h d) -> p h d", d=64)[:, :, 0:32]
            qv2 = qb[:, :].rearrange("p (h d) -> p h d", d=64)[:, :, 32:64]
            scv = sct[:, :].rearrange("p (h j) -> p h j", j=32)
            ssv = sst[:, :].rearrange("p (h j) -> p h j", j=32)
            r1 = rt1[:, :].rearrange("p (h d) -> p h d", d=32)
            r2 = rt2[:, :].rearrange("p (h d) -> p h d", d=32)
            ro1 = ropq[:, :].rearrange("p (h d) -> p h d", d=64)[:, :, 0:32]
            ro2 = ropq[:, :].rearrange("p (h d) -> p h d", d=64)[:, :, 32:64]
            self.ew("vector", "tensor_tensor", r1, qv1, scv, ALU.mult,
                    reads=[("qb",), ("sct",)], writes=[("rt1",)])
            self.ew("vector", "tensor_tensor", r2, qv2, ssv, ALU.mult,
                    reads=[("qb",), ("sst",)], writes=[("rt2",)])
            self.ew("vector", "tensor_tensor", ro1, r1, r2, ALU.add,
                    reads=[("rt1",), ("rt2",)], writes=[("ropq",)])
            self.ew("vector", "tensor_tensor", r1, qv2, scv, ALU.mult,
                    reads=[("qb",), ("sct",)], writes=[("rt1",)])
            self.ew("vector", "tensor_tensor", r2, qv1, ssv, ALU.mult,
                    reads=[("qb",), ("sst",)], writes=[("rt2",)])
            self.ew("vector", "tensor_tensor", ro2, r1, r2, ALU.subtract,
                    reads=[("rt1",), ("rt2",)], writes=[("ropq",)])

        def x2b(i, is_halo):
            kvi = i + (0 if is_halo else 2)
            r = kvi % 4
            for kv in range(4):
                self.mm(pb[2][0:64, kv * 128:(kv + 1) * 128],
                        ropk[:, kv * 64:(kv + 1) * 64], cons["ident"][:, :],
                        True, True, reads=[("ropk",), ("c", "ident")],
                        writes=[("pb", 2)], tag="ktr")
            self.ew("vector", "tensor_copy", kring[r][0:64, :],
                    pb[2][0:64, 0:512], reads=[("pb", 2)],
                    writes=[("kring", r)], tag="krev")
            if is_halo:
                return
            par = i % 2
            for w in range(4):
                bank = w % 2
                for j in range(4):
                    h = 4 * w + j
                    self.mm(pb[bank][0:64, j * 128:(j + 1) * 128],
                            ropq[:, h * 64:(h + 1) * 64], cons["ident"][:, :],
                            True, True, reads=[("ropq",), ("c", "ident")],
                            writes=[("pb", bank)], tag="qtr")
                gg = 2 * (w // 2) + par
                off = (w % 2) * 512
                self.ew("vector", "tensor_copy",
                        qaug[gg][0:64, off:off + 512], pb[bank][0:64, 0:512],
                        reads=[("pb", bank)], writes=[("qaug", gg)],
                        tag="qtrev")

        def y_round(i, g):
            par = i % 2
            gg = 2 * g + par
            ktiles = [i, i + 1, i + 2]
            for n_kt, kvi in enumerate(ktiles):
                r = kvi % 4
                sb0 = 4 if n_kt != 1 else 6
                for kv2 in range(2):
                    kv = 2 * g + kv2
                    self.mm(pb[sb0 + kv2][:, 0:512],
                            kring[r][0:65, kv * 128:(kv + 1) * 128],
                            qaug[gg][0:65, kv2 * 512:(kv2 + 1) * 512],
                            True, True, reads=[("kring", r), ("qaug", gg)],
                            writes=[("pb", sb0 + kv2)], tag="smm")
                ab = attb[g][:, n_kt * 1024:(n_kt + 1) * 1024]
                for b in range(2):
                    self.act(ab[:, b * 512:(b + 1) * 512],
                             pb[sb0 + b][:, 0:512], AF.Exp,
                             reads=[("pb", sb0 + b)], writes=[("attb", g)],
                             tag="exp")
                mk = None
                if n_kt == 2:
                    mk, mkk = cons["cmask"], ("c", "cmask")
                elif n_kt == 0:
                    if i == 0:
                        mk, mkk = cons["mA"], ("c", "mA")
                    elif i == 1:
                        mk, mkk = cons["mC"], ("c", "mC")
                    else:
                        mk, mkk = cons["wmask"], ("c", "wmask")
                elif n_kt == 1 and i == 0:
                    mk, mkk = cons["mB"], ("c", "mB")
                if mk is not None:
                    self.ew("vector", "tensor_tensor",
                            ab.rearrange("p (h t) -> p h t", t=128),
                            ab.rearrange("p (h t) -> p h t", t=128),
                            mk[:, None, :].broadcast_to([128, 8, 128]),
                            ALU.mult, reads=[("attb", g), mkk],
                            writes=[("attb", g)], tag="mask")
            for half in range(2):
                for n_kt in range(3):
                    self.mm(pb[7][32 * half:32 * half + 1, 0:512],
                            cons["onescB"][:, :],
                            attb[g][:, n_kt * 1024 + half * 512:
                                    n_kt * 1024 + (half + 1) * 512],
                            n_kt == 0, n_kt == 2,
                            reads=[("attb", g), ("c", "onescB")],
                            writes=[("pb", 7)], tag="sums",
                            tp=(0, 32) if half else None)
            for kv2 in range(2):
                kv = 2 * g + kv2
                for j in range(4):
                    h8 = kv2 * 4 + j
                    pr = h8 // 2
                    odd = h8 % 2
                    for n_kt in range(3):
                        r = (i + n_kt) % 4
                        vl = vring[r][:, kv * 64:(kv + 1) * 64]
                        self.mm(pb[6][odd * 64:odd * 64 + 64,
                                      pr * 128:(pr + 1) * 128],
                                vl,
                                attb[g][:, n_kt * 1024 + h8 * 128:
                                        n_kt * 1024 + (h8 + 1) * 128],
                                n_kt == 0, n_kt == 2,
                                reads=[("vring", r), ("attb", g)],
                                writes=[("pb", 6)], tag="avmm",
                                tp=(0, 64) if odd else None)
            self.act(srow[:, 0:512], pb[7][0:1, 0:512], AF.Ln,
                     reads=[("pb", 7)], writes=[("srow",)], tag="sln")
            self.act(srow[:, 512:1024], pb[7][32:33, 0:512], AF.Ln,
                     reads=[("pb", 7)], writes=[("srow",)], tag="sln")
            self.act(srow[:, 0:1024], srow[:, 0:1024], AF.Exp,
                     reads=[("srow",)], writes=[("srow",)], scale=-1.0,
                     tag="sexp")
            for pr in range(4):
                he = srow[:, pr * 256:pr * 256 + 128]
                ho = srow[:, pr * 256 + 128:pr * 256 + 256]
                self.mm(pb[7][:, pr * 128:(pr + 1) * 128],
                        cons["lo64"][:, :], he, True, False,
                        reads=[("c", "lo64"), ("srow",)],
                        writes=[("pb", 7)], tag="rbmm")
                self.mm(pb[7][:, pr * 128:(pr + 1) * 128],
                        cons["hi64"][:, :], ho, False, True,
                        reads=[("c", "hi64"), ("srow",)],
                        writes=[("pb", 7)], tag="rbmm")
            self.ew("scalar", "copy", rbsb[:, :], pb[7][:, 0:512],
                    reads=[("pb", 7)], writes=[("rbsb",)], tag="rbev")
            yv = ybuf[:, :].rearrange("p (c t) -> p c t", t=256)[
                :, 4 * g:4 * g + 4, par * 128:par * 128 + 128]
            self.ew("vector", "tensor_tensor", yv,
                    pb[6][:, 0:512].rearrange("p (c t) -> p c t", t=128),
                    rbsb[:, :].rearrange("p (c t) -> p c t", t=128),
                    ALU.mult, reads=[("pb", 6), ("rbsb",)],
                    writes=[("ybuf", par, g)], tag="ydiv")

        def wo_resid(i):
            ykeys = [("ybuf", 0, 0), ("ybuf", 0, 1), ("ybuf", 1, 0),
                     ("ybuf", 1, 1)]
            xpv = x_sb[:, :].rearrange("p (c t) -> p c t", t=TC)
            xk = [("x", i - 1), ("x", i)]
            for ocg in range(2):
                for oc2 in range(4):
                    oc = 4 * ocg + oc2
                    bank = pb[4 + oc2 // 2]
                    slot = (oc2 % 2) * 256
                    for ic in range(8):
                        self.mm(bank[:, slot:slot + 256],
                                womb[:, ic * 1024 + oc * 128:
                                     ic * 1024 + oc * 128 + 128],
                                ybuf[:, ic * 256:(ic + 1) * 256],
                                ic == 0, ic == 7,
                                reads=[("fw", "womb")] + ykeys,
                                writes=[("pb", 4 + oc2 // 2)], tag="womm")
                for b2 in range(2):
                    ch = 4 * ocg + 2 * b2
                    xslice = xpv[:, ch:ch + 2, (i - 1) * 128:(i + 1) * 128]
                    self.ew("vector", "tensor_tensor", xslice, xslice,
                            pb[4 + b2][:, 0:512].rearrange(
                                "p (c t) -> p c t", t=256),
                            ALU.add, reads=xk + [("pb", 4 + b2)], writes=xk,
                            tag="aresid")

        for j in range(2):
            precast(j, True)
            x1(j, True, True)
            x2a(j, True)
            x2b(j, True)
        if "womb" in self.wcached:
            off = self.wcached["womb"]
            self.dma(womb[:, :], self.wdram[:, off:off + 8192],
                     reads=[("wd", "womb"), ("wspill",)],
                     writes=[("fw", "womb")], tag="wfast")
        for c in range(8) if "womb" not in self.wcached else []:
            for j in range(2):
                self.mm(pb[3][:, j:j + 1], cons["sel2"][:, :],
                        cons["pm2"][:, 2 * c + j:2 * c + j + 1], True, True,
                        reads=[("c", "sel2"), ("c", "pm2")],
                        writes=[("pb", 3)], tag="pmcol")
            self.ew("vector", "tensor_copy", scol[:, :], pb[3][:, 0:2],
                    reads=[("pb", 3)], writes=[("scol",)])
            self.ew("vector", "tensor_scalar", mcb[:, 0:64], eyep[:, 0:64],
                    scol[:, 0:1], None, ALU.mult,
                    reads=[("c", "eyepair"), ("scol",)], writes=[("mcb",)])
            self.ew("vector", "tensor_scalar", mcb[:, 64:128], eyep[:, 64:128],
                    scol[:, 1:2], None, ALU.mult,
                    reads=[("c", "eyepair"), ("scol",)], writes=[("mcb",)])
            self.dma(self.wstage[:, :], self.dram_in["wo"][:, c * 1024:
                                                           (c + 1) * 1024],
                     reads=(), writes=[("wstg",)], tag="wload")
            self.ew("vector", "tensor_copy", wochunk[:, :], self.wstage[:, :],
                    reads=[("wstg",)], writes=[("woch",)])
            for j in range(2):
                self.mm(pb[4][:, 0:512], mcb[:, :],
                        wochunk[:, j * 512:(j + 1) * 512], True, True,
                        reads=[("mcb",), ("woch",)], writes=[("pb", 4)],
                        tag="wommm")
                self.ew("scalar", "copy", womb[:, c * 1024 + j * 512:
                                               c * 1024 + (j + 1) * 512],
                        pb[4][:, 0:512], reads=[("pb", 4)],
                        writes=[("fw", "womb")], tag="womev")
        if "womb" not in self.wcached:
            self.spill_weight("womb", 8192, womb)

        precast(0, False)
        x1(0, False, False)
        x2a(0, False)
        precast(1, False)
        for i in range(NQT):
            if i + 2 < NQT:
                precast(i + 2, False)
            if i + 1 < NQT:
                x1(i + 1, False, False)
            x2b(i, False)
            if i + 1 < NQT:
                x2a(i + 1, False)
            y_round(i, 0)
            y_round(i, 1)
            if i % 2 == 1:
                wo_resid(i)
                if ffn2 is not None:
                    a1_tile((i - 1) // 2)
        if ffn2 is not None:
            self.emit_allreduce(f2msum, f2ccin, f2ccout)


def make_in_maps(inputs, k, TC, n_pairs):
    consts = build_consts()
    wst = stage_weights(inputs)
    x = np.asarray(inputs["L0_attn_x"], np.float32)
    in_maps = []
    for core in range(2 * n_pairs):
        p, half = core // 2, core % 2
        m = dict(wst)
        m.update(consts)
        xjobs = [x[j * n_pairs + p] for j in range(x.shape[0] // n_pairs)]
        m.update(build_core_inputs(xjobs, half, TC, consts))
        m = {n: v for n, v in m.items() if n in k.dram_in}
        missing = set(k.dram_in) - set(m)
        assert not missing, missing
        in_maps.append(m)
    return in_maps


def run(inputs, TC=2048, n_pairs=4):
    k = Kern(TC, n_pairs).build()
    in_maps = make_in_maps(inputs, k, TC, n_pairs)
    res = run_bass_kernel_spmd(k.nc, in_maps, core_ids=list(range(2 * n_pairs)))
    B = np.asarray(inputs["L0_attn_x"]).shape[0]
    outs = []
    for b in range(B):
        j, p = b // n_pairs, b % n_pairs
        y0 = res.results[2 * p][f"y{j}"]
        y1 = res.results[2 * p + 1][f"y{j}"]
        outs.append(np.concatenate([y0, y1], axis=1).T)
    return np.stack(outs)


def kernel(**inputs):
    return run(inputs, TC=2048, n_pairs=4).astype(np.float32)


# version 31 (final)
# speedup vs baseline: 1.0470x; 1.0250x over previous
from contextlib import ExitStack

import numpy as np
import ml_dtypes

import concourse.bass as bass
import concourse.mybir as mybir
from concourse.bass_utils import run_bass_kernel_spmd

F32 = mybir.dt.float32
BF16 = mybir.dt.bfloat16
AF = mybir.ActivationFunctionType
ALU = mybir.AluOpType
AX = mybir.AxisListType
BF = ml_dtypes.bfloat16

D = 1024
H = 16
KVH = 4
HD = 64
FT = 256
HT = 256
EPS_LN = 1e-5
EPS_RMS = 1e-6

ENGINES = ("tensor", "vector", "scalar", "gpsimd", "sync")
SELF_SYNC = True
NDMA = 16
A1_IN_ATTN = False


class Emitter:
    def __init__(self):
        self.ops = []
        self.barriers = []

    def op(self, engine, fn, reads=(), writes=(), dma=False, tag=""):
        self.ops.append(dict(engine=engine, fn=fn, reads=tuple(reads),
                             writes=tuple(writes), dma=dma, tag=tag,
                             idx=len(self.ops)))

    def dma_fence(self, emitfn):
        emitfn()

    def barrier(self):
        if not self.barriers or self.barriers[-1] != len(self.ops):
            self.barriers.append(len(self.ops))

    def finalize(self, nc, sems, dma_sem):
        ops, n = self.ops, len(self.ops)
        last_write, readers = {}, {}
        deps = [set() for _ in range(n)]
        for j, op in enumerate(ops):
            for r in op["reads"]:
                if r in last_write:
                    deps[j].add(last_write[r])
            for w in op["writes"]:
                if w in last_write:
                    deps[j].add(last_write[w])
                for rd in readers.get(w, ()):
                    deps[j].add(rd)
            for r in op["reads"]:
                readers.setdefault(r, []).append(j)
            for w in op["writes"]:
                last_write[w] = j
                readers[w] = []
        bidx = [0] * n
        bi = 0
        for j in range(n):
            while bi < len(self.barriers) and self.barriers[bi] <= j:
                bi += 1
            bidx[j] = bi

        def needs_edge(i, j):
            oi, oj = ops[i], ops[j]
            if oi["dma"]:
                return True
            if bidx[i] < bidx[j]:
                return False
            if oi["engine"] == oj["engine"]:
                if oi["engine"] == "tensor":
                    return False
                if not SELF_SYNC:
                    return False
            return True

        def prodkey(i):
            if ops[i]["dma"]:
                return ("dmaop", i)
            return ops[i]["engine"]

        for j in range(n):
            best = {}
            for i in deps[j]:
                if not needs_edge(i, j):
                    continue
                k = prodkey(i)
                if k[0] == "dmaop" if isinstance(k, tuple) else False:
                    best[k] = i
                else:
                    best[k] = max(best.get(k, -1), i)
            deps[j] = set(best.values())
        published = [False] * n
        for j in range(n):
            for i in deps[j]:
                published[i] = True
        for op in ops:
            if op["dma"]:
                published[op["idx"]] = True
        rank = [0] * n
        lane = [0] * n
        cnt = {e: 0 for e in ENGINES}
        lane_cnt = [0] * (NDMA + 1)
        dma_i = 0
        for j in range(n):
            if ops[j]["dma"]:
                if ops[j]["tag"] == "allred":
                    ln = NDMA
                else:
                    ln = dma_i % NDMA
                    dma_i += 1
                lane[j] = ln
                lane_cnt[ln] += 1
                rank[j] = lane_cnt[ln]
            elif published[j]:
                e = ops[j]["engine"]
                cnt[e] += 1
                rank[j] = cnt[e]
        self.lane_totals = list(lane_cnt)

        dkeys = [("dma", l) for l in range(NDMA + 1)]

        def new_vc():
            d = {e: 0 for e in ENGINES}
            for k in dkeys:
                d[k] = 0
            return d

        vc = {e: new_vc() for e in ENGINES}
        snap = {e: {} for e in ENGINES}
        for k in dkeys:
            snap[k] = {}
        waits = [None] * n
        cur_b = 0
        pubs_at = {e: 0 for e in ENGINES}
        lane_seen = [0] * (NDMA + 1)
        for j in range(n):
            while cur_b < len(self.barriers) and self.barriers[cur_b] <= j:
                for e in ENGINES:
                    for f in ENGINES:
                        if vc[e][f] < pubs_at[f]:
                            vc[e][f] = pubs_at[f]
                cur_b += 1
            op = ops[j]
            e = op["engine"]
            need = {}
            if op["tag"] == "dmafence":
                for l in range(NDMA + 1):
                    v = lane_seen[l] * (1 if l == NDMA else 16)
                    if v:
                        need[("dma", l)] = v
            for i in deps[j]:
                if not needs_edge(i, j):
                    continue
                if ops[i]["dma"]:
                    mult = 1 if lane[i] == NDMA else 16
                    key, val = ("dma", lane[i]), rank[i] * mult
                else:
                    key, val = ops[i]["engine"], rank[i]
                if val > need.get(key, 0):
                    need[key] = val
            if op["dma"] and rank[j] > 1:
                key = ("dma", lane[j])
                v = (rank[j] - 1) * (1 if lane[j] == NDMA else 16)
                if v > need.get(key, 0):
                    need[key] = v
            wl = []
            for key, val in sorted(need.items(), key=str):
                if vc[e][key] < val:
                    wl.append((key, val))
                    sn = snap[key].get(val)
                    if sn is not None:
                        for k2, v2 in sn.items():
                            if vc[e][k2] < v2:
                                vc[e][k2] = v2
                    if vc[e][key] < val:
                        vc[e][key] = val
            waits[j] = wl
            if op["dma"]:
                lane_seen[lane[j]] = rank[j]
            if published[j]:
                if op["dma"]:
                    sn = dict(vc[e])
                    k = ("dma", lane[j])
                    mult = 1 if lane[j] == NDMA else 16
                    sn[k] = rank[j] * mult
                    snap[k][rank[j] * mult] = sn
                else:
                    sn = dict(vc[e])
                    sn[e] = rank[j]
                    snap[e][rank[j]] = sn
                    pubs_at[e] = rank[j]

        semmap = dict(sems)
        for l in range(NDMA + 1):
            semmap[("dma", l)] = dma_sem[l]
        seg_bounds = [0] + list(self.barriers) + [n]
        for s in range(len(seg_bounds) - 1):
            lo, hi = seg_bounds[s], seg_bounds[s + 1]
            if hi <= lo:
                continue
            seg = [ops[j] for j in range(lo, hi)]
            with nc.Block() as blk:
                for e in ENGINES:
                    eops = [o for o in seg if o["engine"] == e]
                    if not eops:
                        continue

                    def body(eng, eops=eops, e=e):
                        for o in eops:
                            j = o["idx"]
                            for key, val in waits[j]:
                                eng.wait_ge(semmap[key], val)
                            ins = o["fn"](eng)
                            if published[j]:
                                if o["dma"]:
                                    ins.then_inc(dma_sem[lane[j]],
                                                 1 if lane[j] == NDMA else 16)
                                else:
                                    ins.then_inc(semmap[e], 1)

                    getattr(blk, e)(body)


def _chw(w):
    kin, nout = w.shape
    return np.ascontiguousarray(
        w.reshape(kin // 128, 128, nout).transpose(1, 0, 2).reshape(128, -1))


def build_consts():
    c = {}
    c["ident"] = np.eye(128, dtype=BF)
    ep = np.zeros((128, 128), np.float32)
    for a in range(2):
        for b in range(2):
            ep[a * 64:(a + 1) * 64, b * 64:(b + 1) * 64] = np.eye(64)
    c["eyepair"] = ep.astype(BF)
    c["onesrB"] = np.ones((1, 512), dtype=BF)
    c["onescB"] = np.ones((128, 1), dtype=BF)
    c["onesrF"] = np.ones((1, 128), dtype=np.float32)
    c["negrF"] = -np.ones((1, 128), dtype=np.float32)
    c["epsc"] = np.tile(np.array([[EPS_LN, EPS_RMS]], np.float32), (128, 1))
    lo = np.zeros((1, 128), np.float32); lo[0, :64] = 1.0
    hi = np.zeros((1, 128), np.float32); hi[0, 64:] = 1.0
    c["lo64"], c["hi64"] = lo, hi
    sel2 = np.zeros((2, 128), np.float32)
    sel2[0, :64] = 1.0
    sel2[1, 64:] = 1.0
    c["sel2"] = sel2
    q = np.arange(128)
    c["cmask"] = (q[None, :] >= q[:, None]).astype(BF)
    c["wmask"] = (q[None, :] < q[:, None]).astype(BF)
    c["zmask"] = np.zeros((128, 128), dtype=BF)
    c["onemask"] = np.ones((128, 128), dtype=BF)
    return c


def build_core_inputs(xjobs, half, TC, consts):
    d = {}
    t0 = half * TC
    for j, xb in enumerate(xjobs):
        d[f"x{j}"] = np.ascontiguousarray(xb[t0:t0 + TC].T).astype(np.float32)
        if half >= 1:
            d[f"xhalo{j}"] = np.ascontiguousarray(
                xb[t0 - HT:t0].T).astype(np.float32)
        else:
            d[f"xhalo{j}"] = np.zeros((D, HT), np.float32)
    nkv = TC // 128 + 2
    inv = (1.0 / 10000.0 ** (np.arange(0, HD, 2) / HD)).astype(np.float32)
    pos = np.arange(t0 - HT, t0 + TC, dtype=np.float32)
    pos = np.maximum(pos, 0.0)
    fr = np.outer(pos, inv)
    d["cosq"] = np.ascontiguousarray(np.cos(fr).astype(BF).reshape(
        nkv, 128, 32).transpose(1, 0, 2).reshape(128, -1))
    d["sinq"] = np.ascontiguousarray(np.sin(fr).astype(BF).reshape(
        nkv, 128, 32).transpose(1, 0, 2).reshape(128, -1))
    if half == 0:
        d["mA"], d["mB"], d["mC"] = (consts["zmask"],) * 3
    else:
        d["mA"], d["mB"], d["mC"] = (consts["wmask"], consts["onemask"],
                                     consts["wmask"])
    return d


def stage_weights(ins):
    g = lambda k: np.ascontiguousarray(np.asarray(ins[k], np.float32))
    d = {}
    d["wcn1"] = _chw(g("L0_ffn_cn_w1"))
    d["wcn2"] = _chw(g("L0_ffn_cn_w2"))
    d["wcn3"] = _chw(g("L0_ffn_cn_w3"))
    d["wcn4"] = _chw(g("L0_ffn_cn_w4"))
    d["wgc1"] = _chw(g("L0_ffn_gc_w1"))
    d["wgc2"] = _chw(g("L0_ffn_gc_w2"))
    d["wgc3"] = _chw(g("L0_ffn_gc_w3"))
    d["wq"] = _chw(np.ascontiguousarray(g("L0_attn_Wq").T))
    wkT = _chw(np.ascontiguousarray(g("L0_attn_Wk").T))
    wvT = _chw(np.ascontiguousarray(g("L0_attn_Wv").T))
    d["wkv"] = np.ascontiguousarray(np.concatenate(
        [wkT.reshape(128, 8, 256), wvT.reshape(128, 8, 256)],
        axis=2).reshape(128, 4096))
    d["wo"] = _chw(np.ascontiguousarray(g("L0_attn_Wo").T))
    d["bcn1"] = np.ascontiguousarray(g("L0_ffn_cn_b1").reshape(8, 128).T)
    d["bcn2"] = np.ascontiguousarray(g("L0_ffn_cn_b2").reshape(4, 128).T)
    d["bcn3"] = np.ascontiguousarray(g("L0_ffn_cn_b3").reshape(2, 128).T)
    d["bgc1"] = np.ascontiguousarray(g("L0_ffn_gc_b1").reshape(4, 128).T)
    d["bgc2"] = np.ascontiguousarray(g("L0_ffn_gc_b2").reshape(2, 128).T)
    d["b4s"] = np.array([[float(g("L0_ffn_cn_b4").reshape(-1)[0]),
                          float(g("L0_ffn_gc_b3").reshape(-1)[0]),
                          float(g("L0_ffn_adaptive_range").reshape(-1)[0]),
                          0.0]], np.float32)
    d["qgain"] = g("L0_attn_q_gain").reshape(1, 16)
    pm = g("L0_attn_pair_mix")
    pm2 = np.zeros((2, 16), np.float32)
    for c in range(8):
        pm2[0, 2 * c] = pm[c, 0, 0]
        pm2[1, 2 * c] = pm[c, 1, 0]
        pm2[0, 2 * c + 1] = pm[c, 0, 1]
        pm2[1, 2 * c + 1] = pm[c, 1, 1]
    d["pm2"] = pm2
    return d


class Kern:
    def __init__(self, TC, n_pairs, njobs=2):
        self.TC = TC
        self.n_pairs = n_pairs
        self.njobs = njobs
        self.NFT = TC // FT
        self.NQT = TC // 128
        self.nc = bass.Bass("TRN2", target_bir_lowering=False)
        self.E = Emitter()
        self.es = ExitStack()
        self.dram_in = {}

    def din(self, name, shape, dt=F32):
        t = self.nc.dram_tensor(name, list(shape), dt, kind="ExternalInput")
        self.dram_in[name] = t
        return t

    def sb(self, name, shape, dt, scope=None):
        return (scope or self.es).enter_context(
            self.nc.sbuf_tensor(name, list(shape), dt))

    def dma(self, dst, src, reads, writes, tag=""):
        self.E.op("sync", lambda e, d=dst, s=src: e.dma_start(d, s),
                  reads=reads, writes=writes, dma=True, tag=tag)

    def mm(self, out, lhsT, rhs, start, stop, reads, writes, tag="", tp=None):
        self.E.op("tensor",
                  lambda e, o=out, l=lhsT, r=rhs, st=start, sp=stop, t=tp:
                  e.matmul(o, l, r, start=st, stop=sp, tile_position=t),
                  reads=reads, writes=writes, tag=tag)

    def act(self, out, in_, func, reads, writes, bias=0.0, scale=1.0, tag=""):
        self.E.op("scalar",
                  lambda e, o=out, i=in_, f=func, b=bias, s=scale:
                  e.activation(o, i, f, bias=b, scale=s),
                  reads=reads, writes=writes, tag=tag)

    def ew(self, engine, meth, *args, reads=(), writes=(), tag="", **kw):
        def fn(e, m=meth, a=args, k=kw):
            return getattr(e, m)(*a, **k)
        self.E.op(engine, fn, reads=reads, writes=writes, tag=tag)

    def load_weight(self, name, width, scope, tag="w"):
        dst = self.sb(name + "_b" + tag, [128, width], BF16, scope)
        if name in self.wcached:
            off = self.wcached[name]
            self.dma(dst[:, :], self.wdram[:, off:off + width],
                     reads=[("wd", name), ("wspill",)],
                     writes=[("fw", name)], tag="wfast")
            return dst
        src = self.dram_in[name]
        for i in range(0, width, 1024):
            hi = min(width, i + 1024)
            stg = self.wstage
            kst = ("wstg",)
            self.dma(stg[:, 0:hi - i], src[:, i:hi],
                     reads=[("wspill",)], writes=[kst], tag="wload")
            self.ew("vector", "tensor_copy", dst[:, i:hi],
                    stg[:, 0:hi - i],
                    reads=[kst], writes=[("fw", name)],
                    tag="wcast")
        return dst

    def emit_allreduce(self, msum, ccin, ccout):
        self.dma(ccin[:, :], msum[:, :], reads=[("msum",)], writes=[("ccin",)])
        groups = [[2 * i, 2 * i + 1] for i in range(self.n_pairs)]
        self.E.op("gpsimd",
                  lambda e: e.collective_compute("AllReduce", ALU.add, groups,
                                                 [ccin.ap().opt()],
                                                 [ccout.ap().opt()]),
                  reads=[("ccin",)], writes=[("ccout",)], dma=True,
                  tag="allred")
        self.dma(msum[:, :], ccout[:, :], reads=[("ccout",)],
                 writes=[("msum",)])

    def spill_weight(self, name, width, tensor):
        off = self.wd_off
        self.wd_off += width
        self.wcached[name] = off
        self.dma(self.wdram[:, off:off + width], tensor[:, :],
                 reads=[("fw", name)], writes=[("wd", name), ("wspill",)],
                 tag="wspill")

    def build(self):
        nc, E, TC = self.nc, self.E, self.TC
        NFT, NQT = self.NFT, self.NQT
        nkv = NQT + 2

        x_in = [self.din(f"x{j}", [D, TC]) for j in range(self.njobs)]
        xh_in = [self.din(f"xhalo{j}", [D, HT]) for j in range(self.njobs)]
        for k, shp in dict(wcn1=(128, 8192), wcn2=(128, 4096), wcn3=(128, 1024),
                           wcn4=(128, 2), wgc1=(128, 4096), wgc2=(128, 1024),
                           wgc3=(128, 2), wq=(128, 8192), wkv=(128, 4096),
                           wo=(128, 8192)).items():
            self.din(k, shp)
        cshapes = dict(ident=((128, 128), BF16), eyepair=((128, 128), BF16),
                       onesrB=((1, 512), BF16),
                       onescB=((128, 1), BF16), onesrF=((1, 128), F32),
                       lo64=((1, 128), F32), hi64=((1, 128), F32),
                       negrF=((1, 128), F32),
                       sel2=((2, 128), F32), epsc=((128, 2), F32),
                       cmask=((128, 128), BF16),
                       wmask=((128, 128), BF16), mA=((128, 128), BF16),
                       mB=((128, 128), BF16), mC=((128, 128), BF16),
                       cosq=((128, nkv * 32), BF16), sinq=((128, nkv * 32), BF16),
                       bcn1=((128, 8), F32), bcn2=((128, 4), F32),
                       bcn3=((128, 2), F32), bgc1=((128, 4), F32),
                       bgc2=((128, 2), F32), b4s=((1, 4), F32),
                       qgain=((1, 16), F32), pm2=((2, 16), F32))
        for k, (shp, dt) in cshapes.items():
            self.din(k, shp, dt)
        y_out = [nc.dram_tensor(f"y{j}", [D, TC], F32, kind="ExternalOutput")
                 for j in range(self.njobs)]
        statsd = nc.dram_tensor("statsd", [4 * (NFT + 1), 2 * FT], F32,
                                kind="Internal")
        ccin = nc.dram_tensor("ccin", [128, 8], F32, kind="Internal")
        ccout = nc.dram_tensor("ccout", [128, 8], F32, kind="Internal")

        x_sb = self.sb("x_sb", [128, 8 * TC], F32)
        xh_sb = self.sb("xh_sb", [128, 8 * HT], F32)
        cons = {}
        for k, (shp, dt) in cshapes.items():
            cons[k] = self.sb("c_" + k, shp, dt)
        rowc = self.sb("rowc", [1, 32], F32)
        msum = self.sb("msum", [128, 8], F32)
        gmb = self.sb("gmb", [128, 8], BF16)
        gfh1 = self.sb("gfh1", [128, 4], BF16)
        gfh2 = self.sb("gfh2", [128, 2], BF16)
        grow = self.sb("grow", [128, 16], F32)
        mneg = self.sb("mneg", [1, 16], F32)
        self.wstage = self.sb("wstage", [128, 1024], F32)
        self.wstage2 = self.wstage
        self.wdram = nc.dram_tensor("wdram", [128, 40960], BF16,
                                    kind="Internal")
        self.wcached = {}
        self.wd_off = 0

        pb = [self.es.enter_context(
            nc.psum_tensor(f"pb{i}", [128, 512], F32)) for i in range(8)]
        self.pb = pb

        sems = {e: self.es.enter_context(nc.semaphore("sem_" + e))
                for e in ENGINES}
        dma_sem = [self.es.enter_context(nc.semaphore(f"sem_dma{l}"))
                   for l in range(NDMA + 1)]

        for k in cons:
            self.dma(cons[k][:, :], self.dram_in[k][:, :], reads=(),
                     writes=[("c", k)], tag="cload")
        self.ew("vector", "tensor_scalar", rowc[:, 8:9], cons["b4s"][:, 0:1],
                0.5, None, ALU.mult, reads=[("c", "b4s")], writes=[("c", "rowc")])
        self.ew("vector", "tensor_scalar", rowc[:, 9:10], cons["b4s"][:, 1:2],
                0.5, None, ALU.mult, reads=[("c", "b4s")], writes=[("c", "rowc")])
        self.ew("vector", "tensor_scalar", mneg[:, :], cons["qgain"][:, :],
                -1.0, None, ALU.mult, reads=[("c", "qgain")],
                writes=[("c", "mneg")])
        self.ew("vector", "tensor_tensor", mneg[:, :], mneg[:, :],
                cons["qgain"][:, :], ALU.max,
                reads=[("c", "mneg"), ("c", "qgain")], writes=[("c", "mneg")])
        self.ew("vector", "tensor_scalar", mneg[:, :], mneg[:, :], -8.08,
                -1e-3, ALU.mult, ALU.add, reads=[("c", "mneg")],
                writes=[("c", "mneg")])
        self.ew("vector", "tensor_scalar", rowc[:, 16:32], cons["qgain"][:, :],
                0.125, None, ALU.mult, reads=[("c", "qgain")],
                writes=[("c", "rowc")])
        self.mm(pb[0][:, 0:16], cons["onesrF"][:, :], rowc[:, 16:32], True,
                True, reads=[("c", "onesrF"), ("c", "rowc")],
                writes=[("pb", 0)], tag="growb")
        self.ew("vector", "tensor_copy", grow[:, :], pb[0][:, 0:16],
                reads=[("pb", 0)], writes=[("c", "grow")])
        E.barrier()

        for job in range(self.njobs):
            xv_all = x_in[job][:, :].rearrange("(c p) t -> p c t", p=128)
            for ft in range(NFT):
                dst = x_sb[:, :].rearrange("p (c t) -> p c t", t=TC)[
                    :, :, ft * FT:(ft + 1) * FT]
                self.dma(dst, xv_all[:, :, ft * FT:(ft + 1) * FT], reads=(),
                         writes=[("x", ft * (FT // 128) + q)
                                 for q in range(FT // 128)], tag="xload")
            self.dma(xh_sb[:, :].rearrange("p (c t) -> p c t", t=HT),
                     xh_in[job][:, :].rearrange("(c p) t -> p c t", p=128),
                     reads=(), writes=[("xh", 0), ("xh", 1)])

            with ExitStack() as sc:
                wspec = dict(wcn1=8192, wcn2=4096, wcn3=1024, wcn4=2,
                             wgc1=4096, wgc2=1024, wgc3=2)
                fw = {k: self.load_weight(k, w, sc, tag=f"1{job}")
                      for k, w in wspec.items()}
                if job == 0:
                    for k, w in wspec.items():
                        self.spill_weight(k, w, fw[k])
                    self.E.op("sync", lambda e: e.nop(),
                              reads=[("wspill",)], writes=(), tag="wjoin")
                self.ffn_phase(fw, cons, rowc, msum, gmb, gfh1, gfh2, grow,
                               x_sb, xh_sb, statsd, ccin, ccout, sc,
                               call=2 * job + 0, halo=True, store=None,
                               y_out=None)
            E.op("sync", lambda e: e.nop(), reads=(), writes=(),
                 tag="dmafence")
            E.barrier()

            with ExitStack() as sc:
                self.attn_phase(cons, grow, mneg, x_sb, xh_sb, sc, job,
                                ffn2=(2 * job + 1, statsd, ccin, ccout, msum)
                                if A1_IN_ATTN else None)
            E.op("sync", lambda e: e.nop(), reads=(), writes=(),
                 tag="dmafence")
            E.barrier()
            if job == 0:
                self.E.op("sync", lambda e: e.nop(), reads=[("wspill",)],
                          writes=(), tag="wjoin")
                E.barrier()

            with ExitStack() as sc:
                fw = {k: self.load_weight(k, w, sc, tag=f"2{job}") for k, w in
                      dict(wcn1=8192, wcn2=4096, wcn3=1024, wcn4=2, wgc1=4096,
                           wgc2=1024, wgc3=2).items()}
                self.ffn_phase(fw, cons, rowc, msum, gmb, gfh1, gfh2, grow,
                               x_sb, xh_sb, statsd, ccin, ccout, sc,
                               call=2 * job + 1, halo=False, store=True,
                               y_out=y_out[job], skip_a1=A1_IN_ATTN)
            E.op("sync", lambda e: e.nop(), reads=(), writes=(),
                 tag="dmafence")
            E.barrier()

        def finwait(e):
            ins = None
            for l in range(NDMA + 1):
                if E.lane_totals[l]:
                    m = 1 if l == NDMA else 16
                    ins = e.wait_ge(dma_sem[l], m * E.lane_totals[l])
            return ins
        E.op("sync", finwait, reads=(), writes=(), tag="final")

        E.finalize(nc, sems, dma_sem)
        nc.finalize()
        return self

    def ffn_phase(self, fw, cons, rowc, msum, gmb, gfh1, gfh2, grow,
                  x_sb, xh_sb, statsd, ccin, ccout, sc,
                  call, halo, store, y_out, skip_a1=False):
        nc, E, TC, NFT = self.nc, self.E, self.TC, self.NFT
        pb = self.pb
        W = FT
        sbs = lambda nm, shape, dt: self.sb(f"{nm}_{call}", shape, dt, sc)
        NP = 3
        out0 = [sbs(f"out0{p}", [128, 8 * W], F32) for p in range(NP)]
        ob = [sbs(f"ob{p}", [128, 8 * W], BF16) for p in range(NP)]
        h1b = [sbs(f"h1b{p}", [128, 8 * W], BF16) for p in range(NP)]
        h2b = [sbs(f"h2b{p}", [128, 4 * W], BF16) for p in range(2)]
        h3b = [sbs(f"h3b{p}", [128, 2 * W], BF16) for p in range(2)]
        utmp = [sbs(f"utmp{p}", [128, 4 * W], F32) for p in range(2)]
        xb2 = [sbs(f"xb{p}", [128, 8 * W], BF16) for p in range(2)]
        strow2 = [sbs(f"strow{p}", [1, 2 * W], F32) for p in range(2)]
        trow2 = [sbs(f"trow{p}", [1, W], F32) for p in range(2)]
        crow2 = [sbs(f"crow{p}", [1, 2 * W], F32) for p in range(2)]
        red2 = [sbs(f"red{p}", [128, 8], F32) for p in range(2)]
        xsqb = [sbs("xsqb", [128, 8 * W], BF16)] * 2

        ntile = NFT + (1 if halo else 0)

        def tile_info(ft):
            is_halo = ft == NFT
            src_ = xh_sb if is_halo else x_sb
            TT = HT if is_halo else TC
            toff = 0 if is_halo else ft * W
            pfx = "xh" if is_halo else "x"
            xkeys = [(pfx, (toff // 128) + q) for q in range(W // 128)]
            xv = src_[:, :].rearrange("p (c t) -> p c t", t=TT)[
                :, :, toff:toff + W]
            return is_halo, xv, xkeys

        if not skip_a1:
            self.ew("vector", "memset", msum[:, :], 0.0, reads=(),
                    writes=[("msum",)])
        for ft in range(0 if skip_a1 else ntile):
            is_halo, xv, xkeys = tile_info(ft)
            p = ft % 2
            xb, strow, red = xb2[p], strow2[p], red2[p]
            kxb, kst = ("xb", call, p), ("strow", call, p)
            srow_bank = 7 - 2 * p
            bc_bank = 6 - 2 * p
            xbv = xb[:, :].rearrange("p (c t) -> p c t", t=W)
            self.ew("vector", "tensor_copy", xbv, xv, reads=xkeys,
                    writes=[kxb], tag="a1cast")
            for ic in range(8):
                self.mm(pb[srow_bank][0:1, 0:W], cons["onescB"][:, :],
                        xb[:, ic * W:(ic + 1) * W], ic == 0, ic == 7,
                        reads=[("c", "onescB"), kxb],
                        writes=[("pb", srow_bank)], tag="a1s1")
            xsq = xsqb[p]
            kxs = ("xsqb", call)
            self.act(xsq[:, :], xb[:, :], AF.Square, reads=[kxb],
                     writes=[kxs], tag="a1sq")
            for ic in range(8):
                self.mm(pb[srow_bank][0:1, W:2 * W], cons["onescB"][:, :],
                        xsq[:, ic * W:(ic + 1) * W], ic == 0, ic == 7,
                        reads=[("c", "onescB"), kxs],
                        writes=[("pb", srow_bank)], tag="a1s2")
            self.ew("scalar", "mul", strow[:, 0:W],
                    pb[srow_bank][0:1, 0:W], 1.0 / D,
                    reads=[("pb", srow_bank)], writes=[kst], tag="a1m")
            self.act(strow[:, W:2 * W], strow[:, 0:W], AF.Square,
                     reads=[kst], writes=[kst], tag="a1msq")
            self.ew("vector", "scalar_tensor_tensor", strow[:, W:2 * W],
                    pb[srow_bank][0:1, W:2 * W], 1.0 / D, strow[:, W:2 * W],
                    ALU.mult, ALU.subtract,
                    reads=[("pb", srow_bank), kst], writes=[kst], tag="a1var")
            self.act(strow[:, W:2 * W], strow[:, W:2 * W], AF.Ln,
                     reads=[kst, ("c", "epsc")], writes=[kst],
                     bias=cons["epsc"][0:1, 0:1])
            self.act(strow[:, W:2 * W], strow[:, W:2 * W], AF.Exp,
                     reads=[kst], writes=[kst], scale=-0.5)
            self.dma(statsd[call * (NFT + 1) + ft:call * (NFT + 1) + ft + 1,
                            0:2 * W], strow[0:1, 0:2 * W],
                     reads=[kst], writes=[("std", call, ft)])
            if is_halo:
                continue
            self.mm(pb[bc_bank][:, 0:W], cons["onesrF"][:, :], strow[:, 0:W],
                    True, True, reads=[("c", "onesrF"), kst],
                    writes=[("pb", bc_bank)], tag="a1bm")
            self.mm(pb[bc_bank][:, W:2 * W], cons["onesrF"][:, :],
                    strow[:, W:2 * W], True, True,
                    reads=[("c", "onesrF"), kst],
                    writes=[("pb", bc_bank)], tag="a1br")
            self.ew("vector", "tensor_tensor", xbv, xv,
                    pb[bc_bank][:, None, 0:W].broadcast_to([128, 8, W]),
                    ALU.subtract, reads=xkeys + [("pb", bc_bank)],
                    writes=[kxb], tag="a1sub")
            self.ew("vector", "tensor_tensor", xbv, xbv,
                    pb[bc_bank][:, None, W:2 * W].broadcast_to([128, 8, W]),
                    ALU.mult, reads=[kxb, ("pb", bc_bank)], writes=[kxb],
                    tag="a1mul")
            self.ew("vector", "reduce_sum", red[:, 0:8], xbv, axis=AX.X,
                    reads=[kxb], writes=[("red", call, p)], tag="a1red")
            self.ew("vector", "tensor_tensor", msum[:, :], msum[:, :],
                    red[:, 0:8], ALU.add,
                    reads=[("msum",), ("red", call, p)], writes=[("msum",)])

        if not skip_a1:
            self.emit_allreduce(msum, ccin, ccout)

        def emit_a2():
         self.ew("vector", "tensor_scalar", gmb[:, :], msum[:, :],
                 1.0 / (2 * TC), None, ALU.mult, reads=[("msum",)],
                 writes=[("gmb",)], tag="gmb")
         for oc in range(4):
             for ic in range(8):
                 self.mm(pb[6][:, oc:oc + 1],
                         fw["wgc1"][:, ic * 512 + oc * 128:
                                    ic * 512 + oc * 128 + 128],
                         gmb[:, ic:ic + 1], ic == 0, ic == 7,
                         reads=[("fw", "wgc1"), ("gmb",)], writes=[("pb", 6)],
                         tag="gf1")
         for oc in range(4):
             self.act(gfh1[:, oc:oc + 1], pb[6][:, oc:oc + 1], AF.Gelu,
                      reads=[("pb", 6), ("c", "bgc1")], writes=[("gfh1",)],
                      bias=cons["bgc1"][:, oc:oc + 1])
         for oc in range(2):
             for ic in range(4):
                 self.mm(pb[6][:, 8 + oc:9 + oc],
                         fw["wgc2"][:, ic * 256 + oc * 128:
                                    ic * 256 + oc * 128 + 128],
                         gfh1[:, ic:ic + 1], ic == 0, ic == 3,
                         reads=[("fw", "wgc2"), ("gfh1",)], writes=[("pb", 6)],
                         tag="gf2")
         for oc in range(2):
             self.act(gfh2[:, oc:oc + 1], pb[6][:, 8 + oc:9 + oc], AF.Gelu,
                      reads=[("pb", 6), ("c", "bgc2")], writes=[("gfh2",)],
                      bias=cons["bgc2"][:, oc:oc + 1])
         for ic in range(2):
             self.mm(pb[6][0:1, 12:13], fw["wgc3"][:, ic:ic + 1],
                     gfh2[:, ic:ic + 1], ic == 0, ic == 1,
                     reads=[("fw", "wgc3"), ("gfh2",)], writes=[("pb", 6)],
                     tag="gf3")
         self.act(rowc[:, 10:11], pb[6][0:1, 12:13], AF.Tanh,
                  reads=[("pb", 6), ("c", "rowc")], writes=[("c", "rowc")],
                  bias=rowc[:, 9:10], scale=0.5, tag="gftanh")
         self.ew("vector", "tensor_scalar", rowc[:, 0:1], cons["b4s"][:, 2:3],
                 0.3, None, ALU.mult, reads=[("c", "b4s")],
                 writes=[("c", "rowc")])
         self.ew("vector", "tensor_tensor", rowc[:, 11:12], rowc[:, 10:11],
                 cons["b4s"][:, 2:3], ALU.mult,
                 reads=[("c", "rowc"), ("c", "b4s")], writes=[("c", "rowc")])
         self.ew("vector", "tensor_scalar", rowc[:, 1:2], rowc[:, 11:12], 0.7,
                 0.1, ALU.mult, ALU.add, reads=[("c", "rowc")],
                 writes=[("c", "rowc")])
         self.ew("vector", "tensor_scalar", rowc[:, 2:4], rowc[:, 0:2], 0.5,
                 None, ALU.mult, reads=[("c", "rowc")], writes=[("c", "rowc")])
         self.ew("vector", "tensor_scalar", rowc[:, 4:8], rowc[:, 0:4], -1.0,
                 None, ALU.mult, reads=[("c", "rowc")], writes=[("c", "rowc")])

        def emit_xhat(ft):
            is_halo, xv, xkeys = tile_info(ft)
            p = ft % 3
            strow = strow2[ft % 2]
            kst = ("strow", call, ft % 2)
            o0 = out0[p]
            o0v = o0[:, :].rearrange("p (c t) -> p c t", t=W)
            o0keys = [("o0", call, p, q) for q in range(4)]
            bcb = 6
            self.dma(strow[0:1, 0:2 * W],
                     statsd[call * (NFT + 1) + ft:call * (NFT + 1) + ft + 1,
                            0:2 * W],
                     reads=[("std", call, ft)], writes=[kst], tag="streload")
            self.mm(pb[bcb][:, 0:W], cons["onesrF"][:, :], strow[:, 0:W],
                    True, True, reads=[("c", "onesrF"), kst],
                    writes=[("pb", bcb)], tag="a3bm")
            self.mm(pb[bcb][:, W:2 * W], cons["onesrF"][:, :],
                    strow[:, W:2 * W], True, True,
                    reads=[("c", "onesrF"), kst],
                    writes=[("pb", bcb)], tag="a3br")
            self.ew("vector", "tensor_tensor", o0v, xv,
                    pb[bcb][:, None, 0:W].broadcast_to([128, 8, W]),
                    ALU.subtract, reads=xkeys + [("pb", bcb)],
                    writes=o0keys, tag="a3sub")
            self.ew("vector", "tensor_tensor", o0v, o0v,
                    pb[bcb][:, None, W:2 * W].broadcast_to([128, 8, W]),
                    ALU.mult, reads=o0keys + [("pb", bcb)],
                    writes=o0keys, tag="a3mul")
            self.ew("vector", "tensor_copy", ob[p][:, :], o0[:, :],
                    reads=o0keys, writes=[("ob", call, p)], tag="obcast")

        def emit_cycle(ft, cyc):
            p = ft % 3
            p2 = ft % 2
            ba, bb_ = 2 * p, 2 * p + 1
            o0 = out0[p]
            obp, h1p, h2p, h3p = ob[p], h1b[p], h2b[p2], h3b[p2]
            utp, trow, crow = utmp[p2], trow2[p2], crow2[p2]
            ko = ("ob", call, p)
            kh1, kh2, kh3 = ("h1b", call, p), ("h2b", call, p2), ("h3b", call, p2)
            ku, ktr, kcr = ("ut", call, p2), ("trow", call, p2), ("crow", call, p2)
            o0keys = [("o0", call, p, q) for q in range(4)]
            for oc in range(8):
                bank = ba if oc % 2 == 0 else bb_
                for ic in range(8):
                    self.mm(pb[bank][:, 0:W],
                            fw["wcn1"][:, ic * 1024 + oc * 128:
                                       ic * 1024 + oc * 128 + 128],
                            obp[:, ic * W:(ic + 1) * W], ic == 0, ic == 7,
                            reads=[("fw", "wcn1"), ko],
                            writes=[("pb", bank)], tag="h1mm")
                self.act(h1p[:, oc * W:(oc + 1) * W], pb[bank][:, 0:W],
                         AF.Gelu, reads=[("pb", bank), ("c", "bcn1")],
                         writes=[kh1], bias=cons["bcn1"][:, oc:oc + 1],
                         tag="h1act")
            for oc in range(4):
                bank = ba if oc % 2 == 0 else bb_
                for ic in range(8):
                    self.mm(pb[bank][:, 0:W],
                            fw["wcn2"][:, ic * 512 + oc * 128:
                                       ic * 512 + oc * 128 + 128],
                            h1p[:, ic * W:(ic + 1) * W], ic == 0, ic == 7,
                            reads=[("fw", "wcn2"), kh1],
                            writes=[("pb", bank)], tag="h2mm")
                self.act(h2p[:, oc * W:(oc + 1) * W], pb[bank][:, 0:W],
                         AF.Gelu, reads=[("pb", bank), ("c", "bcn2")],
                         writes=[kh2], bias=cons["bcn2"][:, oc:oc + 1],
                         tag="h2act")
            for oc in range(2):
                bank = ba if oc % 2 == 0 else bb_
                for ic in range(4):
                    self.mm(pb[bank][:, 0:W],
                            fw["wcn3"][:, ic * 256 + oc * 128:
                                       ic * 256 + oc * 128 + 128],
                            h2p[:, ic * W:(ic + 1) * W], ic == 0, ic == 3,
                            reads=[("fw", "wcn3"), kh2],
                            writes=[("pb", bank)], tag="h3mm")
                self.act(h3p[:, oc * W:(oc + 1) * W], pb[bank][:, 0:W],
                         AF.Gelu, reads=[("pb", bank), ("c", "bcn3")],
                         writes=[kh3], bias=cons["bcn3"][:, oc:oc + 1],
                         tag="h3act")
            for ic in range(2):
                self.mm(pb[ba][0:1, 0:W], fw["wcn4"][:, ic:ic + 1],
                        h3p[:, ic * W:(ic + 1) * W], ic == 0, ic == 1,
                        reads=[("fw", "wcn4"), kh3],
                        writes=[("pb", ba)], tag="zmm")
            self.act(trow[:, 0:W], pb[ba][0:1, 0:W], AF.Tanh,
                     reads=[("pb", ba), ("c", "rowc")], writes=[ktr],
                     bias=rowc[:, 8:9], scale=0.5, tag="tanh")

        def emit_tail(ft, cyc):
            p = ft % 3
            p2 = ft % 2
            ba, bb_ = 2 * p, 2 * p + 1
            o0 = out0[p]
            obp = ob[p]
            utp, trow, crow = utmp[p2], trow2[p2], crow2[p2]
            ko = ("ob", call, p)
            ku, ktr, kcr = ("ut", call, p2), ("trow", call, p2), ("crow", call, p2)
            o0keys = [("o0", call, p, q) for q in range(4)]
            aoff = 2 if cyc == 2 else 0
            self.ew("gpsimd", "tensor_scalar", crow[:, 0:W], trow[:, 0:W],
                    rowc[:, aoff:aoff + 1], rowc[:, 1 + aoff:2 + aoff],
                    ALU.mult, ALU.add, reads=[ktr, ("c", "rowc")],
                    writes=[kcr], tag="crowp")
            self.ew("gpsimd", "tensor_scalar", crow[:, W:2 * W], trow[:, 0:W],
                    rowc[:, 4 + aoff:5 + aoff], rowc[:, 5 + aoff:6 + aoff],
                    ALU.mult, ALU.add, reads=[ktr, ("c", "rowc")],
                    writes=[kcr], tag="crown")
            self.mm(pb[bb_][:, 0:2 * W], cons["onesrF"][:, :],
                    crow[:, 0:2 * W], True, True,
                    reads=[("c", "onesrF"), kcr], writes=[("pb", bb_)],
                    tag="cbmm")
            for (qa, qb_, sgn_a, sgn_b) in ((0, 2, 0, W), (1, 3, W, 0)):
                offa, offb = 2 * qa * W, 2 * qb_ * W
                va = o0[:, offa:offa + 2 * W].rearrange(
                    "p (c t) -> p c t", t=W)
                vb = o0[:, offb:offb + 2 * W].rearrange(
                    "p (c t) -> p c t", t=W)
                ta = utp[:, 0:2 * W].rearrange("p (c t) -> p c t", t=W)
                tb = utp[:, 2 * W:4 * W].rearrange("p (c t) -> p c t", t=W)
                ka, kb_ = ("o0", call, p, qa), ("o0", call, p, qb_)
                self.ew("vector", "tensor_tensor", ta, vb,
                        pb[bb_][:, None, sgn_a:sgn_a + W].broadcast_to(
                            [128, 2, W]), ALU.mult,
                        reads=[kb_, ("pb", bb_)], writes=[ku], tag="umulA")
                self.ew("vector", "tensor_tensor", tb, va,
                        pb[bb_][:, None, sgn_b:sgn_b + W].broadcast_to(
                            [128, 2, W]), ALU.mult,
                        reads=[ka, ("pb", bb_)], writes=[ku], tag="umulB")
                if cyc < 2:
                    self.ew("vector", "tensor_tensor", va, va, ta, ALU.add,
                            reads=[ka, ku], writes=[ka], tag="uaddA")
                    self.ew("vector", "tensor_tensor", vb, vb, tb, ALU.add,
                            reads=[kb_, ku], writes=[kb_], tag="uaddB")
                else:
                    self.ew("vector", "scalar_tensor_tensor", va, va, 0.5,
                            ta, ALU.mult, ALU.add, reads=[ka, ku],
                            writes=[ka], tag="uhalfA")
                    self.ew("vector", "scalar_tensor_tensor", vb, vb, 0.5,
                            tb, ALU.mult, ALU.add, reads=[kb_, ku],
                            writes=[kb_], tag="uhalfB")
            if cyc < 2:
                self.ew("vector", "tensor_copy", obp[:, :], o0[:, :],
                        reads=o0keys, writes=[ko], tag="obcast")

        def emit_resid(ft):
            is_halo, xv, xkeys = tile_info(ft)
            p = ft % 3
            o0v = out0[p][:, :].rearrange("p (c t) -> p c t", t=W)
            o0keys = [("o0", call, p, q) for q in range(4)]
            self.ew("vector", "tensor_tensor", xv, xv, o0v, ALU.add,
                    reads=xkeys + o0keys, writes=xkeys, tag="resid")
            if store is not None and not is_halo:
                dst = y_out[:, :].rearrange("(c p) t -> p c t", p=128)[
                    :, :, ft * W:(ft + 1) * W]
                self.dma(dst, xv, reads=xkeys, writes=[("yout", call, ft)],
                         tag="ystore")

        for t in range(min(3, ntile)):
            emit_xhat(t)
        first = [True]

        def cyc_full(t, cyc):
            emit_cycle(t, cyc)
            if first[0]:
                emit_a2()
                first[0] = False
            emit_tail(t, cyc)

        for k in range(0, ntile, 3):
            tiles = [t for t in range(k, min(k + 3, ntile))]
            for cyc in range(2):
                for t in tiles:
                    cyc_full(t, cyc)
            for t in tiles:
                cyc_full(t, 2)
                emit_resid(t)
                if t + 3 < ntile:
                    emit_xhat(t + 3)

    def attn_phase(self, cons, grow, mneg, x_sb, xh_sb, sc, job=0,
                   ffn2=None):
        nc, E, TC, NQT = self.nc, self.E, self.TC, self.NQT
        pb = self.pb
        jj = str(job)
        wq = self.load_weight("wq", 8192, sc, tag=f"a{job}")
        wkv = self.load_weight("wkv", 4096, sc, tag=f"a{job}")
        womb = self.sb("womb" + jj, [128, 8192], BF16, sc)
        wochunk = self.sb("wochunk" + jj, [128, 1024], BF16, sc)
        mcb = self.sb("mcb" + jj, [128, 128], BF16, sc)
        scol = self.sb("scol" + jj, [128, 2], F32, sc)
        eyep = cons["eyepair"]
        xbq2 = [self.sb(f"xbq{p}" + jj, [128, 1024], BF16, sc)
                for p in range(2)]
        xsq2 = [self.sb(f"xsq{p}" + jj, [128, 1024], BF16, sc)
                for p in range(2)]
        colb2 = [self.sb(f"colb{p}" + jj, [128, 4], F32, sc)
                 for p in range(2)]
        wsq = self.sb("wsq" + jj, [128, 1024], F32, sc)
        wskv = self.sb("wskv" + jj, [128, 512], F32, sc)
        qb = self.sb("qb" + jj, [128, 1024], BF16, sc)
        kb = self.sb("kb" + jj, [128, 256], BF16, sc)
        q2b = self.sb("q2b" + jj, [128, 1024], BF16, sc)
        rsq = self.sb("rsq" + jj, [128, 24], F32, sc)
        ropq = self.sb("ropq" + jj, [128, 1024], BF16, sc)
        ropk = self.sb("ropk" + jj, [128, 256], BF16, sc)
        rt1 = self.sb("rt1" + jj, [128, 512], BF16, sc)
        rt2 = self.sb("rt2" + jj, [128, 512], BF16, sc)
        sct = self.sb("sct" + jj, [128, 512], BF16, sc)
        sst = self.sb("sst" + jj, [128, 512], BF16, sc)
        sck = self.sb("sck" + jj, [128, 128], BF16, sc)
        ssk = self.sb("ssk" + jj, [128, 128], BF16, sc)
        qaug = [self.sb(f"qaug{g}" + jj, [65, 1024], BF16, sc)
                for g in range(4)]
        kring = [self.sb(f"kring{r}" + jj, [65, 512], BF16, sc)
                 for r in range(4)]
        vring = [self.sb(f"vring{r}" + jj, [128, 256], BF16, sc)
                 for r in range(4)]
        attb = [self.sb(f"attb{g}" + jj, [128, 3072], BF16, sc)
                for g in range(2)]
        ybuf = self.sb("ybuf" + jj, [128, 2048], BF16, sc)
        srow = self.sb("srow" + jj, [1, 1024], F32, sc)
        rbsb = self.sb("rbsb" + jj, [128, 512], F32, sc)
        strw = self.sb("strw" + jj, [1, 256], F32, sc)
        if ffn2 is not None:
            f2call, f2statsd, f2ccin, f2ccout, f2msum = ffn2
            a1xb = self.sb("a1xb" + jj, [128, 8 * FT], BF16, sc)
            a1sq = self.sb("a1sq" + jj, [128, 8 * FT], BF16, sc)
            a1st = self.sb("a1st" + jj, [1, 2 * FT], F32, sc)
            a1rd = self.sb("a1rd" + jj, [128, 8], F32, sc)
            a1bc = self.sb("a1bc" + jj, [128, 512], F32, sc)
            self.ew("vector", "memset", f2msum[:, :], 0.0, reads=(),
                    writes=[("msum",)])

        def a1_tile(ft):
            W = FT
            NFT = self.NFT
            xkeys = [("x", 2 * ft), ("x", 2 * ft + 1)]
            xv = x_sb[:, :].rearrange("p (c t) -> p c t", t=TC)[
                :, :, ft * W:(ft + 1) * W]
            xbv = a1xb[:, :].rearrange("p (c t) -> p c t", t=W)
            kxb, kxs, kst = ("a1xb",), ("a1sq",), ("a1st",)
            self.ew("vector", "tensor_copy", xbv, xv, reads=xkeys,
                    writes=[kxb], tag="fa1cast")
            for ic in range(8):
                self.mm(pb[4][0:1, 0:W], cons["onescB"][:, :],
                        a1xb[:, ic * W:(ic + 1) * W], ic == 0, ic == 7,
                        reads=[("c", "onescB"), kxb], writes=[("pb", 4)],
                        tag="fa1s1")
            self.act(a1sq[:, :], a1xb[:, :], AF.Square, reads=[kxb],
                     writes=[kxs], tag="fa1sq")
            for ic in range(8):
                self.mm(pb[4][0:1, W:2 * W], cons["onescB"][:, :],
                        a1sq[:, ic * W:(ic + 1) * W], ic == 0, ic == 7,
                        reads=[("c", "onescB"), kxs], writes=[("pb", 4)],
                        tag="fa1s2")
            self.ew("scalar", "mul", a1st[:, 0:W], pb[4][0:1, 0:W], 1.0 / D,
                    reads=[("pb", 4)], writes=[kst], tag="fa1m")
            self.act(a1st[:, W:2 * W], a1st[:, 0:W], AF.Square,
                     reads=[kst], writes=[kst], tag="fa1msq")
            self.ew("vector", "scalar_tensor_tensor", a1st[:, W:2 * W],
                    pb[4][0:1, W:2 * W], 1.0 / D, a1st[:, W:2 * W],
                    ALU.mult, ALU.subtract, reads=[("pb", 4), kst],
                    writes=[kst], tag="fa1var")
            self.act(a1st[:, W:2 * W], a1st[:, W:2 * W], AF.Ln,
                     reads=[kst, ("c", "epsc")], writes=[kst],
                     bias=cons["epsc"][0:1, 0:1])
            self.act(a1st[:, W:2 * W], a1st[:, W:2 * W], AF.Exp,
                     reads=[kst], writes=[kst], scale=-0.5)
            self.dma(f2statsd[f2call * (NFT + 1) + ft:
                              f2call * (NFT + 1) + ft + 1, 0:2 * W],
                     a1st[0:1, 0:2 * W], reads=[kst],
                     writes=[("std", f2call, ft)], tag="fa1save")
            self.dma(a1bc[:, 0:2 * W],
                     f2statsd[f2call * (NFT + 1) + ft:
                              f2call * (NFT + 1) + ft + 1,
                              0:2 * W].partition_broadcast(128),
                     reads=[("std", f2call, ft)], writes=[("a1bc",)],
                     tag="fa1bc")
            self.ew("vector", "tensor_tensor", xbv, xv,
                    a1bc[:, None, 0:W].broadcast_to([128, 8, W]),
                    ALU.subtract, reads=xkeys + [("a1bc",)], writes=[kxb],
                    tag="fa1sub")
            self.ew("vector", "tensor_tensor", xbv, xbv,
                    a1bc[:, None, W:2 * W].broadcast_to([128, 8, W]),
                    ALU.mult, reads=[kxb, ("a1bc",)], writes=[kxb],
                    tag="fa1mul")
            self.ew("vector", "reduce_sum", a1rd[:, 0:8], xbv, axis=AX.X,
                    reads=[kxb], writes=[("a1rd",)], tag="fa1red")
            self.ew("vector", "tensor_tensor", f2msum[:, :], f2msum[:, :],
                    a1rd[:, 0:8], ALU.add, reads=[("msum",), ("a1rd",)],
                    writes=[("msum",)], tag="fa1acc")

        if "wq" not in self.wcached:
            self.spill_weight("wq", 8192, wq)
            self.spill_weight("wkv", 4096, wkv)
        for ic in range(8):
            self.mm(pb[4][0:1, 0:512], cons["onescB"][:, :],
                    wq[:, ic * 1024:ic * 1024 + 512], ic == 0, ic == 7,
                    reads=[("c", "onescB"), ("fw", "wq")], writes=[("pb", 4)],
                    tag="wsum")
            self.mm(pb[4][0:1, 512:1024] if False else pb[5][0:1, 0:512],
                    cons["onescB"][:, :],
                    wq[:, ic * 1024 + 512:(ic + 1) * 1024], ic == 0, ic == 7,
                    reads=[("c", "onescB"), ("fw", "wq")], writes=[("pb", 5)],
                    tag="wsum")
            self.mm(pb[6][0:1, 0:512], cons["onescB"][:, :],
                    wkv[:, ic * 512:(ic + 1) * 512], ic == 0, ic == 7,
                    reads=[("c", "onescB"), ("fw", "wkv")], writes=[("pb", 6)],
                    tag="wsum")
        self.ew("vector", "tensor_copy", srow[:, 0:512], pb[4][0:1, 0:512],
                reads=[("pb", 4)], writes=[("srow",)], tag="wsev")
        self.ew("vector", "tensor_copy", srow[:, 512:1024], pb[5][0:1, 0:512],
                reads=[("pb", 5)], writes=[("srow",)], tag="wsev")
        for h in range(2):
            self.mm(pb[4][:, 0:512], cons["negrF"][:, :],
                    srow[:, h * 512:(h + 1) * 512], True, True,
                    reads=[("c", "negrF"), ("srow",)], writes=[("pb", 4)],
                    tag="wsbc")
            self.ew("vector", "tensor_copy", wsq[:, h * 512:(h + 1) * 512],
                    pb[4][:, 0:512], reads=[("pb", 4)], writes=[("wsq",)],
                    tag="wsev2")
        self.ew("vector", "tensor_copy", srow[:, 0:512], pb[6][0:1, 0:512],
                reads=[("pb", 6)], writes=[("srow",)], tag="wsev")
        self.mm(pb[4][:, 0:512], cons["negrF"][:, :], srow[:, 0:512],
                True, True, reads=[("c", "negrF"), ("srow",)],
                writes=[("pb", 4)], tag="wsbc")
        self.ew("vector", "tensor_copy", wskv[:, :], pb[4][:, 0:512],
                reads=[("pb", 4)], writes=[("wskv",)], tag="wsev2")

        for gg in range(4):
            g = gg % 2
            for h8 in range(8):
                h = 8 * g + h8
                self.ew("vector", "tensor_scalar",
                        qaug[gg][64:65, h8 * 128:(h8 + 1) * 128],
                        cons["onesrF"][:, 0:128], mneg[0:1, h:h + 1], None,
                        ALU.mult, reads=[("c", "onesrF"), ("c", "mneg")],
                        writes=[("qaug", gg)], tag="qaug64")
        for r in range(4):
            self.ew("vector", "memset", kring[r][64:65, :], 1.0, reads=(),
                    writes=[("kring", r)], tag="kr64")

        def precast(i, is_halo):
            if is_halo:
                src, TT, toff = xh_sb, HT, i * 128
                xkeys = [("xh", i)]
            else:
                src, TT, toff = x_sb, TC, i * 128
                xkeys = [("x", i)]
            xv = src[:, :].rearrange("p (c t) -> p c t", t=TT)[
                :, :, toff:toff + 128]
            p = i % 2
            xbv = xbq2[p][:, :].rearrange("p (c t) -> p c t", t=128)
            self.ew("vector", "tensor_copy", xbv, xv, reads=xkeys,
                    writes=[("xbq", p)], tag="l2cast")
            self.act(xsq2[p][:, :], xbq2[p][:, :], AF.Square,
                     reads=[("xbq", p)], writes=[("xsq", p)], tag="l2sq")

        def x1(i, is_halo, kv_only):
            p = i % 2
            xbq, xsq, colb = xbq2[p], xsq2[p], colb2[p]
            kxb, kxs, kcb = ("xbq", p), ("xsq", p), ("colb", p)
            if not kv_only:
                for ic in range(8):
                    lhs = xbq[:, ic * 128:(ic + 1) * 128]
                    self.mm(pb[0][:, 0:512], lhs,
                            wq[:, ic * 1024:ic * 1024 + 512], ic == 0,
                            ic == 7, reads=[kxb, ("fw", "wq")],
                            writes=[("pb", 0)], tag="qmm")
                    self.mm(pb[1][:, 0:512], lhs,
                            wq[:, ic * 1024 + 512:(ic + 1) * 1024], ic == 0,
                            ic == 7, reads=[kxb, ("fw", "wq")],
                            writes=[("pb", 1)], tag="qmm")
            for ic in range(8):
                self.mm(pb[2][:, 0:512], xbq[:, ic * 128:(ic + 1) * 128],
                        wkv[:, ic * 512:(ic + 1) * 512],
                        ic == 0, ic == 7, reads=[kxb, ("fw", "wkv")],
                        writes=[("pb", 2)], tag="kvmm")
            for ic in range(8):
                self.mm(pb[3][:, 0:1], xbq[:, ic * 128:(ic + 1) * 128],
                        cons["onescB"][:, :], ic == 0, ic == 7,
                        reads=[kxb, ("c", "onescB")], writes=[("pb", 3)],
                        tag="l2s1")
            for ic in range(8):
                self.mm(pb[3][:, 1:2], xsq[:, ic * 128:(ic + 1) * 128],
                        cons["onescB"][:, :], ic == 0, ic == 7,
                        reads=[kxs, ("c", "onescB")], writes=[("pb", 3)],
                        tag="l2s2")
            self.ew("scalar", "mul", colb[:, 0:1], pb[3][:, 0:1], 1.0 / D,
                    reads=[("pb", 3)], writes=[kcb], tag="l2m")
            self.act(colb[:, 2:3], colb[:, 0:1], AF.Square,
                     reads=[kcb], writes=[kcb], tag="l2msq")
            self.ew("vector", "scalar_tensor_tensor", colb[:, 2:3],
                    pb[3][:, 1:2], 1.0 / D, colb[:, 2:3], ALU.mult,
                    ALU.subtract, reads=[("pb", 3), kcb],
                    writes=[kcb], tag="l2var")
            self.act(colb[:, 2:3], colb[:, 2:3], AF.Ln,
                     reads=[kcb, ("c", "epsc")], writes=[kcb],
                     bias=cons["epsc"][:, 0:1])
            self.act(colb[:, 1:2], colb[:, 2:3], AF.Exp,
                     reads=[kcb], writes=[kcb], scale=-0.5,
                     tag="l2rstd")
            self.ew("vector", "tensor_tensor", colb[:, 3:4], colb[:, 0:1],
                    colb[:, 1:2], ALU.mult, reads=[kcb],
                    writes=[kcb], tag="l2rm")
            if not kv_only:
                for h in range(2):
                    self.act(qb[:, h * 512:(h + 1) * 512], pb[h][:, 0:512],
                             AF.Copy, reads=[("pb", h), kcb],
                             writes=[("qb",)], scale=colb[:, 1:2],
                             tag="qfix1")
                    self.ew("vector", "scalar_tensor_tensor",
                            qb[:, h * 512:(h + 1) * 512],
                            wsq[:, h * 512:(h + 1) * 512], colb[:, 3:4],
                            qb[:, h * 512:(h + 1) * 512], ALU.mult, ALU.add,
                            reads=[("wsq",), kcb, ("qb",)],
                            writes=[("qb",)], tag="qfix2")
            r = (i + (0 if is_halo else 2)) % 4
            self.act(kb[:, :], pb[2][:, 0:256], AF.Copy,
                     reads=[("pb", 2), kcb], writes=[("kb",)],
                     scale=colb[:, 1:2], tag="kfix1")
            self.ew("vector", "scalar_tensor_tensor", kb[:, :],
                    wskv[:, 0:256], colb[:, 3:4], kb[:, :], ALU.mult, ALU.add,
                    reads=[("wskv",), kcb, ("kb",)], writes=[("kb",)],
                    tag="kfix2")
            self.act(vring[r][:, :], pb[2][:, 256:512], AF.Copy,
                     reads=[("pb", 2), kcb], writes=[("vring", r)],
                     scale=colb[:, 1:2], tag="vfix1")
            self.ew("vector", "scalar_tensor_tensor", vring[r][:, :],
                    wskv[:, 256:512], colb[:, 3:4], vring[r][:, :], ALU.mult,
                    ALU.add, reads=[("wskv",), kcb, ("vring", r)],
                    writes=[("vring", r)], tag="vfix2")

        def x2a_k(i, is_halo):
            kvi = i + (0 if is_halo else 2)
            cos = cons["cosq"][:, kvi * 32:(kvi + 1) * 32]
            sin = cons["sinq"][:, kvi * 32:(kvi + 1) * 32]
            self.ew("vector", "tensor_tensor", q2b[:, 0:256], kb[:, :],
                    kb[:, :], ALU.mult, reads=[("kb",)], writes=[("q2b",)],
                    tag="krms")
            self.ew("vector", "reduce_sum", rsq[:, 16:20],
                    q2b[:, 0:256].rearrange("p (h d) -> p h d", d=64),
                    axis=AX.X, reads=[("q2b",)], writes=[("rsq",)])
            self.act(rsq[:, 16:20], rsq[:, 16:20], AF.Ln,
                     reads=[("rsq",), ("c", "epsc")], writes=[("rsq",)],
                     bias=cons["epsc"][:, 1:2], scale=1.0 / HD)
            self.act(rsq[:, 16:20], rsq[:, 16:20], AF.Exp, reads=[("rsq",)],
                     writes=[("rsq",)], scale=-0.5)
            self.ew("vector", "tensor_tensor",
                    sck[:, :].rearrange("p (h j) -> p h j", j=32),
                    cos[:, None, :].broadcast_to([128, 4, 32]),
                    rsq[:, 16:20, None].broadcast_to([128, 4, 32]), ALU.mult,
                    reads=[("c", "cosq"), ("rsq",)], writes=[("sck",)],
                    tag="sck")
            self.ew("vector", "tensor_tensor",
                    ssk[:, :].rearrange("p (h j) -> p h j", j=32),
                    sin[:, None, :].broadcast_to([128, 4, 32]),
                    rsq[:, 16:20, None].broadcast_to([128, 4, 32]), ALU.mult,
                    reads=[("c", "sinq"), ("rsq",)], writes=[("ssk",)],
                    tag="ssk")
            kv1 = kb[:, :].rearrange("p (h d) -> p h d", d=64)[:, :, 0:32]
            kv2 = kb[:, :].rearrange("p (h d) -> p h d", d=64)[:, :, 32:64]
            scv = sck[:, :].rearrange("p (h j) -> p h j", j=32)
            ssv = ssk[:, :].rearrange("p (h j) -> p h j", j=32)
            r1 = rt1[:, 0:128].rearrange("p (h d) -> p h d", d=32)
            r2 = rt2[:, 0:128].rearrange("p (h d) -> p h d", d=32)
            ro1 = ropk[:, :].rearrange("p (h d) -> p h d", d=64)[:, :, 0:32]
            ro2 = ropk[:, :].rearrange("p (h d) -> p h d", d=64)[:, :, 32:64]
            self.ew("vector", "tensor_tensor", r1, kv1, scv, ALU.mult,
                    reads=[("kb",), ("sck",)], writes=[("rt1",)])
            self.ew("vector", "tensor_tensor", r2, kv2, ssv, ALU.mult,
                    reads=[("kb",), ("ssk",)], writes=[("rt2",)])
            self.ew("vector", "tensor_tensor", ro1, r1, r2, ALU.add,
                    reads=[("rt1",), ("rt2",)], writes=[("ropk",)])
            self.ew("vector", "tensor_tensor", r1, kv2, scv, ALU.mult,
                    reads=[("kb",), ("sck",)], writes=[("rt1",)])
            self.ew("vector", "tensor_tensor", r2, kv1, ssv, ALU.mult,
                    reads=[("kb",), ("ssk",)], writes=[("rt2",)])
            self.ew("vector", "tensor_tensor", ro2, r1, r2, ALU.subtract,
                    reads=[("rt1",), ("rt2",)], writes=[("ropk",)])

        def x2a(i, is_halo):
            if is_halo:
                return
            kvi = i + 2
            cos = cons["cosq"][:, kvi * 32:(kvi + 1) * 32]
            sin = cons["sinq"][:, kvi * 32:(kvi + 1) * 32]
            self.ew("vector", "tensor_tensor", q2b[:, :], qb[:, :], qb[:, :],
                    ALU.mult, reads=[("qb",)], writes=[("q2b",)], tag="qrms")
            self.ew("vector", "reduce_sum", rsq[:, 0:16],
                    q2b[:, :].rearrange("p (h d) -> p h d", d=64), axis=AX.X,
                    reads=[("q2b",)], writes=[("rsq",)])
            self.act(rsq[:, 0:16], rsq[:, 0:16], AF.Ln,
                     reads=[("rsq",), ("c", "epsc")], writes=[("rsq",)],
                     bias=cons["epsc"][:, 1:2], scale=1.0 / HD)
            self.act(rsq[:, 0:16], rsq[:, 0:16], AF.Exp, reads=[("rsq",)],
                     writes=[("rsq",)], scale=-0.5)
            self.ew("vector", "tensor_tensor", rsq[:, 0:16], rsq[:, 0:16],
                    grow[:, :], ALU.mult, reads=[("rsq",), ("c", "grow")],
                    writes=[("rsq",)], tag="qgsc")
            self.ew("vector", "tensor_tensor",
                    sct[:, :].rearrange("p (h j) -> p h j", j=32),
                    cos[:, None, :].broadcast_to([128, 16, 32]),
                    rsq[:, 0:16, None].broadcast_to([128, 16, 32]), ALU.mult,
                    reads=[("c", "cosq"), ("rsq",)], writes=[("sct",)],
                    tag="sct")
            self.ew("vector", "tensor_tensor",
                    sst[:, :].rearrange("p (h j) -> p h j", j=32),
                    sin[:, None, :].broadcast_to([128, 16, 32]),
                    rsq[:, 0:16, None].broadcast_to([128, 16, 32]), ALU.mult,
                    reads=[("c", "sinq"), ("rsq",)], writes=[("sst",)],
                    tag="sst")
            qv1 = qb[:, :].rearrange("p (h d) -> p h d", d=64)[:, :, 0:32]
            qv2 = qb[:, :].rearrange("p (h d) -> p h d", d=64)[:, :, 32:64]
            scv = sct[:, :].rearrange("p (h j) -> p h j", j=32)
            ssv = sst[:, :].rearrange("p (h j) -> p h j", j=32)
            r1 = rt1[:, :].rearrange("p (h d) -> p h d", d=32)
            r2 = rt2[:, :].rearrange("p (h d) -> p h d", d=32)
            ro1 = ropq[:, :].rearrange("p (h d) -> p h d", d=64)[:, :, 0:32]
            ro2 = ropq[:, :].rearrange("p (h d) -> p h d", d=64)[:, :, 32:64]
            self.ew("vector", "tensor_tensor", r1, qv1, scv, ALU.mult,
                    reads=[("qb",), ("sct",)], writes=[("rt1",)])
            self.ew("vector", "tensor_tensor", r2, qv2, ssv, ALU.mult,
                    reads=[("qb",), ("sst",)], writes=[("rt2",)])
            self.ew("vector", "tensor_tensor", ro1, r1, r2, ALU.add,
                    reads=[("rt1",), ("rt2",)], writes=[("ropq",)])
            self.ew("vector", "tensor_tensor", r1, qv2, scv, ALU.mult,
                    reads=[("qb",), ("sct",)], writes=[("rt1",)])
            self.ew("vector", "tensor_tensor", r2, qv1, ssv, ALU.mult,
                    reads=[("qb",), ("sst",)], writes=[("rt2",)])
            self.ew("vector", "tensor_tensor", ro2, r1, r2, ALU.subtract,
                    reads=[("rt1",), ("rt2",)], writes=[("ropq",)])

        def x2b(i, is_halo):
            kvi = i + (0 if is_halo else 2)
            r = kvi % 4
            for kv in range(4):
                self.mm(pb[2][0:64, kv * 128:(kv + 1) * 128],
                        ropk[:, kv * 64:(kv + 1) * 64], cons["ident"][:, :],
                        True, True, reads=[("ropk",), ("c", "ident")],
                        writes=[("pb", 2)], tag="ktr")
            self.ew("vector", "tensor_copy", kring[r][0:64, :],
                    pb[2][0:64, 0:512], reads=[("pb", 2)],
                    writes=[("kring", r)], tag="krev")
            if is_halo:
                return
            par = i % 2
            for w in range(4):
                bank = w % 2
                for j in range(4):
                    h = 4 * w + j
                    self.mm(pb[bank][0:64, j * 128:(j + 1) * 128],
                            ropq[:, h * 64:(h + 1) * 64], cons["ident"][:, :],
                            True, True, reads=[("ropq",), ("c", "ident")],
                            writes=[("pb", bank)], tag="qtr")
                gg = 2 * (w // 2) + par
                off = (w % 2) * 512
                self.ew("vector", "tensor_copy",
                        qaug[gg][0:64, off:off + 512], pb[bank][0:64, 0:512],
                        reads=[("pb", bank)], writes=[("qaug", gg)],
                        tag="qtrev")

        def y_round(i, g):
            par = i % 2
            gg = 2 * g + par
            ktiles = [i, i + 1, i + 2]
            for n_kt, kvi in enumerate(ktiles):
                r = kvi % 4
                sb0 = 4 if n_kt != 1 else 6
                for kv2 in range(2):
                    kv = 2 * g + kv2
                    self.mm(pb[sb0 + kv2][:, 0:512],
                            kring[r][0:65, kv * 128:(kv + 1) * 128],
                            qaug[gg][0:65, kv2 * 512:(kv2 + 1) * 512],
                            True, True, reads=[("kring", r), ("qaug", gg)],
                            writes=[("pb", sb0 + kv2)], tag="smm")
                ab = attb[g][:, n_kt * 1024:(n_kt + 1) * 1024]
                for b in range(2):
                    self.act(ab[:, b * 512:(b + 1) * 512],
                             pb[sb0 + b][:, 0:512], AF.Exp,
                             reads=[("pb", sb0 + b)], writes=[("attb", g)],
                             tag="exp")
                mk = None
                if n_kt == 2:
                    mk, mkk = cons["cmask"], ("c", "cmask")
                elif n_kt == 0:
                    if i == 0:
                        mk, mkk = cons["mA"], ("c", "mA")
                    elif i == 1:
                        mk, mkk = cons["mC"], ("c", "mC")
                    else:
                        mk, mkk = cons["wmask"], ("c", "wmask")
                elif n_kt == 1 and i == 0:
                    mk, mkk = cons["mB"], ("c", "mB")
                if mk is not None:
                    self.ew("vector", "tensor_tensor",
                            ab.rearrange("p (h t) -> p h t", t=128),
                            ab.rearrange("p (h t) -> p h t", t=128),
                            mk[:, None, :].broadcast_to([128, 8, 128]),
                            ALU.mult, reads=[("attb", g), mkk],
                            writes=[("attb", g)], tag="mask")
            for half in range(2):
                for n_kt in range(3):
                    self.mm(pb[7][32 * half:32 * half + 1, 0:512],
                            cons["onescB"][:, :],
                            attb[g][:, n_kt * 1024 + half * 512:
                                    n_kt * 1024 + (half + 1) * 512],
                            n_kt == 0, n_kt == 2,
                            reads=[("attb", g), ("c", "onescB")],
                            writes=[("pb", 7)], tag="sums",
                            tp=(0, 32) if half else None)
            for kv2 in range(2):
                kv = 2 * g + kv2
                for j in range(4):
                    h8 = kv2 * 4 + j
                    pr = h8 // 2
                    odd = h8 % 2
                    for n_kt in range(3):
                        r = (i + n_kt) % 4
                        vl = vring[r][:, kv * 64:(kv + 1) * 64]
                        self.mm(pb[6][odd * 64:odd * 64 + 64,
                                      pr * 128:(pr + 1) * 128],
                                vl,
                                attb[g][:, n_kt * 1024 + h8 * 128:
                                        n_kt * 1024 + (h8 + 1) * 128],
                                n_kt == 0, n_kt == 2,
                                reads=[("vring", r), ("attb", g)],
                                writes=[("pb", 6)], tag="avmm",
                                tp=(0, 64) if odd else None)
            self.act(srow[:, 0:512], pb[7][0:1, 0:512], AF.Ln,
                     reads=[("pb", 7)], writes=[("srow",)], tag="sln")
            self.act(srow[:, 512:1024], pb[7][32:33, 0:512], AF.Ln,
                     reads=[("pb", 7)], writes=[("srow",)], tag="sln")
            self.act(srow[:, 0:1024], srow[:, 0:1024], AF.Exp,
                     reads=[("srow",)], writes=[("srow",)], scale=-1.0,
                     tag="sexp")
            for pr in range(4):
                he = srow[:, pr * 256:pr * 256 + 128]
                ho = srow[:, pr * 256 + 128:pr * 256 + 256]
                self.mm(pb[7][:, pr * 128:(pr + 1) * 128],
                        cons["lo64"][:, :], he, True, False,
                        reads=[("c", "lo64"), ("srow",)],
                        writes=[("pb", 7)], tag="rbmm")
                self.mm(pb[7][:, pr * 128:(pr + 1) * 128],
                        cons["hi64"][:, :], ho, False, True,
                        reads=[("c", "hi64"), ("srow",)],
                        writes=[("pb", 7)], tag="rbmm")
            self.ew("scalar", "copy", rbsb[:, :], pb[7][:, 0:512],
                    reads=[("pb", 7)], writes=[("rbsb",)], tag="rbev")
            yv = ybuf[:, :].rearrange("p (c t) -> p c t", t=256)[
                :, 4 * g:4 * g + 4, par * 128:par * 128 + 128]
            self.ew("vector", "tensor_tensor", yv,
                    pb[6][:, 0:512].rearrange("p (c t) -> p c t", t=128),
                    rbsb[:, :].rearrange("p (c t) -> p c t", t=128),
                    ALU.mult, reads=[("pb", 6), ("rbsb",)],
                    writes=[("ybuf", par, g)], tag="ydiv")

        def wo_resid(i):
            ykeys = [("ybuf", 0, 0), ("ybuf", 0, 1), ("ybuf", 1, 0),
                     ("ybuf", 1, 1)]
            xpv = x_sb[:, :].rearrange("p (c t) -> p c t", t=TC)
            xk = [("x", i - 1), ("x", i)]
            for ocg in range(2):
                for oc2 in range(4):
                    oc = 4 * ocg + oc2
                    bank = pb[4 + oc2 // 2]
                    slot = (oc2 % 2) * 256
                    for ic in range(8):
                        self.mm(bank[:, slot:slot + 256],
                                womb[:, ic * 1024 + oc * 128:
                                     ic * 1024 + oc * 128 + 128],
                                ybuf[:, ic * 256:(ic + 1) * 256],
                                ic == 0, ic == 7,
                                reads=[("fw", "womb")] + ykeys,
                                writes=[("pb", 4 + oc2 // 2)], tag="womm")
                for b2 in range(2):
                    ch = 4 * ocg + 2 * b2
                    xslice = xpv[:, ch:ch + 2, (i - 1) * 128:(i + 1) * 128]
                    self.ew("vector", "tensor_tensor", xslice, xslice,
                            pb[4 + b2][:, 0:512].rearrange(
                                "p (c t) -> p c t", t=256),
                            ALU.add, reads=xk + [("pb", 4 + b2)], writes=xk,
                            tag="aresid")

        for j in range(2):
            precast(j, True)
            x1(j, True, True)
            x2a_k(j, True)
            x2b(j, True)
        if "womb" in self.wcached:
            off = self.wcached["womb"]
            self.dma(womb[:, :], self.wdram[:, off:off + 8192],
                     reads=[("wd", "womb"), ("wspill",)],
                     writes=[("fw", "womb")], tag="wfast")
        for c in range(8) if "womb" not in self.wcached else []:
            for j in range(2):
                self.mm(pb[3][:, j:j + 1], cons["sel2"][:, :],
                        cons["pm2"][:, 2 * c + j:2 * c + j + 1], True, True,
                        reads=[("c", "sel2"), ("c", "pm2")],
                        writes=[("pb", 3)], tag="pmcol")
            self.ew("vector", "tensor_copy", scol[:, :], pb[3][:, 0:2],
                    reads=[("pb", 3)], writes=[("scol",)])
            self.ew("vector", "tensor_scalar", mcb[:, 0:64], eyep[:, 0:64],
                    scol[:, 0:1], None, ALU.mult,
                    reads=[("c", "eyepair"), ("scol",)], writes=[("mcb",)])
            self.ew("vector", "tensor_scalar", mcb[:, 64:128], eyep[:, 64:128],
                    scol[:, 1:2], None, ALU.mult,
                    reads=[("c", "eyepair"), ("scol",)], writes=[("mcb",)])
            self.dma(self.wstage[:, :], self.dram_in["wo"][:, c * 1024:
                                                           (c + 1) * 1024],
                     reads=(), writes=[("wstg",)], tag="wload")
            self.ew("vector", "tensor_copy", wochunk[:, :], self.wstage[:, :],
                    reads=[("wstg",)], writes=[("woch",)])
            for j in range(2):
                self.mm(pb[4][:, 0:512], mcb[:, :],
                        wochunk[:, j * 512:(j + 1) * 512], True, True,
                        reads=[("mcb",), ("woch",)], writes=[("pb", 4)],
                        tag="wommm")
                self.ew("scalar", "copy", womb[:, c * 1024 + j * 512:
                                               c * 1024 + (j + 1) * 512],
                        pb[4][:, 0:512], reads=[("pb", 4)],
                        writes=[("fw", "womb")], tag="womev")
        if "womb" not in self.wcached:
            self.spill_weight("womb", 8192, womb)

        precast(0, False)
        x1(0, False, False)
        x2a_k(0, False)
        x2a(0, False)
        precast(1, False)
        for i in range(NQT):
            if i + 2 < NQT:
                precast(i + 2, False)
            if i + 1 < NQT:
                x1(i + 1, False, False)
            x2b(i, False)
            y_round(i, 0)
            if i + 1 < NQT:
                x2a_k(i + 1, False)
            y_round(i, 1)
            if i + 1 < NQT:
                x2a(i + 1, False)
            if i % 2 == 1:
                wo_resid(i)
                if ffn2 is not None:
                    a1_tile((i - 1) // 2)
        if ffn2 is not None:
            self.emit_allreduce(f2msum, f2ccin, f2ccout)


def make_in_maps(inputs, k, TC, n_pairs):
    consts = build_consts()
    wst = stage_weights(inputs)
    x = np.asarray(inputs["L0_attn_x"], np.float32)
    in_maps = []
    for core in range(2 * n_pairs):
        p, half = core // 2, core % 2
        m = dict(wst)
        m.update(consts)
        xjobs = [x[j * n_pairs + p] for j in range(x.shape[0] // n_pairs)]
        m.update(build_core_inputs(xjobs, half, TC, consts))
        m = {n: v for n, v in m.items() if n in k.dram_in}
        missing = set(k.dram_in) - set(m)
        assert not missing, missing
        in_maps.append(m)
    return in_maps


def run(inputs, TC=2048, n_pairs=4):
    k = Kern(TC, n_pairs).build()
    in_maps = make_in_maps(inputs, k, TC, n_pairs)
    res = run_bass_kernel_spmd(k.nc, in_maps, core_ids=list(range(2 * n_pairs)))
    B = np.asarray(inputs["L0_attn_x"]).shape[0]
    outs = []
    for b in range(B):
        j, p = b // n_pairs, b % n_pairs
        y0 = res.results[2 * p][f"y{j}"]
        y1 = res.results[2 * p + 1][f"y{j}"]
        outs.append(np.concatenate([y0, y1], axis=1).T)
    return np.stack(outs)


def kernel(**inputs):
    return run(inputs, TC=2048, n_pairs=4).astype(np.float32)
